# version 09 (final)
# baseline (speedup 1.0000x reference)
import contextlib
import os
import numpy as np
import ml_dtypes
import concourse.bass as bass
import concourse.mybir as mybir
from concourse.bass_utils import run_bass_kernel_spmd

F32 = mybir.dt.float32
BF16 = mybir.dt.bfloat16
AF = mybir.ActivationFunctionType
ALU = mybir.AluOpType
AX = mybir.AxisListType

PE, ACT, DVE, POOL, SYNC = "tensor", "scalar", "vector", "gpsimd", "sync"
ENGINES = [PE, ACT, DVE, POOL, SYNC]

B_PER_CORE = 2
T = 2048
D = 1024
NT = T // 128
ND = D // 128


class Prog:

    def __init__(self, nc, es):
        self.nc = nc
        self.sems = {}
        for e in ENGINES:
            self.sems["c:" + e] = es.enter_context(nc.semaphore(f"sc_{e}"))
            self.sems["d:" + e] = es.enter_context(nc.semaphore(f"sd_{e}"))
        self.cnt = {e: 0 for e in ENGINES}
        self.dmacnt = {e: 0 for e in ENGINES}
        self.items = {e: [] for e in ENGINES}
        self.lastwait = {e: {} for e in ENGINES}

    def _waits(self, e, deps):
        for (skey, val) in deps:
            if skey == "c:" + e:
                continue
            lw = self.lastwait[e]
            if lw.get(skey, -1) >= val:
                pass
            else:
                lw[skey] = val
            sem = self.sems[skey]
            self.items[e].append(lambda eng, st, sem=sem, val=val: eng.wait_ge(sem, val))

    def op(self, e, fn, deps=()):
        self._waits(e, deps)

        def run(eng, st, fn=fn):
            st["last"] = fn(eng)
        self.items[e].append(run)

    def tok(self, e):
        self.cnt[e] += 1
        val = self.cnt[e]
        sem = self.sems["c:" + e]

        def m(eng, st, sem=sem, e=e):
            if e == PE:
                li = st.get("last")
                if li is not None and li is not st.get("last_inc"):
                    li.then_inc(sem, 1)
                    st["last_inc"] = li
                else:
                    eng.nop().then_inc(sem, 1)
            else:
                eng.drain()
                eng.nop().then_inc(sem, 1)
        self.items[e].append(m)
        return ("c:" + e, val)

    def flush(self, e):
        tokn = self.tok(e)
        sem = self.sems["c:" + e]
        val = tokn[1]

        def m(eng, st, sem=sem, val=val):
            eng.wait_ge(sem, val)
        self.items[e].append(m)
        return tokn

    def dma(self, q, out_ap, in_ap, deps=()):
        self._waits(q, deps)
        self.dmacnt[q] += 16
        val = self.dmacnt[q]
        sem = self.sems["d:" + q]
        self.items[q].append(
            lambda eng, st, sem=sem, o=out_ap, i=in_ap: eng.dma_start(out=o, in_=i).then_inc(sem, 16))
        return ("d:" + q, val)

    def emit(self, block):
        for e in ENGINES:
            dec = getattr(block, e)
            items = self.items[e]

            def body(eng, items=items):
                st = {"last": None, "last_inc": None}
                for fn in items:
                    fn(eng, st)
            dec(body)




class Bufs:
    def __init__(self, nc, es, need=()):
        A = lambda name, shape, dt: es.enter_context(nc.sbuf_tensor(name, shape, dt))
        self.x = A("x_tok", [128, NT * D], F32)
        self.xn = A("xnorm_tok", [128, NT * D], BF16)
        self.xnT = A("xnT", [128, ND * T], BF16)
        self.ident = A("ident", [128, 128], BF16)
        self.identf = A("identf", [128, 128], F32)
        self.stats = A("stats", [128, 256], F32)
        self.ones = A("ones_col", [128, 128], BF16)
        self.gtmp = A("gtmp", [128, 128], BF16)
        self.PSb = [es.enter_context(nc.psum_tensor(f"PS{i}", [128, 512], F32))
                    for i in range(8)]

    def psb(self, bank, nbanks=1, dtype=F32):
        assert nbanks == 1
        v = self.PSb[bank][:]
        if dtype == BF16:
            v = v.bitcast(BF16)
        return v


def emit_const_setup(P, B):
    P.op(POOL, lambda e: e.iota(B.ident[:], pattern=[[1, 128]], base=0,
                                channel_multiplier=-1,
                                allow_small_or_imprecise_dtypes=True))
    P.flush(POOL)
    P.op(POOL, lambda e: e.tensor_scalar(out=B.ident[:], in0=B.ident[:],
                                         scalar1=0.0, scalar2=None, op0=ALU.is_equal))
    P.op(POOL, lambda e: e.iota(B.identf[:], pattern=[[1, 128]], base=0,
                                channel_multiplier=-1,
                                allow_small_or_imprecise_dtypes=True))
    P.flush(POOL)
    P.op(POOL, lambda e: e.tensor_scalar(out=B.identf[:], in0=B.identf[:],
                                         scalar1=0.0, scalar2=None, op0=ALU.is_equal))
    P.op(POOL, lambda e: e.memset(B.ones[:], 1.0))
    return P.tok(POOL)


def emit_sel8_setup(P, M):
    P.op(POOL, lambda e: e.iota(M.sel8[:], pattern=[[1, 1024]], base=0,
                                channel_multiplier=-128,
                                allow_small_or_imprecise_dtypes=True))
    P.flush(POOL)
    P.op(POOL, lambda e: e.tensor_scalar(out=M.sel8b[:], in0=M.sel8[:],
                                         scalar1=-0.5, scalar2=None, op0=ALU.is_ge))
    P.op(POOL, lambda e: e.tensor_scalar(out=M.sel8[:], in0=M.sel8[:],
                                         scalar1=127.5, scalar2=None, op0=ALU.is_le))
    P.flush(POOL)
    P.op(POOL, lambda e: e.tensor_mul(out=M.sel8[:], in0=M.sel8[:],
                                      in1=M.sel8b[:]))
    return P.tok(POOL)


def emit_load_x(P, B, x_hbm, b, deps=()):
    toks = []
    src = x_hbm[b].rearrange("(tc p) d -> p tc d", p=128)
    dst = B.x[:].rearrange("p (tc d) -> p tc d", d=D)
    for g in range(4):
        toks.append(P.dma(SYNC, dst[:, g * 4:(g + 1) * 4, :],
                          src[:, g * 4:(g + 1) * 4, :], deps=deps))
    return toks


def emit_ln(P, B, const_tok, xdeps, war_deps=(), transpose=True, skip_stats=False):
    war_deps = tuple(war_deps) + (const_tok,)
    xv = B.x[:].rearrange("p (tc d) -> p tc d", d=D)
    xnv = B.xn[:].rearrange("p (tc d) -> p tc d", d=D)
    S = B.stats
    if skip_stats:
        P.op(DVE, lambda e: e.tensor_copy(out=B.xn[:], in_=B.x[:]),
             deps=list(xdeps) + list(war_deps))
        tok_xn = P.flush(DVE)
        return _emit_ln_transposes(P, B, const_tok, tok_xn, xnv)
    P.op(DVE, lambda e: e.memset(S[:, 255:256], 1e-5), deps=war_deps)
    for tc in range(NT):
        deps = list(xdeps) if tc == 0 else []
        P.op(DVE, lambda e, tc=tc: e.bn_stats(out=S[:, tc * 12:tc * 12 + 6],
                                              in_=xv[:, tc, 0:512]), deps=deps)
        P.op(DVE, lambda e, tc=tc: e.bn_stats(out=S[:, tc * 12 + 6:tc * 12 + 12],
                                              in_=xv[:, tc, 512:1024]))
    P.flush(DVE)
    for tc in range(NT):
        P.op(DVE, lambda e, tc=tc: e.bn_aggr(out=S[:, 192 + 2 * tc:194 + 2 * tc],
                                             in_=S[:, tc * 12:tc * 12 + 12]))
    t_aggr = P.flush(DVE)
    P.op(ACT, lambda e: e.activation(out=S[:, 224:240], in_=S[:, 193:225].rearrange(
        "p (tc two) -> p tc two", two=2)[:, :, 0], func=AF.Sqrt, bias=S[:, 255:256]),
        deps=[t_aggr])
    t_sqrt = P.tok(ACT)
    P.op(DVE, lambda e: e.reciprocal(out=S[:, 224:240], in_=S[:, 224:240]), deps=[t_sqrt])
    P.op(DVE, lambda e: e.tensor_scalar_mul(out=S[:, 240:256], in0=S[:, 192:224].rearrange(
        "p (tc two) -> p tc two", two=2)[:, :, 0], scalar1=-1.0))
    P.flush(DVE)
    for tc in range(NT):
        P.op(DVE, lambda e, tc=tc: e.tensor_scalar(
            out=xnv[:, tc, :], in0=xv[:, tc, :], scalar1=S[:, 240 + tc:241 + tc],
            scalar2=S[:, 224 + tc:225 + tc], op0=ALU.add, op1=ALU.mult))
    tok_xn = P.flush(DVE)
    if not transpose:
        return tok_xn, None
    return _emit_ln_transposes(P, B, const_tok, tok_xn, xnv)


def _emit_ln_transposes(P, B, const_tok, tok_xn, xnv):
    xnTv = B.xnT[:].rearrange("p (dc t) -> p dc t", t=T)
    evac_toks = []
    for tc in range(NT):
        deps = [tok_xn, const_tok]
        if tc >= 1:
            deps.append(evac_toks[tc - 1])
        for dc in range(ND):
            P.op(PE, lambda e, tc=tc, dc=dc: e.transpose(
                B.psb(dc, 1, BF16)[:, 0:128], xnv[:, tc, dc * 128:(dc + 1) * 128],
                B.ident[:]), deps=deps if dc == 0 else [])
        mm_tok = P.tok(PE)
        for dc in range(ND):
            P.op(DVE, lambda e, tc=tc, dc=dc: e.tensor_copy(
                out=xnTv[:, dc, tc * 128:(tc + 1) * 128],
                in_=B.psb(dc, 1, BF16)[:, 0:128]), deps=[mm_tok] if dc == 0 else [])
        evac_toks.append(P.tok(DVE))
    return tok_xn, evac_toks[-1]




def host_prep_l0ffn(inp):
    f = _as32
    Wg = f(inp["L0_ffn_Wg"])
    W1s = f(inp["L0_ffn_W1s"])
    W2s = f(inp["L0_ffn_W2s"])
    W1r = f(inp["L0_ffn_W1r"])
    W2r = f(inp["L0_ffn_W2r"])
    d = {}
    wgp = np.zeros((8, 128, 128), np.float32)
    wgp[:, :, 0:8] = Wg.reshape(8, 128, 8)
    d["moe_wg"] = np.ascontiguousarray(wgp.transpose(1, 0, 2).reshape(128, 1024))
    gb = np.zeros((1, 128), np.float32)
    gb[0, 0:8] = f(inp["L0_ffn_gate_bias"])
    d["moe_gbias"] = gb
    lay1 = lambda W: np.ascontiguousarray(W.reshape(8, 128, 4, 128).transpose(1, 0, 2, 3).reshape(128, 4096))
    lay2 = lambda W: np.ascontiguousarray(W.reshape(4, 128, 8, 128).transpose(1, 0, 2, 3).reshape(128, 4096))
    d["moe_w1s"] = lay1(W1s)
    d["moe_w2s"] = lay2(W2s)
    d["moe_w1r"] = np.stack([lay1(W1r[e]) for e in range(8)])
    d["moe_w2r"] = np.stack([lay2(W2r[e]) for e in range(8)])
    d["moe_b2s"] = np.ascontiguousarray(f(inp["L0_ffn_b2s"]).reshape(8, 128).T)
    d["moe_b2r"] = f(inp["L0_ffn_b2r"])
    return d


def decl_l0ffn_hbm(nc):
    h = {}
    h["moe_wg"] = nc.dram_tensor("moe_wg", [128, 1024], F32, kind="ExternalInput").ap()
    h["moe_gbias"] = nc.dram_tensor("moe_gbias", [1, 128], F32, kind="ExternalInput").ap()
    h["moe_w1s"] = nc.dram_tensor("moe_w1s", [128, 4096], F32, kind="ExternalInput").ap()
    h["moe_w2s"] = nc.dram_tensor("moe_w2s", [128, 4096], F32, kind="ExternalInput").ap()
    h["moe_w1r"] = nc.dram_tensor("moe_w1r", [8, 128, 4096], F32, kind="ExternalInput").ap()
    h["moe_w2r"] = nc.dram_tensor("moe_w2r", [8, 128, 4096], F32, kind="ExternalInput").ap()
    h["moe_b2s"] = nc.dram_tensor("moe_b2s", [128, 8], F32, kind="ExternalInput").ap()
    h["moe_b2r"] = nc.dram_tensor("moe_b2r", [8, 1024], F32, kind="ExternalInput").ap()
    h["moe_w1r_bf"] = nc.dram_tensor("moe_w1r_bf", [8, 128, 4096], BF16, kind="Internal").ap()
    h["moe_w2r_bf"] = nc.dram_tensor("moe_w2r_bf", [8, 128, 4096], BF16, kind="Internal").ap()
    return h


class MoeBufs:
    def __init__(self, nc, es, B):
        A = lambda name, shape, dt: es.enter_context(nc.sbuf_tensor(name, shape, dt))
        self.B = B
        self.stg = [B.xn[:].bitcast(F32)[:, 0:4096],
                    B.xn[:].bitcast(F32)[:, 4096:8192]]
        self.heg = B.x[:, 0:9216].bitcast(BF16)
        self.outT = B.xn[:]
        self.w1s = A("moe_w1s_sb", [128, 4096], BF16)
        self.w2s = A("moe_w2s_sb", [128, 4096], BF16)
        self.w1b = A("moe_w1b", [128, 2 * 4096], BF16)
        self.w2b = A("moe_w2b", [128, 2 * 4096], BF16)
        self.wg = A("moe_wg_sb", [128, 1024], BF16)
        self.gbias = A("moe_gbias_sb", [1, 128], BF16)
        self.ones512 = A("moe_ones512", [1, 512], BF16)
        self.b2s = A("moe_b2s_sb", [128, 8], F32)
        self.b2r = A("moe_b2r_sb", [8, 1024], BF16)
        self.logT = A("moe_logT", [128, 2048], F32)
        self.logtok = A("moe_logtok", [128, 16 * 8], F32)
        self.gsc = A("moe_gsc", [128, 16 * 32], F32)
        self.gatesT = A("moe_gatesT", [128, 2048], BF16)
        self.ge = A("moe_ge", [128, 512], F32)
        self.gtmp2 = A("gtmp2", [128, 128], BF16)
        self.sel8 = A("sel8", [8, 1024], BF16)
        self.sel8b = A("sel8b", [8, 1024], BF16)


def emit_moe_prologue(P, M, H, deps):
    t_small = [P.dma(SYNC, M.b2s[:], H["moe_b2s"], deps=deps)]
    P.op(POOL, lambda e: e.memset(M.ones512[:], 1.0))
    P.op(POOL, lambda e: e.memset(M.ones512[0:1, 0:1], 1.0))
    stg = M.stg
    d0 = P.dma(SYNC, stg[0][0:8, 0:1024], H["moe_b2r"], deps=deps)
    dgb = P.dma(SYNC, stg[0][0:1, 2048:2176], H["moe_gbias"], deps=deps)
    P.op(DVE, lambda e: e.tensor_copy(out=M.gbias[:], in_=stg[0][0:1, 2048:2176]),
         deps=[dgb])
    d1 = P.dma(SYNC, stg[0][:, 1024:2048], H["moe_wg"], deps=deps)
    d2 = P.dma(SYNC, stg[0][:, 2048:4096] if False else stg[1][:, 0:4096], H["moe_w1s"], deps=deps)
    P.op(DVE, lambda e: e.tensor_copy(out=M.b2r[:], in_=stg[0][0:8, 0:1024]), deps=[d0])
    P.op(DVE, lambda e: e.tensor_copy(out=M.wg[:], in_=stg[0][:, 1024:2048]), deps=[d1])
    P.op(DVE, lambda e: e.tensor_copy(out=M.w1s[:], in_=stg[1][:, 0:4096]), deps=[d2])
    c1 = P.tok(DVE)
    d3 = P.dma(SYNC, stg[1][:, 0:4096], H["moe_w2s"], deps=[c1])
    P.op(DVE, lambda e: e.tensor_copy(out=M.w2s[:], in_=stg[1][:, 0:4096]), deps=[d3])
    shared_tok = P.tok(DVE)
    conv_toks = [shared_tok, shared_tok]
    store_toks = []
    for i, (src_h, dst_h) in enumerate([(H["moe_w1r"], H["moe_w1r_bf"]),
                                        (H["moe_w2r"], H["moe_w2r_bf"])]):
        for e in range(8):
            par = (i * 8 + e) % 2
            dep_war = [conv_toks[-2]] if len(conv_toks) >= 2 else []
            dl = P.dma(SYNC, stg[par][:, 0:4096], src_h[e], deps=dep_war)
            half = M.w1b[:, par * 4096:(par + 1) * 4096]
            P.op(DVE, lambda e_, s=stg[par], h=half: e_.tensor_copy(out=h, in_=s[:, 0:4096]),
                 deps=[dl] + store_toks[-2:-1])
            ct = P.tok(DVE)
            conv_toks.append(ct)
            store_toks.append(P.dma(SYNC, dst_h[e], half, deps=[ct]))
    return shared_tok, store_toks, conv_toks[-1]


def emit_moe(P, B, M, H, b, tok_xnT, scratch_dep=(), ge_dep=(), stage="full",
             dbg_ltok=None):
    xnTv = B.xnT[:].rearrange("p (dc t) -> p dc t", t=T)
    hegv = M.heg.rearrange("p (em t) -> p em t", t=512)
    outTv = M.outT.rearrange("p (dc t) -> p dc t", t=T)
    plog = B.psb(0)
    rt = []
    ntt_r = 1 if stage == "r1a" else 4
    for tt in range(ntt_r):
        for k in range(ND):
            P.op(PE, lambda e, k=k, tt=tt: e.matmul(
                plog, M.wg[:, k * 128:(k + 1) * 128],
                xnTv[:, k, tt * 512:(tt + 1) * 512],
                start=(k == 0), stop=False),
                deps=[tok_xnT] + (rt[-1:] if rt else []))
        P.op(PE, lambda e, tt=tt: e.matmul(
            plog, M.gbias[:], M.ones512[:], start=False, stop=True))
        mt = P.tok(PE)
        if stage == "r0":
            import os as _os9
            for _ in range(int(_os9.environ.get("PADDVE", "0"))):
                P.op(DVE, lambda e: e.memset(M.ge[:, 0:1], 0.0))
            return mt, mt
        P.op(DVE, lambda e, tt=tt: e.tensor_copy(
            out=M.logT[:, tt * 512:(tt + 1) * 512], in_=plog), deps=[mt])
        if os.environ.get("RTPE", "0") == "1":
            P.tok(DVE)
            rt.append(mt)
        else:
            rt.append(P.tok(DVE))
    if stage in ("r1", "r1a"):
        return rt[-1], rt[-1]
    lt_toks = []
    for blk in range(16):
        bank = blk % 8
        deps = [rt[-1]]
        if blk >= 8:
            deps.append(lt_toks[blk - 8])
        P.op(PE, lambda e, blk=blk, bank=bank: e.transpose(
            B.psb(bank)[:, 0:8], M.logT[0:8, blk * 128:(blk + 1) * 128],
            B.identf[0:8, 0:8]), deps=deps)
        if os.environ.get("DUPT", "0") == "1":
            P.op(PE, lambda e, blk=blk, bank=bank: e.transpose(
                B.psb(bank)[:, 0:8], M.logT[0:8, blk * 128:(blk + 1) * 128],
                B.identf[0:8, 0:8]))
        mt = P.tok(PE)
        P.op(DVE, lambda e, blk=blk, bank=bank: e.tensor_copy(
            out=M.logtok[:, blk * 8:(blk + 1) * 8], in_=B.psb(bank)[:, 0:8]), deps=[mt])
        lt_toks.append(P.tok(DVE))
    if stage == "r2":
        return lt_toks[-1], lt_toks[-1]
    G = M.gsc
    for blk in range(16):
        P.op(DVE, lambda e, blk=blk: e.max(out=G[:, blk * 32:blk * 32 + 8],
                                           in_=M.logtok[:, blk * 8:(blk + 1) * 8]),
             deps=[lt_toks[blk]])
    P.flush(DVE)
    for blk in range(16):
        P.op(DVE, lambda e, blk=blk: e.tensor_scalar_mul(
            out=G[:, blk * 32 + 8:blk * 32 + 9], in0=G[:, blk * 32:blk * 32 + 1],
            scalar1=-1.0))
        P.op(DVE, lambda e, blk=blk: e.tensor_scalar(
            out=G[:, blk * 32 + 16:blk * 32 + 24],
            in0=M.logtok[:, blk * 8:(blk + 1) * 8],
            scalar1=G[:, blk * 32 + 1:blk * 32 + 2], scalar2=None, op0=ALU.is_ge))
    t_neg = P.flush(DVE)
    exdeps = [t_neg]
    if dbg_ltok is not None and b == 0:
        exdeps.append(P.dma(POOL, dbg_ltok[0], M.logtok[:], deps=[t_neg]))
        exdeps.append(P.dma(POOL, dbg_ltok[1], M.gsc[:], deps=[t_neg]))
    for blk in range(16):
        P.op(ACT, lambda e, blk=blk: e.activation(
            out=M.logtok[:, blk * 8:(blk + 1) * 8],
            in_=M.logtok[:, blk * 8:(blk + 1) * 8], func=AF.Exp,
            bias=G[:, blk * 32 + 8:blk * 32 + 9]), deps=exdeps if blk == 0 else [])
    t_exp = P.tok(ACT)
    for blk in range(16):
        P.op(DVE, lambda e, blk=blk: e.tensor_mul(
            out=M.logtok[:, blk * 8:(blk + 1) * 8],
            in0=M.logtok[:, blk * 8:(blk + 1) * 8],
            in1=G[:, blk * 32 + 16:blk * 32 + 24]), deps=[t_exp] if blk == 0 else [])
    P.flush(DVE)
    for blk in range(16):
        P.op(DVE, lambda e, blk=blk: e.reduce_sum(
            out=G[:, blk * 32 + 9:blk * 32 + 10], in_=M.logtok[:, blk * 8:(blk + 1) * 8],
            axis=AX.X))
    P.flush(DVE)
    for blk in range(16):
        P.op(DVE, lambda e, blk=blk: e.reciprocal(
            out=G[:, blk * 32 + 10:blk * 32 + 11], in_=G[:, blk * 32 + 9:blk * 32 + 10]))
    P.flush(DVE)
    for blk in range(16):
        P.op(DVE, lambda e, blk=blk: e.tensor_scalar_mul(
            out=M.logtok[:, blk * 8:(blk + 1) * 8],
            in0=M.logtok[:, blk * 8:(blk + 1) * 8],
            scalar1=G[:, blk * 32 + 10:blk * 32 + 11]))
    t_gates = P.flush(DVE)
    if stage == "r3":
        return t_gates, t_gates
    gt_toks = []
    for blk in range(16):
        bank = blk % 8
        deps = [t_gates]
        if blk >= 8:
            deps.append(gt_toks[blk - 8])
        P.op(DVE, lambda e, blk=blk: e.tensor_copy(
            out=M.gatesT[0:8, 0:1] if False else B.gtmp[:, blk * 8:(blk + 1) * 8],
            in_=M.logtok[:, blk * 8:(blk + 1) * 8]), deps=deps)
        ct = P.tok(DVE)
        P.op(PE, lambda e, blk=blk, bank=bank: e.transpose(
            B.psb(bank, 1, BF16)[0:8, 0:128], B.gtmp[:, blk * 8:(blk + 1) * 8],
            B.ident[:]), deps=[ct])
        mt = P.tok(PE)
        P.op(DVE, lambda e, blk=blk, bank=bank: e.tensor_copy(
            out=M.gatesT[:, blk * 128:(blk + 1) * 128],
            in_=B.psb(bank, 1, BF16)[:, 0:128]), deps=[mt])
        gt_toks.append(P.tok(DVE))
    t_gatesT = gt_toks[-1]
    if stage == "routing":
        return t_gatesT, t_gatesT
    w1bv = [M.w1b[:, 0:4096], M.w1b[:, 4096:8192]]
    w2bv = [M.w2b[:, 0:4096], M.w2b[:, 4096:8192]]
    prev_phase2 = [t_gatesT]
    NTT = 1 if stage == "tt0" else 4
    heg_deps = list(ge_dep)
    p1_use = {}
    p2_use = {}
    w1_toks = {}
    w2_toks = {}
    t_outT = []
    for tt in range(NTT):
        evac_toks = []
        for e9 in range(9):
            if e9 < 8:
                par = e9 % 2
                dep = [p1_use[par]] if par in p1_use else []
                w1_toks[e9] = P.dma(SYNC, w1bv[par], H["moe_w1r_bf"][e9],
                                    deps=dep + list(scratch_dep))
                w1 = w1bv[par]
                wdep = [w1_toks[e9]]
            else:
                w1 = M.w1s
                wdep = []
            for m in range(4):
                for k in range(ND):
                    d0 = (wdep + prev_phase2 + heg_deps + [tok_xnT] + evac_toks[-1:]) \
                        if (m == 0 and k == 0) else []
                    P.op(PE, lambda e, w1=w1, m=m, k=k, tt=tt: e.matmul(
                        B.psb(m), w1[:, (k * 4 + m) * 128:(k * 4 + m) * 128 + 128],
                        xnTv[:, k, tt * 512:(tt + 1) * 512],
                        start=(k == 0), stop=(k == ND - 1)),
                        deps=d0)
            mt = P.tok(PE)
            if e9 < 8:
                p1_use[e9 % 2] = mt
            if e9 < 8:
                P.op(PE, lambda e, e9=e9, tt=tt: e.matmul(
                    B.psb(5), M.sel8[:, e9 * 128:(e9 + 1) * 128],
                    M.gatesT[0:8, tt * 512:(tt + 1) * 512], start=True, stop=True),
                    deps=[t_gatesT])
                mg = P.tok(PE)
                P.op(DVE, lambda e: e.tensor_copy(out=M.ge[:], in_=B.psb(5)), deps=[mg])
                t_ge = P.tok(DVE)
            for m in range(4):
                P.op(ACT, lambda e, e9=e9, m=m: e.activation(
                    out=hegv[:, e9 * 4 + m, :], in_=B.psb(m), func=AF.Gelu),
                    deps=[mt])
            t_act = P.tok(ACT)
            if e9 < 8:
                for m in range(4):
                    P.op(DVE, lambda e, e9=e9, m=m: e.tensor_mul(
                        out=hegv[:, e9 * 4 + m, :], in0=hegv[:, e9 * 4 + m, :],
                        in1=M.ge[:]), deps=[t_act, t_ge])
                evac_toks.append(P.tok(DVE))
            else:
                evac_toks.append(t_act)
        t_p1 = evac_toks
        first_mm = {}
        for e9 in range(9):
            if e9 < 8:
                par = e9 % 2
                dep = [p2_use[par]] if par in p2_use else []
                w2_toks[e9] = P.dma(SYNC, w2bv[par], H["moe_w2r_bf"][e9],
                                    deps=dep + list(scratch_dep))
                w2 = w2bv[par]
                wdep = [w2_toks[e9]]
            else:
                w2 = M.w2s
                wdep = []
            for dc in range(ND):
                for hk in range(4):
                    P.op(PE, lambda e, w2=w2, dc=dc, hk=hk, e9=e9: e.matmul(
                        B.psb(dc), w2[:, (hk * 8 + dc) * 128:(hk * 8 + dc) * 128 + 128],
                        hegv[:, e9 * 4 + hk, :],
                        start=(e9 == 0 and hk == 0), stop=False),
                        deps=(wdep + t_p1) if (dc == 0 and hk == 0) else [])
            mt = P.tok(PE)
            if e9 < 8:
                p2_use[e9 % 2] = mt
        for dc in range(ND):
            P.op(PE, lambda e, dc=dc, tt=tt: e.matmul(
                B.psb(dc), M.b2r[0:8, dc * 128:(dc + 1) * 128],
                M.gatesT[0:8, tt * 512:(tt + 1) * 512], start=False, stop=True),
                deps=[t_gatesT] if dc == 0 else [])
        mt2 = P.tok(PE)
        for dc in range(ND):
            P.op(DVE, lambda e, dc=dc, tt=tt: e.tensor_scalar_add(
                out=outTv[:, dc, tt * 512:(tt + 1) * 512], in0=B.psb(dc),
                scalar1=M.b2s[:, dc:dc + 1]), deps=[mt2])
        prev_phase2 = [P.tok(DVE)]
        t_outT.append(prev_phase2[0])
    return t_outT[-1], prev_phase2[0]


def emit_residual_add(P, B, srcT, tok_src, xdeps, store_y=None, qy=SYNC):
    srcv = srcT.rearrange("p (dc t) -> p dc t", t=T)
    xv = B.x[:].rearrange("p (tc d) -> p tc d", d=D)
    add_toks = []
    store_toks = []
    for tc in range(NT):
        deps = [tok_src] + list(xdeps)
        if tc >= 1:
            deps.append(add_toks[tc - 1])
        for dc in range(ND):
            P.op(PE, lambda e, tc=tc, dc=dc: e.transpose(
                B.psb(dc, 1, BF16)[:, 0:128], srcv[:, dc, tc * 128:(tc + 1) * 128],
                B.ident[:]), deps=deps if dc == 0 else [])
        mt = P.tok(PE)
        for dc in range(ND):
            P.op(DVE, lambda e, tc=tc, dc=dc: e.tensor_add(
                out=xv[:, tc, dc * 128:(dc + 1) * 128],
                in0=xv[:, tc, dc * 128:(dc + 1) * 128],
                in1=B.psb(dc, 1, BF16)[:, 0:128]), deps=[mt] if dc == 0 else [])
        add_toks.append(P.tok(DVE))
        if store_y is not None:
            store_toks.append(P.dma(qy, store_y[tc], xv[:, tc, :], deps=[add_toks[-1]]))
    return add_toks[-1], store_toks, add_toks[-1]


def build_moe_launch(nc, debug=False, stage="full"):
    x_h = nc.dram_tensor("x", [B_PER_CORE, T, D], F32, kind="ExternalInput").ap()
    y_h = nc.dram_tensor("y", [B_PER_CORE, T, D], F32, kind="ExternalOutput").ap()
    H = decl_l0ffn_hbm(nc)
    if debug:
        dbg_g = nc.dram_tensor("dbg_gates", [128, 128], F32, kind="ExternalOutput").ap()
        dbg_h = nc.dram_tensor("dbg_heg", [128, 18432], BF16, kind="ExternalOutput").ap()
        dbg_o = nc.dram_tensor("dbg_outT", [128, 16384], BF16, kind="ExternalOutput").ap()
        dbg_gt = nc.dram_tensor("dbg_gatesT", [128, 2048], BF16, kind="ExternalOutput").ap()
        dbg_sel = nc.dram_tensor("dbg_sel", [8, 1024], BF16, kind="ExternalOutput").ap()
        dbg_lt = nc.dram_tensor("dbg_logT", [128, 2048], F32, kind="ExternalOutput").ap()
        dbg_ltok = [nc.dram_tensor("dbg_ltok", [128, 128], F32, kind="ExternalOutput").ap(),
                    nc.dram_tensor("dbg_gsc", [128, 512], F32, kind="ExternalOutput").ap()]
    else:
        dbg_ltok = None
    es = contextlib.ExitStack()
    with es:
        B = Bufs(nc, es)
        M = MoeBufs(nc, es, B)
        P = Prog(nc, es)
        ct = emit_const_setup(P, B)
        cs = emit_sel8_setup(P, M)
        shared_tok, store_toks, conv_fin = emit_moe_prologue(P, M, H, deps=[cs])
        scratch_dep = [store_toks[-1], store_toks[-2]]
        prev = [conv_fin]
        final_stores = []
        nb = 1 if stage != "full" else B_PER_CORE
        for b in range(nb):
            xtoks = emit_load_x(P, B, x_h, b, deps=prev)
            if stage == "prologue":
                yv = y_h[b].rearrange("(tc p) d -> p tc d", p=128)
                sts = [P.dma(SYNC, yv[:, tc, :],
                             B.x[:].rearrange("p (tc d) -> p tc d", d=D)[:, tc, :],
                             deps=xtoks) for tc in range(NT)]
                P.op(SYNC, lambda e: e.nop(), deps=sts)
                with nc.Block() as block:
                    P.emit(block)
                return nc
            import os as _os3
            lnmode = _os3.environ.get("LNMODE", "full")
            txn, txnT = emit_ln(P, B, ct, xtoks,
                                war_deps=prev if b > 0 else (),
                                transpose=(lnmode != "noT"),
                                skip_stats=(lnmode == "nostat"))
            if txnT is None:
                txnT = txn
            if stage == "ln":
                yv = y_h[b].rearrange("(tc p) d -> p tc d", p=128)
                sts = [P.dma(SYNC, yv[:, tc, :],
                             B.x[:].rearrange("p (tc d) -> p tc d", d=D)[:, tc, :],
                             deps=[txn, txnT]) for tc in range(NT)]
                P.op(SYNC, lambda e: e.nop(), deps=sts)
                with nc.Block() as block:
                    P.emit(block)
                return nc
            t_outT, t_heg = emit_moe(P, B, M, H, b, txnT,
                                     scratch_dep=scratch_dep,
                                     ge_dep=[shared_tok], stage=stage,
                                     dbg_ltok=(dbg_ltok if debug else None))
            if stage in ("r0", "r1", "r1a", "r2", "r3", "routing", "tt0"):
                yv = y_h[b].rearrange("(tc p) d -> p tc d", p=128)
                sts = [P.dma(SYNC, yv[:, tc, :],
                             B.x[:].rearrange("p (tc d) -> p tc d", d=D)[:, tc, :],
                             deps=[t_outT]) for tc in range(NT)]
                P.op(SYNC, lambda e: e.nop(), deps=sts)
                with nc.Block() as block:
                    P.emit(block)
                return nc
            dbg_deps = []
            if debug and b == 0:
                prev_dbg = [P.dma(POOL, dbg_lt, M.logT[:], deps=[t_outT]),
                            P.dma(POOL, dbg_gt, M.gatesT[:], deps=[t_outT]),
                            P.dma(POOL, dbg_sel, B.sel8[:], deps=[t_outT]),
                            P.dma(POOL, dbg_g, M.logtok[:, 0:128], deps=[t_outT]),
                            P.dma(POOL, dbg_h, M.heg, deps=[t_outT]),
                            P.dma(POOL, dbg_o, M.outT, deps=[t_outT])]
                P.op(POOL, lambda e: e.nop(), deps=prev_dbg)
                dbg_deps = prev_dbg
            xtoks2 = emit_load_x(P, B, x_h, b, deps=[t_outT] + dbg_deps)
            yv = y_h[b].rearrange("(tc p) d -> p tc d", p=128)
            store_y = [yv[:, tc, :] for tc in range(NT)]
            tok_x, st_toks, _ = emit_residual_add(P, B, M.outT, t_outT, xtoks2,
                                                  store_y=store_y)
            final_stores += st_toks
            prev = [tok_x] + st_toks
        P.op(SYNC, lambda e: e.nop(), deps=final_stores)
        with nc.Block() as block:
            P.emit(block)
    return nc


def host_prep_l0attn(inp):
    f = _as32
    Wqa = f(inp["L0_attn_Wqa"])
    Wqb = f(inp["L0_attn_Wqb"])
    Wkva = f(inp["L0_attn_Wkva"])
    Wkvb = f(inp["L0_attn_Wkvb"])
    Wout = f(inp["L0_attn_Wout"])
    bout = f(inp["L0_attn_bout"])
    H_, KVH, Dh = 16, 4, 64
    E = 1024
    perm_q = np.zeros(E, np.int64)
    for h in range(H_):
        for dd in range(0, Dh, 2):
            perm_q[h * Dh + dd // 2] = h * Dh + dd
        for dd in range(1, Dh, 2):
            perm_q[h * Dh + 32 + dd // 2] = h * Dh + dd
    swap = np.zeros(E, np.int64)
    for h in range(H_):
        swap[h * Dh:h * Dh + 32] = np.arange(h * Dh + 32, h * Dh + 64)
        swap[h * Dh + 32:h * Dh + 64] = np.arange(h * Dh, h * Dh + 32)
    perm_k = np.zeros(KVH * Dh, np.int64)
    for g in range(KVH):
        for dd in range(0, Dh, 2):
            perm_k[g * Dh + dd // 2] = g * Dh * 2 + dd * 2
        for dd in range(1, Dh, 2):
            perm_k[g * Dh + 32 + dd // 2] = g * Dh * 2 + dd * 2
    swap_k = np.zeros(KVH * Dh, np.int64)
    for g in range(KVH):
        swap_k[g * Dh:g * Dh + 32] = np.arange(g * Dh + 32, g * Dh + 64)
        swap_k[g * Dh + 32:g * Dh + 64] = np.arange(g * Dh, g * Dh + 32)
    perm_v = np.array([g * Dh * 2 + dd * 2 + 1 for g in range(KVH) for dd in range(Dh)])
    Wqb_p = Wqb[:, perm_q]
    Wqb_ps = Wqb_p[:, swap]
    Wk = Wkvb[:, perm_k]
    Wks = Wk[:, swap_k]
    Wv = Wkvb[:, perm_v]
    d = {}
    wlat = np.zeros((1024, 128), np.float32)
    wlat[:, 0:64] = Wqa
    d["a0_wlat"] = np.ascontiguousarray(
        wlat.reshape(8, 128, 128).transpose(1, 0, 2).reshape(128, 1024))
    wlat2 = np.zeros((1024, 128), np.float32)
    wlat2[:, 0:32] = Wkva
    d["a0_wlat2"] = np.ascontiguousarray(
        wlat2.reshape(8, 128, 128).transpose(1, 0, 2).reshape(128, 1024))
    d["a0_wqb"] = np.ascontiguousarray(Wqb_p)
    d["a0_wqbs"] = np.ascontiguousarray(Wqb_ps)
    krep = np.zeros((32, 4 * 128), np.float32)
    ksrep = np.zeros((32, 4 * 128), np.float32)
    for g in range(4):
        krep[:, g * 128:g * 128 + 64] = Wk[:, g * 64:(g + 1) * 64]
        krep[:, g * 128 + 64:g * 128 + 128] = Wk[:, g * 64:(g + 1) * 64]
        ksrep[:, g * 128:g * 128 + 64] = Wks[:, g * 64:(g + 1) * 64]
        ksrep[:, g * 128 + 64:g * 128 + 128] = Wks[:, g * 64:(g + 1) * 64]
    d["a0_wkvb"] = np.ascontiguousarray(
        np.concatenate([krep, ksrep, Wv], axis=1))
    d["a0_wout"] = np.ascontiguousarray(
        Wout.reshape(8, 128, 8, 128).transpose(1, 0, 2, 3).reshape(128, 8192))
    d["a0_bout"] = np.ascontiguousarray(bout.reshape(8, 128).T)
    return d


def decl_l0attn_hbm(nc):
    h = {}
    h["a0_wlat"] = nc.dram_tensor("a0_wlat", [128, 1024], F32, kind="ExternalInput").ap()
    h["a0_wlat2"] = nc.dram_tensor("a0_wlat2", [128, 1024], F32, kind="ExternalInput").ap()
    h["a0_wqb"] = nc.dram_tensor("a0_wqb", [64, 1024], F32, kind="ExternalInput").ap()
    h["a0_wqbs"] = nc.dram_tensor("a0_wqbs", [64, 1024], F32, kind="ExternalInput").ap()
    h["a0_wkvb"] = nc.dram_tensor("a0_wkvb", [32, 1280], F32, kind="ExternalInput").ap()
    h["a0_wout"] = nc.dram_tensor("a0_wout", [128, 8192], F32, kind="ExternalInput").ap()
    h["a0_wout_bf"] = nc.dram_tensor("a0_wout_bf", [128, 8192], BF16, kind="Internal").ap()
    h["a0_bout"] = nc.dram_tensor("a0_bout", [128, 8], F32, kind="ExternalInput").ap()
    return h


class AttnBufs:
    def __init__(self, nc, es, B):
        A = lambda name, shape, dt: es.enter_context(nc.sbuf_tensor(name, shape, dt))
        self.B = B
        self.lat = B.x[:, 1024:3072].bitcast(BF16)
        self.vT = B.x[:, 3072:5120].bitcast(BF16)
        self.vtok = B.x[:, 5120:9216].bitcast(BF16)
        self.PT = B.x[:, 9216:13312].bitcast(BF16)
        self.stg = B.x[:, 0:4096]
        self.attnT = B.xn[:]
        self.qT = A("a0_qT", [128, 8 * T], BF16)
        self.kT = A("a0_kT", [128, 4 * T], BF16)
        self.wob = A("a0_wob", [128, 2 * 1024], BF16)
        self.qs = A("a0_qs", [128, T], BF16)
        self.Ct = A("a0_C", [128, T], BF16)
        self.S2t = A("a0_S2", [128, T], BF16)

        self.wqb = A("a0_wqb_sb", [64, 1024], BF16)
        self.wqbs = A("a0_wqbs_sb", [64, 1024], BF16)
        self.wkvb = A("a0_wkvb_sb", [32, 1280], BF16)
        self.bout = A("a0_bout_sb", [128, 8], F32)
        self.wlat = A("a0_wlat_sb", [128, 1024], BF16)
        self.wlat2 = A("a0_wlat2_sb", [128, 1024], BF16)
        self.nsc = A("a0_nsc", [128, 256], F32)
        self.po = B.x[:, 0:512]
        self.rb = B.x[:, 512:1024]
        self.masks = B.x[:, 13312:14336].bitcast(BF16)
        self.outT = self.qT[:]


def _emit_fmod(P, out_ap, in_ap, m, iscr, fscr):
    P.op(DVE, lambda e: e.tensor_scalar_mul(out=fscr, in0=in_ap, scalar1=1.0 / m))
    P.flush(DVE)
    P.op(DVE, lambda e: e.tensor_copy(out=iscr, in_=fscr))
    P.flush(DVE)
    P.op(DVE, lambda e: e.tensor_copy(out=fscr, in_=iscr))
    P.flush(DVE)
    P.op(DVE, lambda e: e.scalar_tensor_tensor(out=out_ap, in0=fscr, scalar=-m,
                                               in1=in_ap, op0=ALU.mult, op1=ALU.add))
    P.flush(DVE)
    P.op(DVE, lambda e: e.tensor_single_scalar(out=fscr, in_=out_ap, scalar=0.0,
                                               op=ALU.is_lt))
    P.flush(DVE)
    P.op(DVE, lambda e: e.scalar_tensor_tensor(out=out_ap, in0=fscr, scalar=m,
                                               in1=out_ap, op0=ALU.mult, op1=ALU.add))
    P.flush(DVE)


def emit_l0a_tables(P, A_, const_tok):
    B = A_.B
    S = A_.nsc
    P.op(POOL, lambda e: e.iota(S[:, 0:1], pattern=[[0, 1]], base=0,
                                channel_multiplier=1,
                                allow_small_or_imprecise_dtypes=True), deps=[const_tok])
    t_i = P.tok(POOL)
    iscr1 = A_.stg[:, 3072:3073].bitcast(mybir.dt.int32)
    P.op(DVE, lambda e: e.nop() if False else None, deps=[t_i])
    _emit_fmod(P, S[:, 1:2], S[:, 0:1], 32.0, iscr1, S[:, 4:5])
    t0 = P.flush(DVE)
    P.op(ACT, lambda e: e.activation(out=S[:, 2:3], in_=S[:, 1:2], func=AF.Exp,
                                     scale=-(float(np.log(10000.0)) / 32.0)), deps=[t0])
    t1 = P.tok(ACT)
    P.op(POOL, lambda e: e.iota(A_.stg[:, 0:T], pattern=[[1, T]], base=0,
                                channel_multiplier=0,
                                allow_small_or_imprecise_dtypes=True))
    t2 = P.tok(POOL)
    ang = A_.stg[:, 0:T]
    tmp = A_.stg[:, T:2 * T]
    P.op(DVE, lambda e: e.tensor_scalar_mul(out=ang, in0=ang, scalar1=S[:, 2:3]),
         deps=[t1, t2])
    TWO_PI = float(2 * np.pi)
    iscr = A_.B.x[:, 4096:4096 + T].bitcast(mybir.dt.int32)[:, 0:T]
    P.flush(DVE)
    _emit_fmod(P, ang, ang, TWO_PI, iscr, tmp)
    P.flush(DVE)
    P.op(DVE, lambda e: e.tensor_single_scalar(out=tmp, in_=ang, scalar=float(np.pi),
                                               op=ALU.is_gt))
    P.flush(DVE)
    P.op(DVE, lambda e: e.scalar_tensor_tensor(out=ang, in0=tmp, scalar=-TWO_PI,
                                               in1=ang, op0=ALU.mult, op1=ALU.add))
    t3 = P.flush(DVE)
    P.op(ACT, lambda e: e.activation(out=tmp, in_=ang, func=AF.Sin), deps=[t3])
    t4 = P.tok(ACT)
    P.op(DVE, lambda e: e.tensor_copy(out=S[:, 5:6], in_=S[:, 0:1]), deps=[t_i])
    P.flush(DVE)
    _emit_fmod(P, S[:, 3:4], S[:, 5:6], 64.0, iscr1, S[:, 6:7])
    P.flush(DVE)
    P.op(DVE, lambda e: e.tensor_single_scalar(out=S[:, 3:4], in_=S[:, 3:4],
                                               scalar=31.5, op=ALU.is_gt))
    P.flush(DVE)
    P.op(DVE, lambda e: e.tensor_scalar(out=S[:, 3:4], in0=S[:, 3:4], scalar1=2.0,
                                        scalar2=-1.0, op0=ALU.mult, op1=ALU.add))
    t5 = P.flush(DVE)
    P.op(DVE, lambda e: e.tensor_scalar_mul(out=A_.S2t[:], in0=tmp,
                                            scalar1=S[:, 3:4]), deps=[t4, t5])
    P.op(DVE, lambda e: e.tensor_scalar_add(out=ang, in0=ang,
                                            scalar1=float(np.pi / 2)))
    P.flush(DVE)
    P.op(DVE, lambda e: e.tensor_single_scalar(out=tmp, in_=ang, scalar=float(np.pi),
                                               op=ALU.is_gt))
    P.flush(DVE)
    P.op(DVE, lambda e: e.scalar_tensor_tensor(out=ang, in0=tmp, scalar=-TWO_PI,
                                               in1=ang, op0=ALU.mult, op1=ALU.add))
    t6 = P.flush(DVE)
    P.op(ACT, lambda e: e.activation(out=A_.Ct[:], in_=ang, func=AF.Sin), deps=[t6])
    t7 = P.tok(ACT)
    return [t7, t5]


def emit_l0a_weights(P, A_, H, deps):
    stg = A_.stg
    items = [(H["a0_wlat"], A_.wlat, [128, 1024]),
             (H["a0_wlat2"], A_.wlat2, [128, 1024]),
             (H["a0_wqb"], A_.wqb, [64, 1024]),
             (H["a0_wqbs"], A_.wqbs, [64, 1024]),
             (H["a0_wkvb"], A_.wkvb, [32, 1280])]
    prev = list(deps)
    for (srch, dst, shp) in items:
        if shp[0] == 128:
            d = P.dma(SYNC, stg[:, 0:shp[1]], srch, deps=prev)
            P.op(DVE, lambda e, dst=dst, w=shp[1]: e.tensor_copy(
                out=dst[:], in_=A_.stg[:, 0:w]), deps=[d])
        else:
            d = P.dma(SYNC, stg[0:shp[0], 0:shp[1]], srch, deps=prev)
            P.op(DVE, lambda e, dst=dst, p=shp[0], w=shp[1]: e.tensor_copy(
                out=dst[:], in_=A_.stg[0:p, 0:w]), deps=[d])
        prev = [P.tok(DVE)]
    woutb = A_.stg[:, 2048:3072].bitcast(BF16)
    for pc in range(4):
        d = P.dma(SYNC, stg[:, 0:2048], H["a0_wout"][:, pc * 2048:(pc + 1) * 2048],
                  deps=prev)
        P.op(DVE, lambda e, w=woutb: e.tensor_copy(out=w, in_=A_.stg[:, 0:2048]),
             deps=[d])
        c1 = P.tok(DVE)
        prev = [P.dma(SYNC, H["a0_wout_bf"][:, pc * 2048:(pc + 1) * 2048], woutb,
                      deps=[c1]), c1]
    db = P.dma(SYNC, A_.bout[:], H["a0_bout"], deps=prev)
    return prev + [db]


def emit_l0_attn(P, B, A_, H, b, tok_xnT, tbl_toks, w_tok, xfree_tok,
                 stage="full", dbg=None):
    xnTv = B.xnT[:].rearrange("p (dc t) -> p dc t", t=T)
    qTv = A_.qT[:].rearrange("p (dc t) -> p dc t", t=T)
    scale = float(64 ** -0.5)
    deps0 = [tok_xnT, w_tok] + list(tbl_toks) + [xfree_tok]
    lat_toks = []
    for li, wl in enumerate([A_.wlat, A_.wlat2]):
        for tt in range(4):
            for k in range(ND):
                P.op(PE, lambda e, wl=wl, k=k, tt=tt: e.matmul(
                    B.psb(0), wl[:, k * 128:(k + 1) * 128],
                    xnTv[:, k, tt * 512:(tt + 1) * 512],
                    start=(k == 0), stop=(k == ND - 1)),
                    deps=(deps0 + lat_toks[-1:]))
            mt = P.tok(PE)
            P.op(DVE, lambda e, li=li, tt=tt: e.tensor_copy(
                out=A_.lat[:, li * T + tt * 512:li * T + (tt + 1) * 512],
                in_=B.psb(0)), deps=[mt])
            lat_toks.append(P.tok(DVE))
    t_lat = lat_toks[-1]
    rope_toks = []
    prev_ev = []
    for dc in range(ND):
        for tt in range(4):
            P.op(PE, lambda e, dc=dc, tt=tt: e.matmul(
                B.psb(tt % 2), A_.wqb[:, dc * 128:(dc + 1) * 128],
                A_.lat[0:64, tt * 512:(tt + 1) * 512], start=True, stop=True),
                deps=[t_lat] + prev_ev)
            P.op(PE, lambda e, dc=dc, tt=tt: e.matmul(
                B.psb(2 + tt % 2), A_.wqbs[:, dc * 128:(dc + 1) * 128],
                A_.lat[0:64, tt * 512:(tt + 1) * 512], start=True, stop=True))
            mt = P.tok(PE)
            P.op(DVE, lambda e, dc=dc, tt=tt: e.tensor_copy(
                out=qTv[:, dc, tt * 512:(tt + 1) * 512], in_=B.psb(tt % 2)),
                deps=[mt])
            P.op(DVE, lambda e, dc=dc, tt=tt: e.tensor_copy(
                out=A_.qs[:, tt * 512:(tt + 1) * 512], in_=B.psb(2 + tt % 2)))
            prev_ev = [P.tok(DVE)]
        P.op(DVE, lambda e, dc=dc: e.tensor_mul(out=qTv[:, dc, :], in0=qTv[:, dc, :],
                                                in1=A_.Ct[:]))
        P.op(DVE, lambda e, dc=dc: e.tensor_mul(out=A_.qs[:], in0=A_.qs[:],
                                                in1=A_.S2t[:]))
        P.flush(DVE)
        P.op(DVE, lambda e, dc=dc: e.tensor_add(out=qTv[:, dc, :], in0=qTv[:, dc, :],
                                                in1=A_.qs[:]))
        prev_ev = [P.tok(DVE)]
        rope_toks.append(prev_ev[0])
    kv_dsts = ([(A_.kT, g, False) for g in range(4)] +
               [(None, g, True) for g in range(4)] +
               [(A_.vT, 0, False), (A_.vT, 1, False)])
    ks_scr = A_.qs
    for mi, (dst, half, is_ks) in enumerate(kv_dsts):
        for tt in range(4):
            P.op(PE, lambda e, mi=mi, tt=tt: e.matmul(
                B.psb(4 + tt % 2), A_.wkvb[:, mi * 128:(mi + 1) * 128],
                A_.lat[0:32, T + tt * 512:T + (tt + 1) * 512], start=True, stop=True),
                deps=[t_lat] + prev_ev)
            mt = P.tok(PE)
            if is_ks:
                P.op(DVE, lambda e, mi=mi, tt=tt: e.tensor_copy(
                    out=ks_scr[:, tt * 512:(tt + 1) * 512], in_=B.psb(4 + tt % 2)),
                    deps=[mt])
            else:
                P.op(DVE, lambda e, dst=dst, half=half, tt=tt: e.tensor_copy(
                    out=dst[:, half * T + tt * 512:half * T + (tt + 1) * 512],
                    in_=B.psb(4 + tt % 2)), deps=[mt])
            prev_ev = [P.tok(DVE)]
        if is_ks:
            P.op(DVE, lambda e, half=half: e.tensor_mul(
                out=A_.kT[:, half * T:(half + 1) * T],
                in0=A_.kT[:, half * T:(half + 1) * T], in1=A_.Ct[:]))
            P.op(DVE, lambda e: e.tensor_mul(out=ks_scr[:], in0=ks_scr[:],
                                             in1=A_.S2t[:]))
            P.flush(DVE)
            P.op(DVE, lambda e, half=half: e.tensor_add(
                out=A_.kT[:, half * T:(half + 1) * T],
                in0=A_.kT[:, half * T:(half + 1) * T], in1=ks_scr[:]))
            prev_ev = [P.tok(DVE)]
    t_kv = prev_ev[0]
    t_q = rope_toks[-1]
    if stage == "proj":
        return t_kv
    P.op(POOL, lambda e: e.iota(A_.masks[:], pattern=[[0, 4], [1, 512]], base=0,
                                channel_multiplier=-1,
                                allow_small_or_imprecise_dtypes=True),
         deps=[xfree_tok])
    t8m = P.tok(POOL)
    for r in range(4):
        P.op(DVE, lambda e, r=r: e.tensor_single_scalar(
            out=A_.masks[:, r * 512:(r + 1) * 512],
            in_=A_.masks[:, r * 512:(r + 1) * 512],
            scalar=128.0 * r - 0.5, op=ALU.is_gt), deps=[t8m])
    t_masks = P.flush(DVE)
    vtokv = A_.vtok.rearrange("p (g tc c) -> p g tc c", g=4, c=128)
    pv_prev = [t_kv, xfree_tok]
    for g in range(4):
        for tc in range(NT):
            bank = (g * NT + tc) % 4
            vb = (g % 2) * 64
            P.op(PE, lambda e, g=g, tc=tc, bank=bank, vb=vb: e.transpose(
                B.psb(bank, 1, BF16)[:, 0:64],
                A_.vT[vb:vb + 64,
                      (g // 2) * T + tc * 128:(g // 2) * T + (tc + 1) * 128],
                B.ident[vb:vb + 64, vb:vb + 64]),
                deps=pv_prev if (g == 0 and tc == 0) else
                ([pv_toks[-4]] if g * NT + tc >= 4 else []))
            mt = P.tok(PE)
            P.op(DVE, lambda e, g=g, tc=tc, bank=bank: e.tensor_copy(
                out=vtokv[:, g, tc, 0:64], in_=B.psb(bank, 1, BF16)[:, 0:64]),
                deps=[mt])
            P.op(DVE, lambda e, g=g, tc=tc, bank=bank: e.tensor_copy(
                out=vtokv[:, g, tc, 64:128], in_=B.psb(bank, 1, BF16)[:, 0:64]))
            if g == 0 and tc == 0:
                pv_toks = []
            pv_toks.append(P.tok(DVE))
    t_vtok = pv_toks[-1]
    if stage == "vtok":
        return t_vtok
    S = A_.nsc
    P.op(POOL, lambda e: e.iota(A_.rb[:, 0:128], pattern=[[1, 128]], base=0,
                                channel_multiplier=0,
                                allow_small_or_imprecise_dtypes=True), deps=[t_vtok])
    P.op(POOL, lambda e: e.tensor_single_scalar(out=A_.rb[:, 0:128],
                                                in_=A_.rb[:, 0:128],
                                                scalar=63.5, op=ALU.is_le))
    P.op(POOL, lambda e: e.iota(A_.rb[:, 128:129], pattern=[[0, 1]], base=0,
                                channel_multiplier=1,
                                allow_small_or_imprecise_dtypes=True))
    P.op(POOL, lambda e: e.tensor_single_scalar(out=A_.rb[:, 128:129],
                                                in_=A_.rb[:, 128:129],
                                                scalar=63.5, op=ALU.is_le))
    t_bd0 = P.tok(POOL)
    P.op(DVE, lambda e: e.tensor_scalar_mul(out=A_.rb[:, 256:384], in0=A_.rb[:, 0:128],
                                            scalar1=A_.rb[:, 128:129]), deps=[t_bd0])
    P.op(DVE, lambda e: e.tensor_scalar_mul(out=A_.rb[:, 256:384],
                                            in0=A_.rb[:, 256:384], scalar1=2.0))
    P.op(DVE, lambda e: e.tensor_sub(out=A_.rb[:, 256:384], in0=A_.rb[:, 256:384],
                                     in1=A_.rb[:, 0:128]))
    P.op(DVE, lambda e: e.tensor_scalar(out=A_.rb[:, 256:384], in0=A_.rb[:, 256:384],
                                        scalar1=A_.rb[:, 128:129], scalar2=1.0,
                                        op0=ALU.subtract, op1=ALU.add))
    t_bd1 = P.flush(DVE)
    P.op(DVE, lambda e: e.tensor_copy(out=A_.masks[:].bitcast(F32)[:, 960:1024]
                                      if False else A_.qs[:].bitcast(BF16)[:, 0:128],
                                      in_=A_.rb[:, 256:384]), deps=[t_bd1])
    t_bd = P.flush(DVE)
    bd_bf = A_.qs[:].bitcast(BF16)[:, 0:128]
    def norm_cols(srcv, nchunks, col0, dep):
        toks = [dep]
        for c in range(nchunks):
            P.op(ACT, lambda e, c=c: e.activation(
                out=A_.rb[:, 0:512].bitcast(BF16) if False else A_.po[:].bitcast(BF16),
                in_=srcv(c)[:, 0:512], func=AF.Square), deps=toks[-1:])
            pass
        return None
    def emit_norms(get_chunk, nchunks, col0, dep0):
        dts = list(dep0) if isinstance(dep0, list) else [dep0]
        sq = A_.po[:].bitcast(BF16)
        for c in range(nchunks):
            for tt in range(4):
                P.op(ACT, lambda e, c=c, tt=tt: e.activation(
                    out=sq[:, 0:512], in_=get_chunk(c)[:, tt * 512:(tt + 1) * 512],
                    func=AF.Square), deps=dts[-1:] + [t_bd] + dts[0:1])
                ta = P.tok(ACT)
                P.op(PE, lambda e: e.matmul(B.psb(6), bd_bf, sq[:, 0:512],
                                            start=True, stop=True), deps=[ta])
                mt = P.tok(PE)
                P.op(DVE, lambda e, c=c, tt=tt, col0=col0: e.reduce_max(
                    out=S[:, col0 + 8 + tt:col0 + 9 + tt], in_=B.psb(6), axis=AX.X),
                    deps=[mt])
                dts.append(P.flush(DVE))
            P.op(DVE, lambda e, c=c, col0=col0: e.reduce_max(
                out=S[:, col0 + c:col0 + c + 1], in_=S[:, col0 + 8:col0 + 12],
                axis=AX.X))
            dts.append(P.flush(DVE))
        return dts[-1]
    tq_n = emit_norms(lambda c: qTv[:, c, :], ND, 16, t_q)
    if stage == "norms1":
        return tq_n
    tk_n = emit_norms(lambda c: A_.kT[:, c * T:(c + 1) * T], 4, 32, [t_kv, tq_n])
    P.op(DVE, lambda e: e.reduce_max(out=S[:, 48:49], in_=S[:, 16:24], axis=AX.X),
         deps=[tq_n])
    P.op(DVE, lambda e: e.reduce_max(out=S[:, 49:50], in_=S[:, 32:36], axis=AX.X),
         deps=[tk_n])
    P.op(DVE, lambda e: e.memset(S[:, 64:192], 0.0))
    t_m0 = P.flush(DVE)
    P.op(DVE, lambda e: e.tensor_copy(out=S[:, 64:65], in_=S[:, 48:49]), deps=[t_m0])
    P.op(DVE, lambda e: e.tensor_copy(out=S[:, 65:66], in_=S[:, 49:50]))
    P.op(DVE, lambda e: e.tensor_copy(out=A_.rb[:, 0:128].bitcast(BF16)[:, 0:128]
                                      if False else A_.qs[:].bitcast(BF16)[:, 128:256],
                                      in_=S[:, 64:192]), deps=[])
    t_m1 = P.flush(DVE)
    P.op(PE, lambda e: e.transpose(B.psb(6, 1, BF16)[:, 0:128],
                                   A_.qs[:].bitcast(BF16)[:, 128:256], B.ident[:]),
         deps=[t_m1])
    t_m2 = P.tok(PE)
    P.op(DVE, lambda e: e.tensor_copy(out=S[:, 96:224] if False else S[:, 96:96 + 128],
                                      in_=B.psb(6, 1, BF16)[:, 0:128]), deps=[t_m2])
    t_m3 = P.flush(DVE)
    P.op(DVE, lambda e: e.reduce_max(out=S[:, 50:51], in_=S[:, 96:224],
                                     axis=AX.X), deps=[t_m3])
    t_m4 = P.flush(DVE)
    if stage == "norms2":
        return t_m4
    P.op(ACT, lambda e: e.activation(out=S[:, 51:52], in_=S[:, 50:51],
                                     func=AF.Ln), deps=[t_m4])
    t_m5 = P.tok(ACT)
    P.op(DVE, lambda e: e.tensor_copy(out=A_.qs[:].bitcast(BF16)[0:2, 256:257],
                                      in_=S[0:2, 51:52]), deps=[t_m5])
    t_m6 = P.flush(DVE)
    P.op(PE, lambda e: e.matmul(B.psb(6)[:, 0:1], B.ones[0:2, :],
                                A_.qs[:].bitcast(BF16)[0:2, 256:257],
                                start=True, stop=True), deps=[t_m6])
    t_m7 = P.tok(PE)
    P.op(ACT, lambda e: e.activation(out=S[:, 52:53], in_=B.psb(6)[:, 0:1],
                                     func=AF.Exp, scale=0.5), deps=[t_m7])
    t_m8 = P.tok(ACT)
    P.op(DVE, lambda e: e.tensor_scalar_mul(out=S[:, 53:54], in0=S[:, 52:53],
                                            scalar1=-scale), deps=[t_m8])
    t_mhat = P.flush(DVE)
    if stage == "mhat":
        return t_mhat
    maskv = A_.masks[:].rearrange("p (r c) -> p r c", c=512)
    PTv = A_.PT.rearrange("p (tc c) -> p tc c", c=512)
    t_out_prev = []
    kTv = A_.kT
    NH = 1 if stage == "attn1h" else 16
    for h in range(NH):
        g = h // 4
        qbase = (h % 2) * 64
        kbase = qbase
        koff = g * T
        qchunk = h // 2
        for qb in range(4):
            ntc = 4 * (qb + 1)
            pt_toks = []
            for tc in range(ntc):
                bank = tc % 2
                deps = [t_mhat, t_q, t_kv]
                if tc >= 2:
                    deps.append(pt_toks[tc - 2])
                if t_out_prev:
                    deps.append(t_out_prev[-1])
                P.op(PE, lambda e, tc=tc, bank=bank, kb=kbase, ko=koff,
                     qb_=qbase, qc=qchunk, qq=qb: e.matmul(
                    B.psb(bank),
                    kTv[kb:kb + 64, ko + tc * 128:ko + (tc + 1) * 128],
                    A_.qT[qb_:qb_ + 64, qc * T + qq * 512:qc * T + (qq + 1) * 512],
                    start=True, stop=True), deps=deps)
                mt = P.tok(PE)
                P.op(ACT, lambda e, tc=tc, bank=bank: e.activation(
                    out=PTv[:, tc, :], in_=B.psb(bank), func=AF.Exp,
                    bias=S[:, 53:54], scale=scale), deps=[mt])
                ta = P.tok(ACT)
                if tc >= ntc - 4:
                    r = tc - (ntc - 4)
                    P.op(DVE, lambda e, tc=tc, r=r: e.tensor_mul(
                        out=PTv[:, tc, :], in0=PTv[:, tc, :], in1=maskv[:, r, :]),
                        deps=[ta, t_masks])
                    pt_toks.append(P.tok(DVE))
                else:
                    pt_toks.append(ta)
            for tc in range(ntc):
                P.op(PE, lambda e, tc=tc, g=g, ntc=ntc: e.matmul(
                    B.psb(7), vtokv[:, g, tc, :], PTv[:, tc, :],
                    start=(tc == 0), stop=(tc == ntc - 1)),
                    deps=[pt_toks[tc]])
                P.op(PE, lambda e, tc=tc, ntc=ntc: e.matmul(
                    B.psb(6), B.ones[:], PTv[:, tc, :],
                    start=(tc == 0), stop=(tc == ntc - 1)))
            mt = P.tok(PE)
            P.op(DVE, lambda e: e.tensor_copy(out=A_.po[:], in_=B.psb(7)), deps=[mt])
            P.op(DVE, lambda e: e.tensor_copy(out=A_.rb[:], in_=B.psb(6)))
            t_po = P.flush(DVE)
            P.op(DVE, lambda e, qb_=qbase: e.reciprocal(
                out=A_.rb[qb_:qb_ + 64, :] if qb_ else A_.rb[0:64, :],
                in_=A_.rb[qb_:qb_ + 64, :] if qb_ else A_.rb[0:64, :]), deps=[t_po])
            t_r = P.flush(DVE)
            P.op(DVE, lambda e, qb_=qbase, qc=qchunk, qq=qb: e.tensor_mul(
                out=A_.attnT[qb_:qb_ + 64,
                             qc * T + qq * 512:qc * T + (qq + 1) * 512],
                in0=A_.po[qb_:qb_ + 64, :], in1=A_.rb[qb_:qb_ + 64, :]), deps=[t_r])
            t_out_prev = [P.flush(DVE)]
    t_attn = t_out_prev[-1]
    if stage == "attn1h":
        return t_attn
    if dbg is not None and b == 0:
        dd = [P.dma(POOL, dbg["Ct"], A_.Ct[:], deps=[t_attn]),
              P.dma(POOL, dbg["S2t"], A_.S2t[:], deps=[t_attn]),
              P.dma(POOL, dbg["qT"], A_.qT[:], deps=[t_attn]),
              P.dma(POOL, dbg["kT"], A_.kT[:], deps=[t_attn]),
              P.dma(POOL, dbg["nsc"][:, 52:54], A_.nsc[:, 52:54], deps=[t_attn]),
              P.dma(POOL, dbg["attnT"], A_.attnT[:, 0:ND * T], deps=[t_attn])]
        t_attn2 = P.tok(POOL)
        P.op(POOL, lambda e: e.nop(), deps=dd)
        t_attn = P.tok(POOL)
    out_toks = []
    attnTv = A_.attnT[:].rearrange("p (dc t) -> p dc t", t=T)
    outTv = A_.outT.rearrange("p (dc t) -> p dc t", t=T)
    wsrc = H["a0_wout_bf"].rearrange("p (k m c) -> p k m c", m=8, c=128)
    wobv = A_.wob[:].rearrange("p (pb k c) -> p pb k c", pb=2, c=128)
    wtoks = {}
    for dc in range(ND):
        par = dc % 2
        wdep = [wtoks[dc - 2]] if dc >= 2 else []
        wd = P.dma(SYNC, wobv[:, par, :, :], wsrc[:, :, dc, :],
                   deps=wdep + [t_attn])
        for tt in range(4):
            bank = (dc * 4 + tt) % 2
            for k in range(ND):
                P.op(PE, lambda e, par=par, tt=tt, k=k, bank=bank: e.matmul(
                    B.psb(bank), wobv[:, par, k, :],
                    attnTv[:, k, tt * 512:(tt + 1) * 512],
                    start=(k == 0), stop=(k == ND - 1)),
                    deps=([wd, t_attn] + out_toks[-2:-1]) if k == 0 else [])
            mt = P.tok(PE)
            if tt == 3:
                wtoks[dc] = mt
            P.op(DVE, lambda e, dc=dc, tt=tt, bank=bank: e.tensor_scalar_add(
                out=outTv[:, dc, tt * 512:(tt + 1) * 512], in0=B.psb(bank),
                scalar1=A_.bout[:, dc:dc + 1]), deps=[mt])
            out_toks.append(P.tok(DVE))
    return out_toks[-1]


def build_l0attn_launch(nc):
    x_h = nc.dram_tensor("x", [B_PER_CORE, T, D], F32, kind="ExternalInput").ap()
    y_h = nc.dram_tensor("y", [B_PER_CORE, T, D], F32, kind="ExternalOutput").ap()
    H = decl_l0attn_hbm(nc)
    es = contextlib.ExitStack()
    with es:
        B = Bufs(nc, es)
        A_ = AttnBufs(nc, es, B)
        P = Prog(nc, es)
        ct = emit_const_setup(P, B)
        tbl = emit_l0a_tables(P, A_, ct)
        wt = emit_l0a_weights(P, A_, H, tbl)
        prev = list(wt)
        final_stores = []
        for b in range(B_PER_CORE):
            xtoks = emit_load_x(P, B, x_h, b, deps=prev)
            txn, txnT = emit_ln(P, B, ct, xtoks, war_deps=prev if b > 0 else ())
            stage = os.environ.get("A0_STAGE", "full")
            dbg = None
            if os.environ.get("A0_DBG", "0") == "1" and b == 0:
                dbg = {"qT": nc.dram_tensor("dbg_qT", [128, 8 * T], BF16,
                                            kind="ExternalOutput").ap(),
                       "kT": nc.dram_tensor("dbg_kT", [128, 4 * T], BF16,
                                            kind="ExternalOutput").ap(),
                       "nsc": nc.dram_tensor("dbg_nsc", [128, 256], F32,
                                             kind="ExternalOutput").ap(),
                       "attnT": nc.dram_tensor("dbg_attnT", [128, 8 * T], BF16,
                                               kind="ExternalOutput").ap(),
                       "Ct": nc.dram_tensor("dbg_Ct", [128, T], BF16,
                                            kind="ExternalOutput").ap(),
                       "S2t": nc.dram_tensor("dbg_S2t", [128, T], BF16,
                                             kind="ExternalOutput").ap()}
            t_outT = emit_l0_attn(P, B, A_, H, b, txnT, tbl, wt[0], txn, stage=stage,
                                  dbg=dbg)
            if stage != "full":
                yv0 = y_h[b].rearrange("(tc p) d -> p tc d", p=128)
                xv0 = B.stats
                sts = [P.dma(SYNC, yv0[:, tc, 0:256], B.stats[:],
                             deps=[t_outT]) for tc in range(NT)]
                P.op(SYNC, lambda e: e.nop(), deps=sts)
                with nc.Block() as block:
                    P.emit(block)
                return nc
            xtoks2 = emit_load_x(P, B, x_h, b, deps=[t_outT])
            yv = y_h[b].rearrange("(tc p) d -> p tc d", p=128)
            store_y = [yv[:, tc, :] for tc in range(NT)]
            tok_x, st_toks, _ = emit_residual_add(P, B, A_.outT, t_outT, xtoks2,
                                                  store_y=store_y)
            final_stores += st_toks
            prev = [tok_x] + st_toks
        P.op(SYNC, lambda e: e.nop(), deps=final_stores)
        with nc.Block() as block:
            P.emit(block)
    return nc


def host_prep_l1ffn(inp):
    f = _as32
    w1 = f(inp["L1_ffn_w1"])
    w2 = f(inp["L1_ffn_w2"])
    wih = f(inp["L1_ffn_gru_wih"])
    whh = f(inp["L1_ffn_gru_whh"])
    wv = f(inp["L1_ffn_wv"])
    wo = f(inp["L1_ffn_wo"])
    d = {}
    w1a = np.ascontiguousarray(w1[:, 0:1024].T)
    d["f1_w1a"] = np.ascontiguousarray(
        w1a.reshape(8, 128, 4, 128).transpose(1, 0, 2, 3).reshape(128, 4096))
    w1b = np.ascontiguousarray(w1[:, 1024:2048].T)
    d["f1_w1b"] = np.ascontiguousarray(
        w1b.reshape(8, 128, 4, 128).transpose(1, 0, 2, 3).reshape(128, 4096))
    w2tp = np.zeros((512, 128), np.float32)
    w2tp[:, 0:64] = w2.T
    d["f1_w2tp"] = np.ascontiguousarray(
        w2tp.reshape(4, 128, 128).transpose(1, 0, 2).reshape(128, 512))
    lay = lambda W, M: np.ascontiguousarray(
        W.T.reshape(8, 128, M, 128).transpose(1, 2, 0, 3).reshape(128, M * 8 * 128))
    d["f1_wih"] = lay(wih, 24)
    d["f1_whh"] = lay(whh, 24)
    d["f1_wv"] = lay(wv, 8)
    d["f1_wo"] = lay(wo, 8)
    d["f1_b1"] = np.ascontiguousarray(f(inp["L1_ffn_b1"]).reshape(4, 128).T)
    b2p = np.zeros((128, 1), np.float32)
    b2p[0:64, 0] = f(inp["L1_ffn_b2"])
    d["f1_b2"] = b2p
    d["f1_bih"] = np.ascontiguousarray(f(inp["L1_ffn_gru_bih"]).reshape(24, 128).T)
    d["f1_bhh"] = np.ascontiguousarray(f(inp["L1_ffn_gru_bhh"]).reshape(24, 128).T)
    d["f1_bv"] = np.ascontiguousarray(f(inp["L1_ffn_bv"]).reshape(8, 128).T)
    d["f1_bo"] = np.ascontiguousarray(f(inp["L1_ffn_bo"]).reshape(8, 128).T)
    d["f1_sinitT"] = np.ascontiguousarray(
        f(inp["L1_ffn_slot_init"]).T.reshape(8, 128, 64).transpose(1, 0, 2).reshape(128, 512))
    d["f1_sscale"] = np.ascontiguousarray(f(inp["L1_ffn_slot_scale"]).reshape(1, 64))
    d["f1_tau"] = f(inp["L1_ffn_tau"]).reshape(1, 1)
    return d


def decl_l1ffn_hbm(nc):
    h = {}
    for nm, shp in [("f1_w1a", [128, 4096]), ("f1_w1b", [128, 4096]),
                    ("f1_w2tp", [128, 512]), ("f1_wih", [128, 24576]),
                    ("f1_whh", [128, 24576]), ("f1_wv", [128, 8192]),
                    ("f1_wo", [128, 8192]), ("f1_b1", [128, 4]), ("f1_b2", [128, 1]),
                    ("f1_bih", [128, 24]), ("f1_bhh", [128, 24]),
                    ("f1_bv", [128, 8]), ("f1_bo", [128, 8]),
                    ("f1_sinitT", [128, 512]), ("f1_sscale", [1, 64]),
                    ("f1_tau", [1, 1])]:
        h[nm] = nc.dram_tensor(nm, shp, F32, kind="ExternalInput").ap()
    for nm, shp in [("f1_wih_bf", [128, 24576]), ("f1_whh_bf", [128, 24576]),
                    ("f1_wv_bf", [128, 8192]), ("f1_wo_bf", [128, 8192])]:
        h[nm] = nc.dram_tensor(nm, shp, BF16, kind="Internal").ap()
    if os.environ.get("F1_DBG"):
        for nm, shp, dt in [("d1_log", [128, 1024], F32), ("d1_atok", [128, 2048], BF16),
                            ("d1_aT", [128, 2048], BF16), ("d1_gi", [128, 1024], BF16),
                            ("d1_SI", [128, 1024], F32), ("d1_ST", [128, 512], BF16),
                            ("d1_SnT", [128, 512], BF16), ("d1_svT", [128, 1024], BF16),
                            ("d1_ghT", [128, 1536], F32), ("d1_hT", [128, 8192], BF16),
                            ("d1_outT", [128, 16384], BF16), ("d1_c16", [128, 96], F32), ("d1_cols", [128, 96], F32)]:
            h[nm] = nc.dram_tensor(nm, shp, dt, kind="ExternalOutput").ap()
    return h


class F1Bufs:
    def __init__(self, nc, es, B):
        A = lambda name, shape, dt: es.enter_context(nc.sbuf_tensor(name, shape, dt))
        self.B = B
        self.stg = B.xn[:].bitcast(F32)[:, 0:4096]
        self.outT = B.xnT[:]
        self.hT = A("f1_hT", [128, 4 * T], BF16)
        self.ghT = A("f1_ghT", [128, 24 * 64], F32)
        self.logtok = A("f1_logtok", [128, 16 * 64], F32)
        self.atok = A("f1_atok", [128, 16 * 128], BF16)
        self.aT = A("f1_aT", [128, 2048], BF16)
        self.giT = A("f1_giT", [128, 8 * 64], BF16)
        self.ST = A("f1_ST", [128, 8 * 64], BF16)
        self.SnT = A("f1_SnT", [128, 8 * 64], BF16)
        self.svT = A("f1_svT", [128, 8 * 128], BF16)
        self.svs = A("f1_svs", [128, 8 * 128], BF16)
        self.gsc = A("f1_gsc", [128, 16 * 16], F32)
        self.wsb = A("f1_wsb", [128, 2 * 1024], BF16)
        self.w1a = A("f1_w1a_sb", [128, 4096], BF16)
        self.w1b = A("f1_w1b_sb", [128, 4096], BF16)
        self.w2tp = A("f1_w2tp_sb", [128, 512], BF16)
        self.cols = A("f1_cols", [128, 96], F32)
        self.sg = A("f1_sg", [128, 256], F32)
        self.nb = A("f1_nb", [128, 512], F32)
        self.bvb = A("f1_bvb", [128, 8], BF16)
        self.gi = A("f1_gi", [128, 1024], BF16)
        self.SI = A("f1_SI", [128, 1024], F32)


def emit_f1_prologue(P, B, F, H, deps):
    stg = F.stg
    prev = list(deps)
    small = [("f1_b1", F.cols[:, 0:4]), ("f1_b2", F.cols[:, 8:9]),
             ("f1_bih", F.cols[:, 16:40]), ("f1_bhh", F.cols[:, 40:64]),
             ("f1_bv", F.cols[:, 64:72]), ("f1_bo", F.cols[:, 72:80])]
    toks = [P.dma(SYNC, dst, H[nm], deps=deps) for nm, dst in small]
    t_tau = P.dma(SYNC, F.cols[0:1, 9:10], H["f1_tau"], deps=deps)
    toks.append(t_tau)
    for nm, dst, w in [("f1_w1a", F.w1a, 4096), ("f1_w1b", F.w1b, 4096),
                       ("f1_w2tp", F.w2tp, 512)]:
        d = P.dma(SYNC, stg[:, 0:w], H[nm], deps=prev)
        P.op(DVE, lambda e, dst=dst, w=w: e.tensor_copy(out=dst[:], in_=F.stg[:, 0:w]),
             deps=[d])
        prev = [P.tok(DVE)]
    dsi = P.dma(SYNC, stg[:, 0:512], H["f1_sinitT"], deps=prev)
    dsc = P.dma(SYNC, stg[0:1, 512:576], H["f1_sscale"], deps=prev)
    P.op(DVE, lambda e: e.tensor_copy(out=F.sg[0:1, 0:64].bitcast(BF16)[0:1, 0:64],
                                      in_=F.stg[0:1, 512:576]), deps=[dsc])
    t0 = P.flush(DVE)
    P.op(PE, lambda e: e.matmul(B.psb(6)[:, 0:64], B.ones[0:1, :],
                                F.sg[0:1, 0:64].bitcast(BF16)[0:1, 0:64],
                                start=True, stop=True), deps=[t0])
    t1 = P.tok(PE)
    P.op(DVE, lambda e: e.tensor_copy(out=F.sg[:, 0:64],
                                      in_=B.psb(6)[:, 0:64]), deps=[t1, dsi])
    t2 = P.flush(DVE)
    for dc in range(ND):
        P.op(DVE, lambda e, dc=dc: e.tensor_mul(
            out=F.ST[:, dc * 64:(dc + 1) * 64],
            in0=F.stg[:, dc * 64:(dc + 1) * 64],
            in1=F.sg[:, 0:64]), deps=[t2])
    t3 = P.flush(DVE)
    prev = prev + [t3]
    wb = F.stg[:, 2048:3072].bitcast(BF16)
    for nm, dstnm, npieces in [("f1_wih", "f1_wih_bf", 12), ("f1_whh", "f1_whh_bf", 12),
                               ("f1_wv", "f1_wv_bf", 4), ("f1_wo", "f1_wo_bf", 4)]:
        for pc in range(npieces):
            d = P.dma(SYNC, stg[:, 0:2048], H[nm][:, pc * 2048:(pc + 1) * 2048],
                      deps=prev)
            P.op(DVE, lambda e: e.tensor_copy(out=wb[:], in_=F.stg[:, 0:2048]),
                 deps=[d])
            c1 = P.tok(DVE)
            prev = [P.dma(SYNC, H[dstnm][:, pc * 2048:(pc + 1) * 2048], wb,
                          deps=[c1]), c1]
    return prev + toks + [t3]


def emit_l1_ffn(P, B, F, H, b, tok_xn, tok_xnT, pro_tok):
    xnTv = B.xnT[:].rearrange("p (dc t) -> p dc t", t=T)
    xnv = B.xn[:].rearrange("p (tc d) -> p tc d", d=D)
    hTv = F.hT[:].rearrange("p (m t) -> p m t", t=T)
    C = F.cols
    deps0 = [tok_xn, tok_xnT] + list(pro_tok)
    P.op(ACT, lambda e: e.activation(out=C[0:1, 10:11], in_=C[0:1, 9:10],
                                     func=AF.Abs), deps=deps0)
    ta = P.tok(ACT)
    P.op(DVE, lambda e: e.tensor_scalar_add(out=C[0:1, 10:11], in0=C[0:1, 10:11],
                                            scalar1=0.1), deps=[ta])
    P.flush(DVE)
    P.op(DVE, lambda e: e.reciprocal(out=C[0:1, 11:12], in_=C[0:1, 10:11]))
    P.flush(DVE)
    P.op(DVE, lambda e: e.tensor_copy(out=F.gi[0:1, 8:9],
                                      in_=C[0:1, 11:12]))
    t0 = P.flush(DVE)
    P.op(PE, lambda e: e.matmul(B.psb(6)[:, 0:1], B.ones[0:1, :],
                                F.gi[0:1, 8:9],
                                start=True, stop=True), deps=[t0])
    t1 = P.tok(PE)
    P.op(DVE, lambda e: e.tensor_copy(out=C[:, 12:13], in_=B.psb(6)[:, 0:1]),
         deps=[t1])
    t_invt = P.flush(DVE)
    for dc in range(ND):
        P.op(DVE, lambda e, dc=dc: e.reduce_sum(out=C[:, 80 + dc:81 + dc],
                                                in_=F.ST[:, dc * 64:(dc + 1) * 64],
                                                axis=AX.X), deps=deps0)
    P.flush(DVE)
    P.op(DVE, lambda e: e.tensor_scalar_mul(out=C[:, 80:88], in0=C[:, 80:88],
                                            scalar1=1.0 / 64.0))
    P.flush(DVE)
    P.op(DVE, lambda e: e.tensor_copy(out=F.gi[:, 0:8],
                                      in_=C[:, 80:88]))
    t_sm = P.flush(DVE)
    ev = t_sm
    for hc in range(4):
        for k in range(ND):
            P.op(PE, lambda e, hc=hc, k=k: e.matmul(
                B.psb(7)[:, 0:1], F.w1b[:, (k * 4 + hc) * 128:(k * 4 + hc) * 128 + 128],
                F.gi[:, k:k + 1], start=(k == 0), stop=(k == ND - 1)),
                deps=[ev] if k == 0 else [])
        mt = P.tok(PE)
        P.op(DVE, lambda e, hc=hc: e.tensor_copy(out=C[:, 4 + hc:5 + hc],
                                                 in_=B.psb(7)[:, 0:1]), deps=[mt])
        ev = P.tok(DVE)
    P.flush(DVE)
    P.op(DVE, lambda e: e.tensor_add(out=C[:, 4:8], in0=C[:, 4:8], in1=C[:, 0:4]))
    t_c1 = P.flush(DVE)
    wstr = F.wsb[:].rearrange("p (pb k c) -> p pb k c", pb=2, c=128)
    whh_src = H["f1_whh_bf"].rearrange("p (m k c) -> p m k c", k=8, c=128)
    wih_src = H["f1_wih_bf"].rearrange("p (m k c) -> p m k c", k=8, c=128)
    wv_src = H["f1_wv_bf"].rearrange("p (m k c) -> p m k c", k=8, c=128)
    wo_src = H["f1_wo_bf"].rearrange("p (m k c) -> p m k c", k=8, c=128)
    wtk = {}
    prev_ev = [t_c1]
    for m in range(24):
        par = m % 2
        wdep = [wtk[m - 2]] if m >= 2 else [t_c1]
        wd = P.dma(SYNC, wstr[:, par, :, :], whh_src[:, m, :, :],
                   deps=wdep + list(pro_tok))
        for k in range(ND):
            P.op(PE, lambda e, par=par, k=k, m=m: e.matmul(
                B.psb(m % 2)[:, 0:64], wstr[:, par, k, :],
                F.ST[:, k * 64:(k + 1) * 64], start=(k == 0), stop=(k == ND - 1)),
                deps=[wd] + prev_ev if k == 0 else [])
        mt = P.tok(PE)
        wtk[m] = mt
        P.op(DVE, lambda e, m=m: e.tensor_scalar_add(
            out=F.ghT[:, m * 64:(m + 1) * 64], in0=B.psb(m % 2)[:, 0:64],
            scalar1=C[:, 40 + m:41 + m]), deps=[mt])
        prev_ev = [P.tok(DVE)]
    t_gh = prev_ev[0]
    h_toks = []
    for m in range(4):
        for tt in range(4):
            bank = (m * 4 + tt) % 2 + 2
            for k in range(ND):
                P.op(PE, lambda e, m=m, tt=tt, k=k, bank=bank: e.matmul(
                    B.psb(bank), F.w1a[:, (k * 4 + m) * 128:(k * 4 + m) * 128 + 128],
                    xnTv[:, k, tt * 512:(tt + 1) * 512],
                    start=(k == 0), stop=(k == ND - 1)),
                    deps=([t_c1, tok_xnT] + h_toks[-2:-1]) if k == 0 else [])
            mt = P.tok(PE)
            P.op(ACT, lambda e, m=m, tt=tt, bank=bank: e.activation(
                out=hTv[:, m, tt * 512:(tt + 1) * 512], in_=B.psb(bank), func=AF.Gelu,
                bias=C[:, 4 + m:5 + m]), deps=[mt])
            h_toks.append(P.tok(ACT))
    t_h = h_toks[-1]
    lg_toks = []
    for tt in range(4):
        for k in range(4):
            P.op(PE, lambda e, tt=tt, k=k: e.matmul(
                B.psb(4), F.w2tp[:, k * 128:(k + 1) * 128],
                hTv[:, k, tt * 512:(tt + 1) * 512],
                start=(k == 0), stop=(k == 3)),
                deps=([t_h] + lg_toks[-1:]) if k == 0 else [])
        mt = P.tok(PE)
        P.op(DVE, lambda e, tt=tt: e.tensor_scalar(
            out=F.SI[:, 0:512], in0=B.psb(4), scalar1=C[:, 8:9], scalar2=C[:, 12:13],
            op0=ALU.add, op1=ALU.mult), deps=[mt, t_invt])
        ct = P.flush(DVE)
        P.op(DVE, lambda e, tt=tt: e.tensor_copy(
            out=F.svs[:, 0:512], in_=F.SI[:, 0:512]), deps=[ct])
        ct2 = P.flush(DVE)
        for bk in range(4):
            bank = 5 + bk % 2
            P.op(PE, lambda e, tt=tt, bk=bk, bank=bank: e.transpose(
                B.psb(bank, 1, BF16)[:, 0:128], F.svs[:, bk * 128:(bk + 1) * 128],
                B.ident[:]), deps=[ct2] + (lg_toks[-1:] if (tt or bk) else []))
            mt2 = P.tok(PE)
            P.op(DVE, lambda e, tt=tt, bk=bk, bank=bank: e.tensor_scalar_add(
                out=F.logtok[:, (tt * 4 + bk) * 64:(tt * 4 + bk + 1) * 64],
                in0=B.psb(bank, 1, BF16)[:, 0:64], scalar1=0.0), deps=[mt2])
            lg_toks.append(P.tok(DVE))
    t_log = lg_toks[-1]
    G = F.gsc
    P.op(DVE, lambda e: e.memset(F.atok[:], 0.0), deps=deps0)
    t_az = P.flush(DVE)
    for blk in range(16):
        P.op(DVE, lambda e, blk=blk: e.max(out=G[:, blk * 16:blk * 16 + 8],
                                           in_=F.logtok[:, blk * 64:(blk + 1) * 64]),
             deps=[t_log])
    P.flush(DVE)
    for blk in range(16):
        P.op(DVE, lambda e, blk=blk: e.tensor_scalar_mul(
            out=G[:, blk * 16 + 8:blk * 16 + 9], in0=G[:, blk * 16:blk * 16 + 1],
            scalar1=-1.0))
        P.op(DVE, lambda e, blk=blk: e.tensor_single_scalar(
            out=F.atok[:, blk * 128:blk * 128 + 64].bitcast(F32)[:, 0:32] if False
            else F.SI[:, 0:64],
            in_=F.logtok[:, blk * 64:(blk + 1) * 64],
            scalar=0.0, op=ALU.is_ge))
    t_neg = P.flush(DVE)
    msk_toks = []
    for blk in range(16):
        P.op(DVE, lambda e, blk=blk: e.tensor_scalar(
            out=F.SI[:, 64 + blk * 0:128] if False else F.SI[:, 64:128],
            in0=F.logtok[:, blk * 64:(blk + 1) * 64],
            scalar1=G[:, blk * 16 + 7:blk * 16 + 8], scalar2=None, op0=ALU.is_ge),
            deps=[t_neg])
        tmsk = P.flush(DVE)
        P.op(ACT, lambda e, blk=blk: e.activation(
            out=F.logtok[:, blk * 64:(blk + 1) * 64],
            in_=F.logtok[:, blk * 64:(blk + 1) * 64], func=AF.Exp,
            bias=G[:, blk * 16 + 8:blk * 16 + 9]), deps=[tmsk])
        tae = P.tok(ACT)
        P.op(DVE, lambda e, blk=blk: e.tensor_mul(
            out=F.logtok[:, blk * 64:(blk + 1) * 64],
            in0=F.logtok[:, blk * 64:(blk + 1) * 64],
            in1=F.SI[:, 64:128]), deps=[tae])
        P.flush(DVE)
        P.op(DVE, lambda e, blk=blk: e.reduce_sum(
            out=G[:, blk * 16 + 9:blk * 16 + 10],
            in_=F.logtok[:, blk * 64:(blk + 1) * 64], axis=AX.X))
        P.flush(DVE)
        P.op(DVE, lambda e, blk=blk: e.reciprocal(
            out=G[:, blk * 16 + 10:blk * 16 + 11],
            in_=G[:, blk * 16 + 9:blk * 16 + 10]))
        P.flush(DVE)
        P.op(DVE, lambda e, blk=blk: e.tensor_scalar_mul(
            out=F.atok[:, blk * 128:blk * 128 + 64],
            in0=F.logtok[:, blk * 64:(blk + 1) * 64],
            scalar1=G[:, blk * 16 + 10:blk * 16 + 11]), deps=[t_az])
        msk_toks.append(P.flush(DVE))
    t_gates = msk_toks[-1]
    at_toks = []
    for blk in range(16):
        bank = blk % 2 + 5
        deps = [t_gates] + (at_toks[-2:-1] if len(at_toks) >= 2 else [])
        P.op(PE, lambda e, blk=blk, bank=bank: e.transpose(
            B.psb(bank, 1, BF16)[:, 0:128], F.atok[:, blk * 128:(blk + 1) * 128],
            B.ident[:]), deps=deps)
        mt = P.tok(PE)
        P.op(DVE, lambda e, blk=blk, bank=bank: e.tensor_copy(
            out=F.aT[:, blk * 128:(blk + 1) * 128],
            in_=B.psb(bank, 1, BF16)[:, 0:128]), deps=[mt])
        at_toks.append(P.tok(DVE))
    t_aT = at_toks[-1]
    si_toks = []
    for half in range(2):
        for tb in range(NT):
            P.op(PE, lambda e, half=half, tb=tb: e.matmul(
                B.psb(half), F.atok[:, tb * 128:(tb + 1) * 128],
                xnv[:, tb, half * 512:half * 512 + 512],
                start=(tb == 0), stop=(tb == NT - 1)),
                deps=([t_gates, tok_xn] + si_toks) if tb == 0 else [])
    for tb in range(NT):
        P.op(PE, lambda e, tb=tb: e.matmul(
            B.psb(2)[:, 0:1], F.atok[:, tb * 128:(tb + 1) * 128], B.ones[:, 0:1],
            start=(tb == 0), stop=(tb == NT - 1)))
    mt = P.tok(PE)
    P.op(DVE, lambda e: e.tensor_copy(out=F.SI[:, 0:512], in_=B.psb(0)), deps=[mt])
    P.op(DVE, lambda e: e.tensor_copy(out=F.SI[:, 512:1024], in_=B.psb(1)))
    P.op(DVE, lambda e: e.tensor_scalar_add(out=C[:, 88:89], in0=B.psb(2)[:, 0:1],
                                            scalar1=1e-8))
    t_si = P.flush(DVE)
    P.op(DVE, lambda e: e.reciprocal(out=C[:, 89:90], in_=C[:, 88:89]), deps=[t_si])
    t_ws = P.flush(DVE)
    P.op(DVE, lambda e: e.tensor_scalar_mul(out=F.gi[:, 0:1024], in0=F.SI[:, 0:1024],
                                            scalar1=C[:, 89:90]), deps=[t_ws])
    t_gi = P.flush(DVE)
    gt_toks = []
    for dc in range(ND):
        bank = dc % 2 + 3
        deps = [t_gi] + (gt_toks[-2:-1] if len(gt_toks) >= 2 else [])
        P.op(PE, lambda e, dc=dc, bank=bank: e.transpose(
            B.psb(bank, 1, BF16)[:, 0:64], F.gi[0:64, dc * 128:(dc + 1) * 128],
            B.ident[0:64, 0:64]), deps=deps)
        mt = P.tok(PE)
        P.op(DVE, lambda e, dc=dc, bank=bank: e.tensor_copy(
            out=F.giT[:, dc * 64:(dc + 1) * 64],
            in_=B.psb(bank, 1, BF16)[:, 0:64]), deps=[mt])
        gt_toks.append(P.tok(DVE))
    t_giT = gt_toks[-1]
    rb_ = F.SI[:, 0:512]
    zb_ = F.SI[:, 512:1024]
    sgA = F.sg[:, 0:64]
    sgB = F.sg[:, 64:128]
    wtk2 = {}
    prev2 = [t_giT, t_gh]
    for m in range(24):
        par = m % 2
        wdep = [wtk2[m - 2]] if m >= 2 else []
        wd = P.dma(SYNC, wstr[:, par, :, :], wih_src[:, m, :, :],
                   deps=wdep + list(pro_tok) + [t_gh])
        for k in range(ND):
            P.op(PE, lambda e, par=par, k=k, m=m: e.matmul(
                B.psb(6 + m % 2)[:, 0:64], wstr[:, par, k, :],
                F.giT[:, k * 64:(k + 1) * 64], start=(k == 0), stop=(k == ND - 1)),
                deps=([wd] + prev2) if k == 0 else [])
        mt = P.tok(PE)
        wtk2[m] = mt
        P.op(DVE, lambda e, m=m: e.tensor_add(
            out=sgA, in0=B.psb(6 + m % 2)[:, 0:64],
            in1=F.ghT[:, m * 64:(m + 1) * 64]), deps=[mt])
        tt_ = P.flush(DVE)
        if m < 8:
            P.op(ACT, lambda e, m=m: e.activation(
                out=rb_[:, m * 64:(m + 1) * 64], in_=sgA, func=AF.Sigmoid,
                bias=C[:, 16 + m:17 + m]), deps=[tt_])
            prev2 = [P.tok(ACT)]
        elif m < 16:
            P.op(ACT, lambda e, m=m: e.activation(
                out=zb_[:, (m - 8) * 64:(m - 7) * 64], in_=sgA, func=AF.Sigmoid,
                bias=C[:, 16 + m:17 + m]), deps=[tt_])
            prev2 = [P.tok(ACT)]
        else:
            dc = m - 16
            P.op(DVE, lambda e, m=m: e.tensor_scalar_add(
                out=sgA, in0=sgA, scalar1=C[:, 16 + m:17 + m]), deps=[tt_])
            P.op(DVE, lambda e, dc=dc: e.tensor_scalar_add(
                out=sgB, in0=rb_[:, dc * 64:(dc + 1) * 64], scalar1=-1.0))
            P.flush(DVE)
            P.op(DVE, lambda e, dc=dc, m=m: e.tensor_mul(
                out=sgB, in0=sgB, in1=F.ghT[:, m * 64:(m + 1) * 64]))
            P.flush(DVE)
            P.op(DVE, lambda e: e.tensor_add(out=sgA, in0=sgA, in1=sgB))
            t3_ = P.flush(DVE)
            P.op(ACT, lambda e, dc=dc: e.activation(
                out=F.nb[:, dc * 64:(dc + 1) * 64], in_=sgA, func=AF.Tanh),
                deps=[t3_])
            prev2 = [P.tok(ACT)]
    t_gru = prev2[0]
    hn_toks = []
    for dc in range(ND):
        P.op(DVE, lambda e, dc=dc: e.tensor_sub(
            out=F.sg[:, 128:192], in0=F.ST[:, dc * 64:(dc + 1) * 64],
            in1=F.nb[:, dc * 64:(dc + 1) * 64]), deps=[t_gru])
        P.flush(DVE)
        P.op(DVE, lambda e, dc=dc: e.tensor_mul(
            out=F.sg[:, 128:192], in0=F.sg[:, 128:192],
            in1=zb_[:, dc * 64:(dc + 1) * 64]))
        P.flush(DVE)
        P.op(DVE, lambda e, dc=dc: e.tensor_add(
            out=F.SnT[:, dc * 64:(dc + 1) * 64], in0=F.nb[:, dc * 64:(dc + 1) * 64],
            in1=F.sg[:, 128:192]))
        hn_toks.append(P.flush(DVE))
    t_snt = hn_toks[-1]
    P.op(DVE, lambda e: e.memset(F.svT[:], 0.0), deps=deps0)
    t_svz = P.flush(DVE)
    wtk3 = {}
    prev3 = [t_snt, t_svz]
    for m in range(ND):
        par = m % 2
        wdep = [wtk3[m - 2]] if m >= 2 else []
        wd = P.dma(SYNC, wstr[:, par, :, :], wv_src[:, m, :, :],
                   deps=wdep + list(pro_tok) + [t_gru])
        for k in range(ND):
            P.op(PE, lambda e, par=par, k=k, m=m: e.matmul(
                B.psb(6 + m % 2)[:, 0:64], wstr[:, par, k, :],
                F.SnT[:, k * 64:(k + 1) * 64], start=(k == 0), stop=(k == ND - 1)),
                deps=([wd] + prev3) if k == 0 else [])
        mt = P.tok(PE)
        wtk3[m] = mt
        P.op(DVE, lambda e, m=m: e.tensor_scalar_add(
            out=F.svT[:, m * 128:m * 128 + 64], in0=B.psb(6 + m % 2)[:, 0:64],
            scalar1=0.0), deps=[mt])
        prev3 = [P.tok(DVE)]
    t_svT = prev3[0]
    sv_toks = []
    for dc in range(ND):
        bank = dc % 2 + 3
        deps = [t_svT] + (sv_toks[-2:-1] if len(sv_toks) >= 2 else [])
        P.op(PE, lambda e, dc=dc, bank=bank: e.transpose(
            B.psb(bank, 1, BF16)[:, 0:128], F.svT[:, dc * 128:(dc + 1) * 128],
            B.ident[:]), deps=deps)
        mt = P.tok(PE)
        P.op(DVE, lambda e, dc=dc, bank=bank: e.tensor_copy(
            out=F.svs[:, dc * 128:(dc + 1) * 128],
            in_=B.psb(bank, 1, BF16)[:, 0:128]), deps=[mt])
        sv_toks.append(P.tok(DVE))
    t_svs = sv_toks[-1]
    outTv = F.outT.rearrange("p (dc t) -> p dc t", t=T)
    o_toks = []
    for dc in range(ND):
        for tt in range(4):
            bank = (dc * 4 + tt) % 2
            P.op(PE, lambda e, dc=dc, tt=tt, bank=bank: e.matmul(
                B.psb(bank), F.svs[0:64, dc * 128:(dc + 1) * 128],
                F.aT[0:64, tt * 512:(tt + 1) * 512], start=True, stop=True),
                deps=([t_svs, t_aT] + o_toks[-2:-1]))
            mt = P.tok(PE)
            P.op(DVE, lambda e, dc=dc, tt=tt, bank=bank: e.tensor_copy(
                out=outTv[:, dc, tt * 512:(tt + 1) * 512], in_=B.psb(bank)),
                deps=[mt])
            o_toks.append(P.tok(DVE))
    t_outp = o_toks[-1]
    P.op(DVE, lambda e: e.tensor_copy(out=F.bvb[:], in_=C[:, 64:72]), deps=deps0)
    t_bvb = P.flush(DVE)
    yTv = B.xn[:].rearrange("p (dc t) -> p dc t", t=T)
    wtk4 = {}
    y_toks = []
    for dc in range(ND):
        par = dc % 2
        wdep = [wtk4[dc - 2]] if dc >= 2 else [t_outp]
        wd = P.dma(SYNC, wstr[:, par, :, :], wo_src[:, dc, :, :],
                   deps=wdep + list(pro_tok) + [t_gru])
        for k in range(ND):
            P.op(PE, lambda e, par=par, k=k: e.matmul(
                B.psb(2)[:, 0:1], wstr[:, par, k, :], F.bvb[:, k:k + 1],
                start=(k == 0), stop=(k == ND - 1)),
                deps=([wd, t_bvb] + y_toks[-2:-1]))
        mtb = P.tok(PE)
        P.op(DVE, lambda e, dc=dc: e.tensor_scalar_add(
            out=C[:, 90:91], in0=B.psb(2)[:, 0:1], scalar1=C[:, 72 + dc:73 + dc]),
            deps=[mtb])
        t_bvy = P.flush(DVE)
        for tt in range(4):
            bank = (dc * 4 + tt) % 2
            for k in range(ND):
                P.op(PE, lambda e, par=par, dc=dc, tt=tt, k=k, bank=bank: e.matmul(
                    B.psb(bank), wstr[:, par, k, :],
                    outTv[:, k, tt * 512:(tt + 1) * 512],
                    start=(k == 0), stop=(k == ND - 1)),
                    deps=([t_outp] + y_toks[-2:-1]) if k == 0 else [])
            mt = P.tok(PE)
            if tt == 3:
                wtk4[dc] = mt
            P.op(DVE, lambda e, dc=dc, tt=tt, bank=bank: e.tensor_scalar_add(
                out=yTv[:, dc, tt * 512:(tt + 1) * 512], in0=B.psb(bank),
                scalar1=C[:, 90:91]), deps=[mt, t_bvy])
            y_toks.append(P.tok(DVE))
    F.dbg_toks = []
    if os.environ.get("F1_DBG") and b == 0:
        pairs = [("d1_log", F.logtok[:]), ("d1_atok", F.atok[:]),
                 ("d1_aT", F.aT[:]), ("d1_gi", F.gi[:]), ("d1_SI", F.SI[:]),
                 ("d1_ST", F.ST[:]), ("d1_SnT", F.SnT[:]), ("d1_svT", F.svT[:]),
                 ("d1_ghT", F.ghT[:]), ("d1_hT", F.hT[:]),
                 ("d1_outT", B.xnT[:]), ("d1_c16", F.cols[:, 0:96])]
        F.dbg_toks = [P.dma(SYNC, H[nm], ap, deps=[y_toks[-1]])
                      for nm, ap in pairs]
    return y_toks[-1]


def build_l1ffn_launch(nc):
    x_h = nc.dram_tensor("x", [B_PER_CORE, T, D], F32, kind="ExternalInput").ap()
    y_h = nc.dram_tensor("y", [B_PER_CORE, T, D], F32, kind="ExternalOutput").ap()
    H = decl_l1ffn_hbm(nc)
    es = contextlib.ExitStack()
    with es:
        B = Bufs(nc, es)
        F = F1Bufs(nc, es, B)
        P = Prog(nc, es)
        ct = emit_const_setup(P, B)
        pro = emit_f1_prologue(P, B, F, H, [ct])
        prev = list(pro)
        final_stores = []
        for b in range(B_PER_CORE):
            xtoks = emit_load_x(P, B, x_h, b, deps=prev)
            txn, txnT = emit_ln(P, B, ct, xtoks, war_deps=prev if b > 0 else ())
            t_yT = emit_l1_ffn(P, B, F, H, b, txn, txnT, pro)
            xtoks2 = emit_load_x(P, B, x_h, b,
                                 deps=[t_yT] + list(getattr(F, "dbg_toks", [])))
            yv = y_h[b].rearrange("(tc p) d -> p tc d", p=128)
            store_y = [yv[:, tc, :] for tc in range(NT)]
            tok_x, st_toks, _ = emit_residual_add(P, B, B.xn[:], t_yT, xtoks2,
                                                  store_y=store_y)
            final_stores += st_toks
            prev = [tok_x] + st_toks
        P.op(SYNC, lambda e: e.nop(), deps=final_stores)
        with nc.Block() as block:
            P.emit(block)
    return nc


def host_prep_l1attn(inp):
    f = _as32
    lay = lambda W, M: np.ascontiguousarray(
        W.reshape(8, 128, M, 128).transpose(1, 2, 0, 3).reshape(128, M * 8 * 128))
    d = {}
    d["a1_wq"] = lay(f(inp["L1_attn_Wq"]), 8)
    d["a1_wk"] = lay(f(inp["L1_attn_Wk"]), 8)
    d["a1_wv"] = lay(f(inp["L1_attn_Wv"]), 8)
    d["a1_wf"] = lay(f(inp["L1_attn_Wf"]), 2)
    d["a1_wo"] = lay(f(inp["L1_attn_Wo"]), 8)
    gn = f(inp["L1_attn_g_norm_w"]).ravel()
    if gn.size == 256:
        gn = np.tile(gn, 4)
    d["a1_gn"] = np.ascontiguousarray(gn.reshape(8, 128).T)
    return d


def decl_l1attn_hbm(nc):
    h = {}
    for nm, shp in [("a1_wq", [128, 8192]), ("a1_wk", [128, 8192]),
                    ("a1_wv", [128, 8192]), ("a1_wf", [128, 2048]),
                    ("a1_wo", [128, 8192]), ("a1_gn", [128, 8])]:
        h[nm] = nc.dram_tensor(nm, shp, F32, kind="ExternalInput").ap()
    for nm, shp in [("a1_wq_bf", [128, 8192]), ("a1_wk_bf", [128, 8192]),
                    ("a1_wv_bf", [128, 8192]), ("a1_wf_bf", [128, 2048]),
                    ("a1_wo_bf", [128, 8192])]:
        h[nm] = nc.dram_tensor(nm, shp, BF16, kind="Internal").ap()
    if os.environ.get("GSA_DBG"):
        for nm, shp, dt in [("d_qT", [128, 8192], BF16), ("d_kT", [128, 8192], BF16),
                            ("d_vT", [128, 8192], BF16), ("d_f", [128, 2048], F32),
                            ("d_G", [128, 2048], F32), ("d_oT", [128, 16384], BF16),
                            ("d_Hk", [128, 512], BF16), ("d_Hv", [128, 512], BF16),
                            ("d_Bt", [128, 128], BF16), ("d_B2t", [128, 128], BF16),
                            ("d_Ett", [128, 128], BF16), ("d_ptp", [128, 128], BF16),
                            ("d_Am", [128, 128], BF16), ("d_Wm", [128, 128], BF16),
                            ("d_sm", [128, 64], F32), ("d_ebc", [128, 64], BF16),
                            ("d_cs", [128, 512], F32)]:
            h[nm] = nc.dram_tensor(nm, shp, dt, kind="ExternalOutput").ap()
    return h


class A1Bufs:
    def __init__(self, nc, es, B):
        A = lambda name, shape, dt: es.enter_context(nc.sbuf_tensor(name, shape, dt))
        self.B = B
        self.stg = B.xn[:].bitcast(F32)[:, 0:4096]
        self.vTh = B.x[:, 0:4096].bitcast(BF16)
        self.G = B.x[:, 4096:6144]
        self.fv = B.x[:, 6144:8192]
        self.cs = B.x[:, 8192:10240]
        self.oT = B.xn[:]
        self.yT = B.xnT[:]
        self.qTh = A("a1_qTh", [128, 8 * 1024], BF16)
        self.kTh = A("a1_kTh", [128, 8 * 1024], BF16)
        self.wsb = A("a1_wsb", [128, 2 * 1024], BF16)
        self.Hk = A("a1_Hk", [128, 4 * 128], BF16)
        self.Hv = A("a1_Hv", [128, 2 * 256], BF16)
        self.cols = A("a1_cols", [128, 32], F32)
        self.mask128 = A("a1_mask", [128, 128], BF16)
        self.oh127 = A("a1_oh127", [128, 128], BF16)
        self.Bt = A("a1_Bt", [128, 128], BF16)
        self.B2t = A("a1_B2t", [128, 128], BF16)
        self.Ett = A("a1_Ett", [128, 128], BF16)
        self.ptp = A("a1_ptp", [128, 128], BF16)
        self.ptT = A("a1_ptT", [128, 128], BF16)
        self.ktok = A("a1_ktok", [128, 2 * 128], BF16)
        self.vtok = A("a1_vtok", [128, 2 * 128], BF16)
        self.Am = A("a1_Am", [128, 128], BF16)
        self.Wm = A("a1_Wm", [128, 128], BF16)
        self.BT2 = A("a1_BT2", [128, 256], BF16)
        self.EtT = A("a1_EtT", [128, 128], BF16)
        self.sm = A("a1_sm", [128, 64], F32)
        self.erow = A("a1_erow", [128, 64], BF16)
        self.ebc = A("a1_ebc", [128, 64], BF16)
        self.hvinc = A("a1_hvinc", [128, 256], BF16)
        self.sch = A("a1_sch", [128, 768], F32)


def emit_a1_prologue(P, A1, H, deps):
    stg = A1.stg
    prev = list(deps)
    for nm, dstnm, npieces in [("a1_wq", "a1_wq_bf", 4), ("a1_wk", "a1_wk_bf", 4),
                               ("a1_wv", "a1_wv_bf", 4), ("a1_wf", "a1_wf_bf", 1),
                               ("a1_wo", "a1_wo_bf", 4)]:
        wb = A1.stg[:, 2048:3072].bitcast(BF16)
        for pc in range(npieces):
            d = P.dma(SYNC, stg[:, 0:2048], H[nm][:, pc * 2048:(pc + 1) * 2048],
                      deps=prev)
            P.op(DVE, lambda e, wb=wb: e.tensor_copy(out=wb, in_=A1.stg[:, 0:2048]),
                 deps=[d])
            c1 = P.tok(DVE)
            prev = [P.dma(SYNC, H[dstnm][:, pc * 2048:(pc + 1) * 2048], wb,
                          deps=[c1]), c1]
    dg = P.dma(SYNC, A1.cols[:, 0:8], H["a1_gn"], deps=deps)
    return prev + [dg]


def emit_a1_consts(P, B, A1, const_tok):
    P.op(POOL, lambda e: e.iota(A1.mask128[:], pattern=[[1, 128]], base=0,
                                channel_multiplier=-1,
                                allow_small_or_imprecise_dtypes=True),
         deps=[const_tok])
    t0 = P.tok(POOL)
    P.op(POOL, lambda e: e.tensor_single_scalar(out=A1.mask128[:],
                                                in_=A1.mask128[:],
                                                scalar=-0.5, op=ALU.is_gt),
         deps=[t0])
    P.op(POOL, lambda e: e.memset(A1.oh127[:], 0.0))
    P.op(POOL, lambda e: e.iota(A1.cols[:, 12:13], pattern=[[0, 1]], base=0,
                                channel_multiplier=1,
                                allow_small_or_imprecise_dtypes=True))
    t1 = P.tok(POOL)
    P.op(POOL, lambda e: e.tensor_single_scalar(out=A1.cols[:, 13:14],
                                                in_=A1.cols[:, 12:13],
                                                scalar=126.5, op=ALU.is_gt),
         deps=[t1])
    t2 = P.tok(POOL)
    P.op(DVE, lambda e: e.tensor_copy(out=A1.oh127[:, 0:1], in_=A1.cols[:, 13:14]),
         deps=[t2])
    P.op(DVE, lambda e: e.memset(A1.cols[:, 10:11], 0.0))
    P.op(DVE, lambda e: e.memset(A1.cols[:, 11:12], 1.0))
    P.op(DVE, lambda e: e.memset(A1.cols[:, 20:21], 1e-5))
    P.op(DVE, lambda e: e.memset(A1.ptp[:], 0.0))
    return P.flush(DVE)


def emit_l1_attn(P, B, A1, H, b, tok_xnT, pro_tok, cst_tok):
    xnTv = B.xnT[:].rearrange("p (dc t) -> p dc t", t=T)
    qThv = A1.qTh[:].rearrange("p (dc t) -> p dc t", t=1024)
    kThv = A1.kTh[:].rearrange("p (dc t) -> p dc t", t=1024)
    vThv = A1.vTh.rearrange("p (dc t) -> p dc t", t=1024)
    fvv = A1.fv.rearrange("p (m t) -> p m t", t=1024)
    Gv = A1.G.rearrange("p (m t) -> p m t", t=1024)
    oTv = A1.oT.rearrange("p (dc t) -> p dc t", t=T)
    yTv = A1.yT.rearrange("p (dc t) -> p dc t", t=T)
    deps0 = [tok_xnT, cst_tok] + list(pro_tok)
    STAGE = int(os.environ.get("A1_STAGE", "9"))
    scale = float(256 ** -0.5)
    wstr = A1.wsb[:].rearrange("p (pb k c) -> p pb k c", pb=2, c=128)
    zc = A1.cols[:, 10:11]
    P.op(DVE, lambda e: e.memset(A1.Hk[:], 0.0), deps=deps0)
    P.op(DVE, lambda e: e.memset(A1.Hv[:], 0.0))
    t_st = P.flush(DVE)
    prev_c = [t_st]
    if STAGE < -4:
        return t_st
    for half in range(2):
        tb = half * 1024
        wtk = {}
        prev_ev = [prev_c[-1]] + deps0
        srcs = [("a1_wq_bf", 8, "q"), ("a1_wk_bf", 8, "k"),
                ("a1_wv_bf", 8, "v"), ("a1_wf_bf", 2, "z")]
        kinds = os.environ.get("A1_KINDS", "qkvz")
        srcs = [s for s in srcs if s[2] in kinds]
        si = 0
        for nm, nm_m, kind in srcs:
            hsrc = H[nm].rearrange("p (m k c) -> p m k c", k=8, c=128)
            for m in range(nm_m):
                par = si % 2
                wdep = [wtk[si - 2]] if si >= 2 else [prev_c[-1]]
                wd = P.dma(SYNC, wstr[:, par, :, :], hsrc[:, m, :, :],
                           deps=wdep + list(pro_tok))
                for tt in range(2):
                    bank = tt % 2
                    for k in range(ND):
                        P.op(PE, lambda e, par=par, tt=tt, k=k, bank=bank, tb=tb:
                             e.matmul(
                                 B.psb(bank), wstr[:, par, k, :],
                                 xnTv[:, k, tb + tt * 512:tb + (tt + 1) * 512],
                                 start=(k == 0), stop=(k == ND - 1)),
                             deps=([wd] + prev_ev) if k == 0 else [])
                    mt = P.tok(PE)
                    if tt == 1:
                        wtk[si] = mt
                    if kind in ("q", "k"):
                        dst = qThv if kind == "q" else kThv
                        P.op(ACT, lambda e, m=m, tt=tt, bank=bank, dst=dst:
                             e.activation(out=dst[:, m, tt * 512:(tt + 1) * 512],
                                          in_=B.psb(bank), func=AF.Silu), deps=[mt])
                        prev_ev = [P.tok(ACT)]
                    elif kind == "v":
                        P.op(DVE, lambda e, m=m, tt=tt, bank=bank: e.tensor_copy(
                            out=vThv[:, m, tt * 512:(tt + 1) * 512], in_=B.psb(bank)),
                            deps=[mt])
                        prev_ev = [P.tok(DVE)]
                    else:
                        sc1 = A1.cs[:, 0:512]
                        sc2 = A1.cs[:, 512:1024]
                        P.op(ACT, lambda e, bank=bank, sc1=sc1: e.activation(
                            out=sc1, in_=B.psb(bank), func=AF.Abs), deps=[mt])
                        t1 = P.tok(ACT)
                        P.op(DVE, lambda e, bank=bank, sc2=sc2: e.tensor_scalar_min(
                            out=sc2, in0=B.psb(bank), scalar1=0.0), deps=[mt, t1])
                        P.flush(DVE)
                        P.op(ACT, lambda e, sc1=sc1: e.activation(
                            out=sc1, in_=sc1, func=AF.Exp, scale=-1.0), deps=[t1])
                        t2 = P.tok(ACT)
                        P.op(ACT, lambda e, sc1=sc1: e.activation(
                            out=sc1, in_=sc1, func=AF.Ln, bias=A1.cols[:, 11:12]),
                            deps=[t2])
                        t3 = P.tok(ACT)
                        P.op(DVE, lambda e, sc1=sc1, sc2=sc2: e.tensor_sub(
                            out=sc2, in0=sc2, in1=sc1), deps=[t3])
                        P.flush(DVE)
                        P.op(DVE, lambda e, m=m, tt=tt, sc2=sc2: e.tensor_scalar_mul(
                            out=fvv[:, m, tt * 512:(tt + 1) * 512], in0=sc2,
                            scalar1=0.125))
                        prev_ev = [P.flush(DVE)]
                si += 1
        t_proj = prev_ev[0]
        if STAGE < -3 and STAGE >= -4 and os.environ.get("A1_NOSCAN"):
            prev_c = [t_proj]
            continue
        for dc in range(ND):
            P.op(DVE, lambda e, dc=dc: e.tensor_scalar_mul(
                out=qThv[:, dc, :], in0=qThv[:, dc, :], scalar1=scale),
                deps=[t_proj])
        P.flush(DVE)
        for m in range(2):
            P.op(DVE, lambda e, m=m: e.tensor_tensor_scan(
                out=Gv[:, m, 0:512], data0=fvv[:, m, 0:512], data1=fvv[:, m, 0:512],
                initial=0.0, op0=ALU.add, op1=ALU.bypass))
            P.op(DVE, lambda e, m=m: e.tensor_tensor_scan(
                out=Gv[:, m, 512:1024], data0=fvv[:, m, 512:1024],
                data1=fvv[:, m, 512:1024], initial=0.0, op0=ALU.add,
                op1=ALU.bypass))
            P.flush(DVE)
            P.op(DVE, lambda e, m=m: e.tensor_scalar_add(
                out=Gv[:, m, 512:1024], in0=Gv[:, m, 512:1024],
                scalar1=Gv[:, m, 511:512]))
        t_G = P.flush(DVE)
        prev_c = [t_G]
        if STAGE < -3:
            continue
        if os.environ.get("GSA_DBG") and b == 0 and half == 0:
            dt0 = P.dma(SYNC, H["d_qT"], A1.qTh[:], deps=[t_G])
            dt1 = P.dma(SYNC, H["d_kT"], A1.kTh[:], deps=[t_G])
            dt2 = P.dma(SYNC, H["d_vT"], A1.vTh, deps=[t_G])
            dt3 = P.dma(SYNC, H["d_f"], A1.fv, deps=[t_G])
            dt4 = P.dma(SYNC, H["d_G"], A1.G, deps=[t_G])
            prev_c = [t_G, dt0, dt1, dt2, dt3, dt4]
        for cl in range(8):
            for pair in range(2):
                gsl = Gv[:, pair, cl * 128:(cl + 1) * 128]
                gp = Gv[:, pair, cl * 128 - 1:cl * 128] if cl > 0 else zc
                P.op(DVE, lambda e, gsl=gsl, gp=gp: e.tensor_scalar(
                    out=A1.cs[:, 1024:1152], in0=gsl, scalar1=gp, scalar2=None,
                    op0=ALU.subtract), deps=prev_c)
                t0 = P.flush(DVE)
                P.op(ACT, lambda e: e.activation(out=A1.cs[:, 1152:1280],
                                                 in_=A1.cs[:, 1024:1152],
                                                 func=AF.Exp, scale=-1.0),
                     deps=[t0])
                P.op(ACT, lambda e: e.activation(out=A1.EtT[:],
                                                 in_=A1.cs[:, 1024:1152],
                                                 func=AF.Exp))
                P.op(ACT, lambda e: e.activation(out=A1.cols[:, 21:22],
                                                 in_=A1.cs[:, 1151:1152],
                                                 func=AF.Exp))
                P.op(ACT, lambda e, pair=pair, cl=cl: e.activation(
                    out=A1.cs[:, 1280:1408],
                    in_=fvv[:, pair, cl * 128:(cl + 1) * 128], func=AF.Exp))
                t1 = P.tok(ACT)
                P.op(DVE, lambda e: e.tensor_scalar(
                    out=A1.cs[:, 1280:1408], in0=A1.cs[:, 1280:1408],
                    scalar1=-1.0, scalar2=1.0, op0=ALU.mult, op1=ALU.add),
                    deps=[t1])
                P.flush(DVE)
                P.op(DVE, lambda e: e.tensor_mul(out=A1.BT2[:, 0:128],
                                                 in0=A1.cs[:, 1152:1280],
                                                 in1=A1.cs[:, 1280:1408]))
                t2 = P.flush(DVE)
                P.op(DVE, lambda e: e.tensor_scalar_mul(
                    out=A1.BT2[:, 128:256], in0=A1.BT2[:, 0:128],
                    scalar1=A1.cols[:, 21:22]), deps=[t2])
                t_bt = P.flush(DVE)
                if STAGE < -2:
                    prev_c = [t_bt]
                    continue
                ttoks = []
                for ti, (srcap, dstt) in enumerate(
                        [(A1.BT2[:, 0:128], A1.Bt), (A1.BT2[:, 128:256], A1.B2t),
                         (A1.EtT[:], A1.Ett)]):
                    bank = 2 + ti % 2
                    P.op(PE, lambda e, srcap=srcap, bank=bank: e.transpose(
                        B.psb(bank, 1, BF16)[:, 0:128], srcap, B.ident[:]),
                        deps=[t_bt] + ttoks[max(0, len(ttoks) - 2):
                                            max(0, len(ttoks) - 1)])
                    mt = P.tok(PE)
                    P.op(DVE, lambda e, dstt=dstt, bank=bank: e.tensor_copy(
                        out=dstt[:], in_=B.psb(bank, 1, BF16)[:, 0:128]),
                        deps=[mt])
                    ttoks.append(P.tok(DVE))
                P.op(DVE, lambda e: e.memset(A1.hvinc[:], 0.0))
                ttoks.append(P.flush(DVE))
                if STAGE < 1:
                    prev_c = [ttoks[-1]]
                    continue
                for hh in range(2):
                    h = 2 * pair + hh
                    mb = hh * 64
                    for i in range(2):
                        dc = 2 * h + i
                        bank = 2 + i
                        P.op(PE, lambda e, dc=dc, cl=cl, bank=bank: e.transpose(
                            B.psb(bank, 1, BF16)[:, 0:128],
                            kThv[:, dc, cl * 128:(cl + 1) * 128], B.ident[:]),
                            deps=ttoks[-2:])
                        mt = P.tok(PE)
                        P.op(DVE, lambda e, i=i, bank=bank: e.tensor_copy(
                            out=A1.ktok[:, i * 128:(i + 1) * 128],
                            in_=B.psb(bank, 1, BF16)[:, 0:128]), deps=[mt])
                        ttoks.append(P.tok(DVE))
                    for i in range(2):
                        dc = 2 * h + i
                        bank = 2 + i
                        P.op(PE, lambda e, dc=dc, cl=cl, bank=bank: e.transpose(
                            B.psb(bank, 1, BF16)[:, 0:128],
                            vThv[:, dc, cl * 128:(cl + 1) * 128], B.ident[:]),
                            deps=ttoks[-2:])
                        mt = P.tok(PE)
                        P.op(DVE, lambda e, i=i, bank=bank: e.tensor_copy(
                            out=A1.vtok[:, i * 128:(i + 1) * 128],
                            in_=B.psb(bank, 1, BF16)[:, 0:128]), deps=[mt])
                        ttoks.append(P.tok(DVE))
                    t_tr = ttoks[-1]
                    for i in range(2):
                        P.op(PE, lambda e, h=h, i=i, cl=cl: e.matmul(
                            B.psb(4)[:, 0:128],
                            kThv[:, 2 * h + i, cl * 128:(cl + 1) * 128],
                            qThv[:, 2 * h + i, cl * 128:(cl + 1) * 128],
                            start=(i == 0), stop=(i == 1)), deps=[t_tr])
                    mt = P.tok(PE)
                    P.op(DVE, lambda e: e.tensor_mul(out=A1.Am[:],
                                                     in0=B.psb(4)[:, 0:128],
                                                     in1=A1.mask128[:]), deps=[mt])
                    t_am = P.flush(DVE)
                    if STAGE < 2:
                        prev_c = [t_am]
                        continue
                    P.op(PE, lambda e, mb=mb: e.matmul(
                        B.psb(5)[:, 0:64], A1.Am[:], A1.Bt[:, mb:mb + 64],
                        start=True, stop=False), deps=[t_am])
                    for i in range(2):
                        P.op(PE, lambda e, h=h, i=i, cl=cl: e.matmul(
                            B.psb(5)[:, 0:64],
                            qThv[:, 2 * h + i, cl * 128:(cl + 1) * 128],
                            A1.Hk[:, h * 128 + i * 64:h * 128 + (i + 1) * 64],
                            start=False, stop=(i == 1)))
                    mt = P.tok(PE)
                    P.op(DVE, lambda e, mb=mb: e.tensor_mul(
                        out=A1.sm[:], in0=B.psb(5)[:, 0:64],
                        in1=A1.Ett[:, mb:mb + 64]), deps=[mt])
                    P.flush(DVE)
                    P.op(DVE, lambda e: e.reduce_max(out=A1.cols[:, 16:17],
                                                     in_=A1.sm[:], axis=AX.X))
                    P.flush(DVE)
                    P.op(DVE, lambda e: e.tensor_scalar_mul(
                        out=A1.cols[:, 17:18], in0=A1.cols[:, 16:17],
                        scalar1=-1.0))
                    t4 = P.flush(DVE)
                    P.op(ACT, lambda e: e.activation(out=A1.sm[:], in_=A1.sm[:],
                                                     func=AF.Exp,
                                                     bias=A1.cols[:, 17:18]),
                         deps=[t4])
                    t5 = P.tok(ACT)
                    P.op(DVE, lambda e: e.reduce_sum(out=A1.cols[:, 18:19],
                                                     in_=A1.sm[:], axis=AX.X),
                         deps=[t5])
                    P.flush(DVE)
                    P.op(DVE, lambda e: e.reciprocal(out=A1.cols[:, 19:20],
                                                     in_=A1.cols[:, 18:19]))
                    t6 = P.flush(DVE)
                    P.op(DVE, lambda e: e.tensor_scalar_mul(
                        out=A1.sm[:], in0=A1.sm[:], scalar1=A1.cols[:, 19:20]),
                        deps=[t6])
                    P.flush(DVE)
                    P.op(DVE, lambda e, mb=mb: e.tensor_mul(
                        out=A1.ptp[:, mb:mb + 64], in0=A1.sm[:],
                        in1=A1.Ett[:, mb:mb + 64]))
                    t_pt = P.flush(DVE)
                    P.op(PE, lambda e: e.transpose(B.psb(6, 1, BF16)[:, 0:128],
                                                   A1.ptp[:], B.ident[:]),
                         deps=[t_pt])
                    mt = P.tok(PE)
                    P.op(DVE, lambda e: e.tensor_copy(
                        out=A1.ptT[:], in_=B.psb(6, 1, BF16)[:, 0:128]), deps=[mt])
                    t_ptT = P.flush(DVE)
                    if STAGE < 3:
                        prev_c = [t_ptT]
                        continue
                    P.op(PE, lambda e, mb=mb: e.matmul(
                        B.psb(7)[:, 0:128], A1.BT2[mb:mb + 64, 0:128],
                        A1.ptT[mb:mb + 64, 0:128], start=True, stop=True),
                        deps=[t_ptT])
                    mt = P.tok(PE)
                    P.op(DVE, lambda e: e.tensor_mul(out=A1.Wm[:],
                                                     in0=B.psb(7)[:, 0:128],
                                                     in1=A1.mask128[:]), deps=[mt])
                    t_wm = P.flush(DVE)
                    for i in range(2):
                        bank = 4 + i
                        P.op(PE, lambda e, i=i, bank=bank: e.matmul(
                            B.psb(bank)[:, 0:128],
                            A1.vtok[:, i * 128:(i + 1) * 128], A1.Wm[:],
                            start=True, stop=False), deps=[t_wm])
                        P.op(PE, lambda e, i=i, mb=mb, pair=pair, bank=bank:
                             e.matmul(
                                 B.psb(bank)[:, 0:128],
                                 A1.Hv[mb:mb + 64,
                                       pair * 256 + i * 128:
                                       pair * 256 + (i + 1) * 128],
                                 A1.ptT[mb:mb + 64, 0:128],
                                 start=False, stop=True))
                    mt = P.tok(PE)
                    for i in range(2):
                        P.op(DVE, lambda e, h=h, i=i, cl=cl, bank=4 + i, tb=tb:
                             e.tensor_copy(
                                 out=oTv[:, 2 * h + i,
                                         tb + cl * 128:tb + (cl + 1) * 128],
                                 in_=B.psb(bank)[:, 0:128]),
                             deps=[mt] if i == 0 else [])
                    t_o = P.flush(DVE)
                    if STAGE < 4:
                        prev_c = [t_o]
                        continue
                    P.op(PE, lambda e, mb=mb: e.matmul(
                        B.psb(6)[:, 0:64], A1.oh127[:], A1.Ett[:, mb:mb + 64],
                        start=True, stop=True), deps=[t_o])
                    mt0 = P.tok(PE)
                    P.op(DVE, lambda e: e.tensor_scalar_add(
                        out=A1.erow[0:1, :], in0=B.psb(6)[0:1, 0:64],
                        scalar1=0.0), deps=[mt0])
                    t_er = P.flush(DVE)
                    P.op(PE, lambda e: e.matmul(
                        B.psb(6)[:, 0:64], B.ones[0:1, :], A1.erow[0:1, :],
                        start=True, stop=True), deps=[t_er])
                    mt1 = P.tok(PE)
                    P.op(DVE, lambda e: e.tensor_copy(out=A1.ebc[:],
                                                      in_=B.psb(6)[:, 0:64]),
                         deps=[mt1])
                    t_ebc = P.flush(DVE)
                    for i in range(2):
                        bank = 4 + i
                        P.op(PE, lambda e, i=i, mb=mb, bank=bank: e.matmul(
                            B.psb(bank)[:, 0:64],
                            A1.ktok[:, i * 128:(i + 1) * 128],
                            A1.B2t[:, mb:mb + 64], start=True, stop=True),
                            deps=[t_o, t_ebc])
                    mt2 = P.tok(PE)
                    for i in range(2):
                        hkap = A1.Hk[:, h * 128 + i * 64:h * 128 + (i + 1) * 64]
                        P.op(DVE, lambda e, hkap=hkap: e.tensor_mul(
                            out=hkap, in0=hkap, in1=A1.ebc[:]),
                            deps=[mt2] if i == 0 else [])
                    P.flush(DVE)
                    for i in range(2):
                        hkap = A1.Hk[:, h * 128 + i * 64:h * 128 + (i + 1) * 64]
                        P.op(DVE, lambda e, hkap=hkap, bank=4 + i: e.tensor_add(
                            out=hkap, in0=hkap, in1=B.psb(bank)[:, 0:64]))
                    t_hk = P.flush(DVE)
                    if STAGE < 5:
                        prev_c = [t_hk]
                        continue
                    for i in range(2):
                        bank = 6 + i
                        P.op(PE, lambda e, i=i, bank=bank: e.matmul(
                            B.psb(bank)[:, 0:128], A1.B2t[:],
                            A1.vtok[:, i * 128:(i + 1) * 128],
                            start=True, stop=True), deps=[t_hk, t_wm, t_ebc])
                    mt3 = P.tok(PE)
                    for i in range(2):
                        P.op(DVE, lambda e, i=i, mb=mb, bank=6 + i:
                             e.tensor_scalar_add(
                                 out=A1.hvinc[mb:mb + 64, i * 128:(i + 1) * 128],
                                 in0=B.psb(bank)[mb:mb + 64, 0:128], scalar1=0.0),
                             deps=[mt3] if i == 0 else [])
                    prev_c = [P.flush(DVE)]
                hv = A1.Hv[:, pair * 256:(pair + 1) * 256]
                P.op(DVE, lambda e, hv=hv: e.tensor_scalar_mul(
                    out=hv, in0=hv, scalar1=A1.cols[:, 21:22]), deps=prev_c)
                P.flush(DVE)
                P.op(DVE, lambda e, hv=hv: e.tensor_add(out=hv, in0=hv,
                                                        in1=A1.hvinc[:]))
                prev_c = [P.flush(DVE)]
                if (os.environ.get("GSA_DBG") and b == 0 and half == 0
                        and cl == 0 and pair == 1):
                    dts = [P.dma(SYNC, H["d_" + nm], getattr(A1, nm)[:]
                                 if nm != "cs" else A1.cs[:, 1024:1536],
                                 deps=prev_c)
                           for nm in ["Hk", "Hv", "Bt", "B2t", "Ett", "ptp",
                                      "Am", "Wm", "sm", "ebc", "cs"]]
                    prev_c = prev_c + dts
    t_chunks = prev_c[0]
    if STAGE < 0:
        return t_chunks
    if os.environ.get("GSA_DBG") and b == 0:
        t_chunks = P.dma(SYNC, H["d_oT"], A1.oT, deps=prev_c)
    last_pe = [t_chunks]
    rms_fl = [t_chunks]
    sq = A1.sch[:, 512:768].bitcast(BF16)
    for h in range(4):
        for tt in range(4):
            for i in range(2):
                P.op(ACT, lambda e, h=h, i=i, tt=tt, sq=sq: e.activation(
                    out=sq, in_=oTv[:, 2 * h + i, tt * 512:(tt + 1) * 512],
                    func=AF.Square), deps=last_pe + rms_fl[-1:])
                ta = P.tok(ACT)
                P.op(PE, lambda e, i=i, sq=sq: e.matmul(
                    B.psb(4), B.ones[:], sq, start=(i == 0), stop=(i == 1)),
                    deps=[ta])
                last_pe = [P.tok(PE)]
            mt = last_pe[0]
            P.op(ACT, lambda e: e.activation(out=A1.sch[:, 0:512], in_=B.psb(4),
                                             func=AF.Ln, scale=1.0 / 256.0,
                                             bias=A1.cols[:, 20:21]),
                 deps=[mt] + rms_fl[-1:])
            ta2 = P.tok(ACT)
            P.op(ACT, lambda e: e.activation(out=A1.sch[:, 0:512],
                                             in_=A1.sch[:, 0:512], func=AF.Exp,
                                             scale=-0.5), deps=[ta2])
            ta3 = P.tok(ACT)
            for i in range(2):
                P.op(DVE, lambda e, h=h, i=i, tt=tt: e.scalar_tensor_tensor(
                    out=oTv[:, 2 * h + i, tt * 512:(tt + 1) * 512],
                    in0=oTv[:, 2 * h + i, tt * 512:(tt + 1) * 512],
                    scalar=A1.cols[:, 2 * h + i:2 * h + i + 1],
                    in1=A1.sch[:, 0:512],
                    op0=ALU.mult, op1=ALU.mult), deps=[ta3] if i == 0 else [])
            rms_fl.append(P.flush(DVE))
    t_rms = rms_fl[-1]
    wo_src = H["a1_wo_bf"].rearrange("p (m k c) -> p m k c", k=8, c=128)
    wtk2 = {}
    y_toks = []
    for dc in range(ND):
        par = dc % 2
        wdep = [wtk2[dc - 2]] if dc >= 2 else []
        wd = P.dma(SYNC, wstr[:, par, :, :], wo_src[:, dc, :, :],
                   deps=wdep + list(pro_tok) + [t_chunks])
        for tt in range(4):
            bank = (dc * 4 + tt) % 2
            for k in range(ND):
                P.op(PE, lambda e, par=par, tt=tt, k=k, bank=bank: e.matmul(
                    B.psb(bank), wstr[:, par, k, :],
                    oTv[:, k, tt * 512:(tt + 1) * 512],
                    start=(k == 0), stop=(k == ND - 1)),
                    deps=([wd, t_rms] + y_toks[-2:-1]) if k == 0 else [])
            mt = P.tok(PE)
            if tt == 3:
                wtk2[dc] = mt
            P.op(DVE, lambda e, dc=dc, tt=tt, bank=bank: e.tensor_copy(
                out=yTv[:, dc, tt * 512:(tt + 1) * 512], in_=B.psb(bank)),
                deps=[mt])
            y_toks.append(P.tok(DVE))
    return y_toks[-1]



def build_l1attn_launch(nc):
    x_h = nc.dram_tensor("x", [B_PER_CORE, T, D], F32, kind="ExternalInput").ap()
    y_h = nc.dram_tensor("y", [B_PER_CORE, T, D], F32, kind="ExternalOutput").ap()
    H = decl_l1attn_hbm(nc)
    es = contextlib.ExitStack()
    with es:
        B = Bufs(nc, es)
        A1 = A1Bufs(nc, es, B)
        P = Prog(nc, es)
        ct = emit_const_setup(P, B)
        cst = emit_a1_consts(P, B, A1, ct)
        pro = emit_a1_prologue(P, A1, H, [ct])
        prev = list(pro) + [cst]
        final_stores = []
        for b in range(B_PER_CORE):
            xtoks = emit_load_x(P, B, x_h, b, deps=prev)
            txn, txnT = emit_ln(P, B, ct, xtoks, war_deps=prev if b > 0 else ())
            t_yT = emit_l1_attn(P, B, A1, H, b, txnT, pro, cst)
            xtoks2 = emit_load_x(P, B, x_h, b, deps=[t_yT])
            yv = y_h[b].rearrange("(tc p) d -> p tc d", p=128)
            store_y = [yv[:, tc, :] for tc in range(NT)]
            tok_x, st_toks, _ = emit_residual_add(P, B, B.xnT[:], t_yT, xtoks2,
                                                  store_y=store_y)
            final_stores += st_toks
            prev = [tok_x] + st_toks
        P.op(SYNC, lambda e: e.nop(), deps=final_stores)
        with nc.Block() as block:
            P.emit(block)
    return nc


def _as32(a):
    return np.ascontiguousarray(np.asarray(a, dtype=np.float32))




def build_full_launch(nc):
    x_h = nc.dram_tensor("x", [B_PER_CORE, T, D], F32, kind="ExternalInput").ap()
    y_h = nc.dram_tensor("y", [B_PER_CORE, T, D], F32, kind="ExternalOutput").ap()
    xs1 = nc.dram_tensor("xs1", [B_PER_CORE, T, D], F32, kind="Internal").ap()
    xs2 = nc.dram_tensor("xs2", [B_PER_CORE, T, D], F32, kind="Internal").ap()
    xs3 = nc.dram_tensor("xs3", [B_PER_CORE, T, D], F32, kind="Internal").ap()
    H1 = decl_l0attn_hbm(nc)
    H2 = decl_l0ffn_hbm(nc)
    H3 = decl_l1attn_hbm(nc)
    H4 = decl_l1ffn_hbm(nc)
    es = contextlib.ExitStack()
    with es:
        B = Bufs(nc, es)
        P = Prog(nc, es)
        ct = emit_const_setup(P, B)
        prev = [ct]

        def barrier(prev):
            for eng in (ACT, DVE, POOL, SYNC):
                P.op(eng, lambda e: e.nop(), deps=prev)

        with contextlib.ExitStack() as es2:
            A_ = AttnBufs(nc, es2, B)
            barrier(prev)
            tbl = emit_l0a_tables(P, A_, ct)
            wt = emit_l0a_weights(P, A_, H1, tbl)
            prev = list(wt)
            for b in range(B_PER_CORE):
                xtoks = emit_load_x(P, B, x_h, b, deps=prev)
                txn, txnT = emit_ln(P, B, ct, xtoks, war_deps=prev)
                t_outT = emit_l0_attn(P, B, A_, H1, b, txnT, tbl, wt[0], txn,
                                      stage="full", dbg=None)
                xtoks2 = emit_load_x(P, B, x_h, b, deps=[t_outT])
                yv = xs1[b].rearrange("(tc p) d -> p tc d", p=128)
                store_y = [yv[:, tc, :] for tc in range(NT)]
                tok_x, st_toks, _ = emit_residual_add(P, B, A_.outT, t_outT,
                                                      xtoks2, store_y=store_y)
                prev = [tok_x] + st_toks

        with contextlib.ExitStack() as es2:
            M = MoeBufs(nc, es2, B)
            barrier(prev)
            cs = emit_sel8_setup(P, M)
            shared_tok, store_toks, conv_fin = emit_moe_prologue(
                P, M, H2, deps=[cs] + prev)
            scratch_dep = [store_toks[-1], store_toks[-2]]
            prev = [conv_fin]
            for bi, b in enumerate([1, 0]):
                xtoks = prev if bi == 0 else emit_load_x(P, B, xs1, b, deps=prev)
                txn, txnT = emit_ln(P, B, ct, xtoks, war_deps=prev)
                t_outT, _ = emit_moe(P, B, M, H2, b, txnT,
                                     scratch_dep=scratch_dep,
                                     ge_dep=[shared_tok], stage="full",
                                     dbg_ltok=None)
                xtoks2 = emit_load_x(P, B, xs1, b, deps=[t_outT])
                yv = xs2[b].rearrange("(tc p) d -> p tc d", p=128)
                store_y = [yv[:, tc, :] for tc in range(NT)]
                tok_x, st_toks, _ = emit_residual_add(P, B, M.outT, t_outT,
                                                      xtoks2, store_y=store_y)
                prev = [tok_x] + st_toks

        with contextlib.ExitStack() as es2:
            A1 = A1Bufs(nc, es2, B)
            barrier(prev)
            cst = emit_a1_consts(P, B, A1, ct)
            pro = emit_a1_prologue(P, A1, H3, [ct] + prev)
            prev = list(pro) + [cst]
            for bi, b in enumerate([0, 1]):
                xtoks = prev if bi == 0 else emit_load_x(P, B, xs2, b, deps=prev)
                txn, txnT = emit_ln(P, B, ct, xtoks, war_deps=prev)
                t_yT = emit_l1_attn(P, B, A1, H3, b, txnT, pro, cst)
                xtoks2 = emit_load_x(P, B, xs2, b, deps=[t_yT])
                yv = xs3[b].rearrange("(tc p) d -> p tc d", p=128)
                store_y = [yv[:, tc, :] for tc in range(NT)]
                tok_x, st_toks, _ = emit_residual_add(P, B, B.xnT[:], t_yT,
                                                      xtoks2, store_y=store_y)
                prev = [tok_x] + st_toks

        with contextlib.ExitStack() as es2:
            F = F1Bufs(nc, es2, B)
            barrier(prev)
            pro = emit_f1_prologue(P, B, F, H4, [ct] + prev)
            prev = list(pro)
            final_stores = []
            for bi, b in enumerate([1, 0]):
                xtoks = prev if bi == 0 else emit_load_x(P, B, xs3, b, deps=prev)
                txn, txnT = emit_ln(P, B, ct, xtoks, war_deps=prev)
                t_yT = emit_l1_ffn(P, B, F, H4, b, txn, txnT, pro)
                yv = y_h[b].rearrange("(tc p) d -> p tc d", p=128)
                store_y = [yv[:, tc, :] for tc in range(NT)]
                tok_x, st_toks, _ = emit_residual_add(P, B, B.xn[:], t_yT,
                                                      xtoks, store_y=store_y)
                final_stores += st_toks
                prev = [tok_x] + st_toks
            P.op(SYNC, lambda e: e.nop(), deps=final_stores)
        with nc.Block() as block:
            P.emit(block)
    return nc


def kernel(**inputs):
    x = _as32(inputs["L0_attn_x"])
    NB = x.shape[0]
    prep = {}
    for p in (host_prep_l0attn(inputs), host_prep_l0ffn(inputs),
              host_prep_l1attn(inputs), host_prep_l1ffn(inputs)):
        prep.update(p)
    n_cores = 8
    shards = [np.ascontiguousarray(x[c * B_PER_CORE:(c + 1) * B_PER_CORE])
              for c in range(n_cores)]
    nc = bass.Bass("TRN2", target_bir_lowering=False, debug=False)
    build_full_launch(nc)
    in_maps = [dict(prep, x=shards[c]) for c in range(n_cores)]
    res = run_bass_kernel_spmd(nc, in_maps, core_ids=list(range(n_cores)))
    shards = [np.asarray(res.results[c]["y"], dtype=np.float32)
              for c in range(n_cores)]
    return np.concatenate(shards, axis=0).reshape(NB, T, D)


# version 10
# speedup vs baseline: 1.1802x; 1.1802x over previous
import contextlib
import os
import numpy as np
import ml_dtypes
import concourse.bass as bass
import concourse.mybir as mybir
from concourse.bass_utils import run_bass_kernel_spmd

F32 = mybir.dt.float32
BF16 = mybir.dt.bfloat16
AF = mybir.ActivationFunctionType
ALU = mybir.AluOpType
AX = mybir.AxisListType

PE, ACT, DVE, POOL, SYNC = "tensor", "scalar", "vector", "gpsimd", "sync"
ENGINES = [PE, ACT, DVE, POOL, SYNC]

B_PER_CORE = 2
T = 2048
D = 1024
NT = T // 128
ND = D // 128


class Prog:

    def __init__(self, nc, es):
        self.nc = nc
        self.sems = {}
        for e in ENGINES:
            self.sems["c:" + e] = es.enter_context(nc.semaphore(f"sc_{e}"))
            self.sems["d:" + e] = es.enter_context(nc.semaphore(f"sd_{e}"))
        self.cnt = {e: 0 for e in ENGINES}
        self.dmacnt = {e: 0 for e in ENGINES}
        self.items = {e: [] for e in ENGINES}
        self.lastwait = {e: {} for e in ENGINES}

    def _waits(self, e, deps):
        for (skey, val) in deps:
            if skey == "c:" + e:
                continue
            lw = self.lastwait[e]
            if lw.get(skey, -1) >= val:
                pass
            else:
                lw[skey] = val
            sem = self.sems[skey]
            self.items[e].append(lambda eng, st, sem=sem, val=val: eng.wait_ge(sem, val))

    def op(self, e, fn, deps=()):
        self._waits(e, deps)

        def run(eng, st, fn=fn):
            st["last"] = fn(eng)
        self.items[e].append(run)

    def tok(self, e):
        self.cnt[e] += 1
        val = self.cnt[e]
        sem = self.sems["c:" + e]

        def m(eng, st, sem=sem, e=e):
            if e == PE:
                li = st.get("last")
                if li is not None and li is not st.get("last_inc"):
                    li.then_inc(sem, 1)
                    st["last_inc"] = li
                else:
                    eng.nop().then_inc(sem, 1)
            else:
                eng.drain()
                eng.nop().then_inc(sem, 1)
        self.items[e].append(m)
        return ("c:" + e, val)

    def flush(self, e):
        tokn = self.tok(e)
        sem = self.sems["c:" + e]
        val = tokn[1]

        def m(eng, st, sem=sem, val=val):
            eng.wait_ge(sem, val)
        self.items[e].append(m)
        return tokn

    def dma(self, q, out_ap, in_ap, deps=()):
        self._waits(q, deps)
        self.dmacnt[q] += 16
        val = self.dmacnt[q]
        sem = self.sems["d:" + q]
        self.items[q].append(
            lambda eng, st, sem=sem, o=out_ap, i=in_ap: eng.dma_start(out=o, in_=i).then_inc(sem, 16))
        return ("d:" + q, val)

    def emit(self, block):
        for e in ENGINES:
            dec = getattr(block, e)
            items = self.items[e]

            def body(eng, items=items):
                st = {"last": None, "last_inc": None}
                for fn in items:
                    fn(eng, st)
            dec(body)




class Bufs:
    def __init__(self, nc, es, need=()):
        A = lambda name, shape, dt: es.enter_context(nc.sbuf_tensor(name, shape, dt))
        self.x = A("x_tok", [128, NT * D], F32)
        self.xn = A("xnorm_tok", [128, NT * D], BF16)
        self.xnT = A("xnT", [128, ND * T], BF16)
        self.ident = A("ident", [128, 128], BF16)
        self.identf = A("identf", [128, 128], F32)
        self.stats = A("stats", [128, 256], F32)
        self.ones = A("ones_col", [128, 128], BF16)
        self.gtmp = A("gtmp", [128, 128], BF16)
        self.PSb = [es.enter_context(nc.psum_tensor(f"PS{i}", [128, 512], F32))
                    for i in range(8)]

    def psb(self, bank, nbanks=1, dtype=F32):
        assert nbanks == 1
        v = self.PSb[bank][:]
        if dtype == BF16:
            v = v.bitcast(BF16)
        return v


def emit_const_setup(P, B):
    P.op(POOL, lambda e: e.iota(B.ident[:], pattern=[[1, 128]], base=0,
                                channel_multiplier=-1,
                                allow_small_or_imprecise_dtypes=True))
    P.flush(POOL)
    P.op(POOL, lambda e: e.tensor_scalar(out=B.ident[:], in0=B.ident[:],
                                         scalar1=0.0, scalar2=None, op0=ALU.is_equal))
    P.op(POOL, lambda e: e.iota(B.identf[:], pattern=[[1, 128]], base=0,
                                channel_multiplier=-1,
                                allow_small_or_imprecise_dtypes=True))
    P.flush(POOL)
    P.op(POOL, lambda e: e.tensor_scalar(out=B.identf[:], in0=B.identf[:],
                                         scalar1=0.0, scalar2=None, op0=ALU.is_equal))
    P.op(POOL, lambda e: e.memset(B.ones[:], 1.0))
    return P.tok(POOL)


def emit_sel8_setup(P, M):
    P.op(POOL, lambda e: e.iota(M.sel8[:], pattern=[[1, 1024]], base=0,
                                channel_multiplier=-128,
                                allow_small_or_imprecise_dtypes=True))
    P.flush(POOL)
    P.op(POOL, lambda e: e.tensor_scalar(out=M.sel8b[:], in0=M.sel8[:],
                                         scalar1=-0.5, scalar2=None, op0=ALU.is_ge))
    P.op(POOL, lambda e: e.tensor_scalar(out=M.sel8[:], in0=M.sel8[:],
                                         scalar1=127.5, scalar2=None, op0=ALU.is_le))
    P.flush(POOL)
    P.op(POOL, lambda e: e.tensor_mul(out=M.sel8[:], in0=M.sel8[:],
                                      in1=M.sel8b[:]))
    return P.tok(POOL)


def emit_load_x(P, B, x_hbm, b, deps=()):
    toks = []
    src = x_hbm[b].rearrange("(tc p) d -> p tc d", p=128)
    dst = B.x[:].rearrange("p (tc d) -> p tc d", d=D)
    for g in range(4):
        toks.append(P.dma(SYNC, dst[:, g * 4:(g + 1) * 4, :],
                          src[:, g * 4:(g + 1) * 4, :], deps=deps))
    return toks


def emit_ln(P, B, const_tok, xdeps, war_deps=(), transpose=True, skip_stats=False):
    war_deps = tuple(war_deps) + (const_tok,)
    xv = B.x[:].rearrange("p (tc d) -> p tc d", d=D)
    xnv = B.xn[:].rearrange("p (tc d) -> p tc d", d=D)
    S = B.stats
    if skip_stats:
        P.op(DVE, lambda e: e.tensor_copy(out=B.xn[:], in_=B.x[:]),
             deps=list(xdeps) + list(war_deps))
        tok_xn = P.flush(DVE)
        return _emit_ln_transposes(P, B, const_tok, tok_xn, xnv)
    P.op(DVE, lambda e: e.memset(S[:, 255:256], 1e-5), deps=war_deps)
    for tc in range(NT):
        deps = list(xdeps) if tc == 0 else []
        P.op(DVE, lambda e, tc=tc: e.bn_stats(out=S[:, tc * 12:tc * 12 + 6],
                                              in_=xv[:, tc, 0:512]), deps=deps)
        P.op(DVE, lambda e, tc=tc: e.bn_stats(out=S[:, tc * 12 + 6:tc * 12 + 12],
                                              in_=xv[:, tc, 512:1024]))
    P.flush(DVE)
    for tc in range(NT):
        P.op(DVE, lambda e, tc=tc: e.bn_aggr(out=S[:, 192 + 2 * tc:194 + 2 * tc],
                                             in_=S[:, tc * 12:tc * 12 + 12]))
    t_aggr = P.flush(DVE)
    P.op(ACT, lambda e: e.activation(out=S[:, 224:240], in_=S[:, 193:225].rearrange(
        "p (tc two) -> p tc two", two=2)[:, :, 0], func=AF.Sqrt, bias=S[:, 255:256]),
        deps=[t_aggr])
    t_sqrt = P.tok(ACT)
    P.op(DVE, lambda e: e.reciprocal(out=S[:, 224:240], in_=S[:, 224:240]), deps=[t_sqrt])
    P.op(DVE, lambda e: e.tensor_scalar_mul(out=S[:, 240:256], in0=S[:, 192:224].rearrange(
        "p (tc two) -> p tc two", two=2)[:, :, 0], scalar1=-1.0))
    P.flush(DVE)
    for tc in range(NT):
        P.op(DVE, lambda e, tc=tc: e.tensor_scalar(
            out=xnv[:, tc, :], in0=xv[:, tc, :], scalar1=S[:, 240 + tc:241 + tc],
            scalar2=S[:, 224 + tc:225 + tc], op0=ALU.add, op1=ALU.mult))
    tok_xn = P.flush(DVE)
    if not transpose:
        return tok_xn, None
    return _emit_ln_transposes(P, B, const_tok, tok_xn, xnv)


def _emit_ln_transposes(P, B, const_tok, tok_xn, xnv):
    xnTv = B.xnT[:].rearrange("p (dc t) -> p dc t", t=T)
    evac_toks = []
    for tc in range(NT):
        deps = [tok_xn, const_tok]
        if tc >= 1:
            deps.append(evac_toks[tc - 1])
        for dc in range(ND):
            P.op(PE, lambda e, tc=tc, dc=dc: e.transpose(
                B.psb(dc, 1, BF16)[:, 0:128], xnv[:, tc, dc * 128:(dc + 1) * 128],
                B.ident[:]), deps=deps if dc == 0 else [])
        mm_tok = P.tok(PE)
        for dc in range(ND):
            P.op(DVE, lambda e, tc=tc, dc=dc: e.tensor_copy(
                out=xnTv[:, dc, tc * 128:(tc + 1) * 128],
                in_=B.psb(dc, 1, BF16)[:, 0:128]), deps=[mm_tok] if dc == 0 else [])
        evac_toks.append(P.tok(DVE))
    return tok_xn, evac_toks[-1]




def host_prep_l0ffn(inp):
    f = _as32
    Wg = f(inp["L0_ffn_Wg"])
    W1s = f(inp["L0_ffn_W1s"])
    W2s = f(inp["L0_ffn_W2s"])
    W1r = f(inp["L0_ffn_W1r"])
    W2r = f(inp["L0_ffn_W2r"])
    d = {}
    wgp = np.zeros((8, 128, 128), np.float32)
    wgp[:, :, 0:8] = Wg.reshape(8, 128, 8)
    d["moe_wg"] = np.ascontiguousarray(wgp.transpose(1, 0, 2).reshape(128, 1024))
    gb = np.zeros((1, 128), np.float32)
    gb[0, 0:8] = f(inp["L0_ffn_gate_bias"])
    d["moe_gbias"] = gb
    lay1 = lambda W: np.ascontiguousarray(W.reshape(8, 128, 4, 128).transpose(1, 0, 2, 3).reshape(128, 4096))
    lay2 = lambda W: np.ascontiguousarray(W.reshape(4, 128, 8, 128).transpose(1, 0, 2, 3).reshape(128, 4096))
    d["moe_w1s"] = lay1(W1s)
    d["moe_w2s"] = lay2(W2s)
    d["moe_w1r"] = np.stack([lay1(W1r[e]) for e in range(8)])
    d["moe_w2r"] = np.stack([lay2(W2r[e]) for e in range(8)])
    d["moe_b2s"] = np.ascontiguousarray(f(inp["L0_ffn_b2s"]).reshape(8, 128).T)
    d["moe_b2r"] = f(inp["L0_ffn_b2r"])
    return d


def decl_l0ffn_hbm(nc):
    h = {}
    h["moe_wg"] = nc.dram_tensor("moe_wg", [128, 1024], F32, kind="ExternalInput").ap()
    h["moe_gbias"] = nc.dram_tensor("moe_gbias", [1, 128], F32, kind="ExternalInput").ap()
    h["moe_w1s"] = nc.dram_tensor("moe_w1s", [128, 4096], F32, kind="ExternalInput").ap()
    h["moe_w2s"] = nc.dram_tensor("moe_w2s", [128, 4096], F32, kind="ExternalInput").ap()
    h["moe_w1r"] = nc.dram_tensor("moe_w1r", [8, 128, 4096], F32, kind="ExternalInput").ap()
    h["moe_w2r"] = nc.dram_tensor("moe_w2r", [8, 128, 4096], F32, kind="ExternalInput").ap()
    h["moe_b2s"] = nc.dram_tensor("moe_b2s", [128, 8], F32, kind="ExternalInput").ap()
    h["moe_b2r"] = nc.dram_tensor("moe_b2r", [8, 1024], F32, kind="ExternalInput").ap()
    h["moe_w1r_bf"] = nc.dram_tensor("moe_w1r_bf", [8, 128, 4096], BF16, kind="Internal").ap()
    h["moe_w2r_bf"] = nc.dram_tensor("moe_w2r_bf", [8, 128, 4096], BF16, kind="Internal").ap()
    return h


class MoeBufs:
    def __init__(self, nc, es, B):
        A = lambda name, shape, dt: es.enter_context(nc.sbuf_tensor(name, shape, dt))
        self.B = B
        self.stg = [B.xn[:].bitcast(F32)[:, 0:4096],
                    B.xn[:].bitcast(F32)[:, 4096:8192]]
        self.heg = B.x[:, 0:9216].bitcast(BF16)
        self.outT = B.xn[:]
        self.w1s = A("moe_w1s_sb", [128, 4096], BF16)
        self.w2s = A("moe_w2s_sb", [128, 4096], BF16)
        self.w1b = A("moe_w1b", [128, 2 * 4096], BF16)
        self.w2b = A("moe_w2b", [128, 2 * 4096], BF16)
        self.wg = A("moe_wg_sb", [128, 1024], BF16)
        self.gbias = A("moe_gbias_sb", [1, 128], BF16)
        self.ones512 = A("moe_ones512", [1, 512], BF16)
        self.b2s = A("moe_b2s_sb", [128, 8], F32)
        self.b2r = A("moe_b2r_sb", [8, 1024], BF16)
        self.logT = A("moe_logT", [128, 2048], F32)
        self.logtok = A("moe_logtok", [128, 16 * 8], F32)
        self.gsc = A("moe_gsc", [128, 16 * 32], F32)
        self.gatesT = A("moe_gatesT", [128, 2048], BF16)
        self.ge = A("moe_ge", [128, 512], F32)
        self.gtmp2 = A("gtmp2", [128, 128], BF16)
        self.sel8 = A("sel8", [8, 1024], BF16)
        self.sel8b = A("sel8b", [8, 1024], BF16)


def emit_moe_prologue(P, M, H, deps):
    t_small = [P.dma(SYNC, M.b2s[:], H["moe_b2s"], deps=deps)]
    P.op(POOL, lambda e: e.memset(M.ones512[:], 1.0))
    P.op(POOL, lambda e: e.memset(M.ones512[0:1, 0:1], 1.0))
    stg = M.stg
    d0 = P.dma(SYNC, stg[0][0:8, 0:1024], H["moe_b2r"], deps=deps)
    dgb = P.dma(SYNC, stg[0][0:1, 2048:2176], H["moe_gbias"], deps=deps)
    P.op(DVE, lambda e: e.tensor_copy(out=M.gbias[:], in_=stg[0][0:1, 2048:2176]),
         deps=[dgb])
    d1 = P.dma(SYNC, stg[0][:, 1024:2048], H["moe_wg"], deps=deps)
    d2 = P.dma(SYNC, stg[0][:, 2048:4096] if False else stg[1][:, 0:4096], H["moe_w1s"], deps=deps)
    P.op(DVE, lambda e: e.tensor_copy(out=M.b2r[:], in_=stg[0][0:8, 0:1024]), deps=[d0])
    P.op(DVE, lambda e: e.tensor_copy(out=M.wg[:], in_=stg[0][:, 1024:2048]), deps=[d1])
    P.op(DVE, lambda e: e.tensor_copy(out=M.w1s[:], in_=stg[1][:, 0:4096]), deps=[d2])
    c1 = P.tok(DVE)
    d3 = P.dma(SYNC, stg[1][:, 0:4096], H["moe_w2s"], deps=[c1])
    P.op(DVE, lambda e: e.tensor_copy(out=M.w2s[:], in_=stg[1][:, 0:4096]), deps=[d3])
    shared_tok = P.tok(DVE)
    conv_toks = [shared_tok, shared_tok]
    store_toks = []
    for i, (src_h, dst_h) in enumerate([(H["moe_w1r"], H["moe_w1r_bf"]),
                                        (H["moe_w2r"], H["moe_w2r_bf"])]):
        for e in range(8):
            par = (i * 8 + e) % 2
            dep_war = [conv_toks[-2]] if len(conv_toks) >= 2 else []
            dl = P.dma(SYNC, stg[par][:, 0:4096], src_h[e], deps=dep_war)
            half = M.w1b[:, par * 4096:(par + 1) * 4096]
            P.op(DVE, lambda e_, s=stg[par], h=half: e_.tensor_copy(out=h, in_=s[:, 0:4096]),
                 deps=[dl] + store_toks[-2:-1])
            ct = P.tok(DVE)
            conv_toks.append(ct)
            store_toks.append(P.dma(SYNC, dst_h[e], half, deps=[ct]))
    return shared_tok, store_toks, conv_toks[-1]


def emit_moe(P, B, M, H, b, tok_xnT, scratch_dep=(), ge_dep=(), stage="full",
             dbg_ltok=None):
    xnTv = B.xnT[:].rearrange("p (dc t) -> p dc t", t=T)
    hegv = M.heg.rearrange("p (em t) -> p em t", t=512)
    outTv = M.outT.rearrange("p (dc t) -> p dc t", t=T)
    plog = B.psb(0)
    rt = []
    ntt_r = 1 if stage == "r1a" else 4
    for tt in range(ntt_r):
        for k in range(ND):
            P.op(PE, lambda e, k=k, tt=tt: e.matmul(
                plog, M.wg[:, k * 128:(k + 1) * 128],
                xnTv[:, k, tt * 512:(tt + 1) * 512],
                start=(k == 0), stop=False),
                deps=[tok_xnT] + (rt[-1:] if rt else []))
        P.op(PE, lambda e, tt=tt: e.matmul(
            plog, M.gbias[:], M.ones512[:], start=False, stop=True))
        mt = P.tok(PE)
        if stage == "r0":
            import os as _os9
            for _ in range(int(_os9.environ.get("PADDVE", "0"))):
                P.op(DVE, lambda e: e.memset(M.ge[:, 0:1], 0.0))
            return mt, mt
        P.op(DVE, lambda e, tt=tt: e.tensor_copy(
            out=M.logT[:, tt * 512:(tt + 1) * 512], in_=plog), deps=[mt])
        if os.environ.get("RTPE", "0") == "1":
            P.tok(DVE)
            rt.append(mt)
        else:
            rt.append(P.tok(DVE))
    if stage in ("r1", "r1a"):
        return rt[-1], rt[-1]
    lt_toks = []
    for blk in range(16):
        bank = blk % 8
        deps = [rt[-1]]
        if blk >= 8:
            deps.append(lt_toks[blk - 8])
        P.op(PE, lambda e, blk=blk, bank=bank: e.transpose(
            B.psb(bank)[:, 0:8], M.logT[0:8, blk * 128:(blk + 1) * 128],
            B.identf[0:8, 0:8]), deps=deps)
        if os.environ.get("DUPT", "0") == "1":
            P.op(PE, lambda e, blk=blk, bank=bank: e.transpose(
                B.psb(bank)[:, 0:8], M.logT[0:8, blk * 128:(blk + 1) * 128],
                B.identf[0:8, 0:8]))
        mt = P.tok(PE)
        P.op(DVE, lambda e, blk=blk, bank=bank: e.tensor_copy(
            out=M.logtok[:, blk * 8:(blk + 1) * 8], in_=B.psb(bank)[:, 0:8]), deps=[mt])
        lt_toks.append(P.tok(DVE))
    if stage == "r2":
        return lt_toks[-1], lt_toks[-1]
    G = M.gsc
    for blk in range(16):
        P.op(DVE, lambda e, blk=blk: e.max(out=G[:, blk * 32:blk * 32 + 8],
                                           in_=M.logtok[:, blk * 8:(blk + 1) * 8]),
             deps=[lt_toks[blk]])
    P.flush(DVE)
    for blk in range(16):
        P.op(DVE, lambda e, blk=blk: e.tensor_scalar_mul(
            out=G[:, blk * 32 + 8:blk * 32 + 9], in0=G[:, blk * 32:blk * 32 + 1],
            scalar1=-1.0))
        P.op(DVE, lambda e, blk=blk: e.tensor_scalar(
            out=G[:, blk * 32 + 16:blk * 32 + 24],
            in0=M.logtok[:, blk * 8:(blk + 1) * 8],
            scalar1=G[:, blk * 32 + 1:blk * 32 + 2], scalar2=None, op0=ALU.is_ge))
    t_neg = P.flush(DVE)
    exdeps = [t_neg]
    if dbg_ltok is not None and b == 0:
        exdeps.append(P.dma(POOL, dbg_ltok[0], M.logtok[:], deps=[t_neg]))
        exdeps.append(P.dma(POOL, dbg_ltok[1], M.gsc[:], deps=[t_neg]))
    for blk in range(16):
        P.op(ACT, lambda e, blk=blk: e.activation(
            out=M.logtok[:, blk * 8:(blk + 1) * 8],
            in_=M.logtok[:, blk * 8:(blk + 1) * 8], func=AF.Exp,
            bias=G[:, blk * 32 + 8:blk * 32 + 9]), deps=exdeps if blk == 0 else [])
    t_exp = P.tok(ACT)
    for blk in range(16):
        P.op(DVE, lambda e, blk=blk: e.tensor_mul(
            out=M.logtok[:, blk * 8:(blk + 1) * 8],
            in0=M.logtok[:, blk * 8:(blk + 1) * 8],
            in1=G[:, blk * 32 + 16:blk * 32 + 24]), deps=[t_exp] if blk == 0 else [])
    P.flush(DVE)
    for blk in range(16):
        P.op(DVE, lambda e, blk=blk: e.reduce_sum(
            out=G[:, blk * 32 + 9:blk * 32 + 10], in_=M.logtok[:, blk * 8:(blk + 1) * 8],
            axis=AX.X))
    P.flush(DVE)
    for blk in range(16):
        P.op(DVE, lambda e, blk=blk: e.reciprocal(
            out=G[:, blk * 32 + 10:blk * 32 + 11], in_=G[:, blk * 32 + 9:blk * 32 + 10]))
    P.flush(DVE)
    for blk in range(16):
        P.op(DVE, lambda e, blk=blk: e.tensor_scalar_mul(
            out=M.logtok[:, blk * 8:(blk + 1) * 8],
            in0=M.logtok[:, blk * 8:(blk + 1) * 8],
            scalar1=G[:, blk * 32 + 10:blk * 32 + 11]))
    t_gates = P.flush(DVE)
    if stage == "r3":
        return t_gates, t_gates
    gt_toks = []
    for blk in range(16):
        bank = blk % 8
        deps = [t_gates]
        if blk >= 8:
            deps.append(gt_toks[blk - 8])
        P.op(DVE, lambda e, blk=blk: e.tensor_copy(
            out=M.gatesT[0:8, 0:1] if False else B.gtmp[:, blk * 8:(blk + 1) * 8],
            in_=M.logtok[:, blk * 8:(blk + 1) * 8]), deps=deps)
        ct = P.tok(DVE)
        P.op(PE, lambda e, blk=blk, bank=bank: e.transpose(
            B.psb(bank, 1, BF16)[0:8, 0:128], B.gtmp[:, blk * 8:(blk + 1) * 8],
            B.ident[:]), deps=[ct])
        mt = P.tok(PE)
        P.op(DVE, lambda e, blk=blk, bank=bank: e.tensor_copy(
            out=M.gatesT[:, blk * 128:(blk + 1) * 128],
            in_=B.psb(bank, 1, BF16)[:, 0:128]), deps=[mt])
        gt_toks.append(P.tok(DVE))
    t_gatesT = gt_toks[-1]
    if stage == "routing":
        return t_gatesT, t_gatesT
    w1bv = [M.w1b[:, 0:4096], M.w1b[:, 4096:8192]]
    w2bv = [M.w2b[:, 0:4096], M.w2b[:, 4096:8192]]
    prev_phase2 = [t_gatesT]
    NTT = 1 if stage == "tt0" else 4
    heg_deps = list(ge_dep)
    p1_use = {}
    p2_use = {}
    w1_toks = {}
    w2_toks = {}
    t_outT = []
    for tt in range(NTT):
        evac_toks = []
        for e9 in range(9):
            if e9 < 8:
                par = e9 % 2
                dep = [p1_use[par]] if par in p1_use else []
                w1_toks[e9] = P.dma(SYNC, w1bv[par], H["moe_w1r_bf"][e9],
                                    deps=dep + list(scratch_dep))
                w1 = w1bv[par]
                wdep = [w1_toks[e9]]
            else:
                w1 = M.w1s
                wdep = []
            for m in range(4):
                for k in range(ND):
                    d0 = (wdep + prev_phase2 + heg_deps + [tok_xnT] + evac_toks[-1:]) \
                        if (m == 0 and k == 0) else []
                    P.op(PE, lambda e, w1=w1, m=m, k=k, tt=tt: e.matmul(
                        B.psb(m), w1[:, (k * 4 + m) * 128:(k * 4 + m) * 128 + 128],
                        xnTv[:, k, tt * 512:(tt + 1) * 512],
                        start=(k == 0), stop=(k == ND - 1)),
                        deps=d0)
            mt = P.tok(PE)
            if e9 < 8:
                p1_use[e9 % 2] = mt
            if e9 < 8:
                P.op(PE, lambda e, e9=e9, tt=tt: e.matmul(
                    B.psb(5), M.sel8[:, e9 * 128:(e9 + 1) * 128],
                    M.gatesT[0:8, tt * 512:(tt + 1) * 512], start=True, stop=True),
                    deps=[t_gatesT])
                mg = P.tok(PE)
                P.op(DVE, lambda e: e.tensor_copy(out=M.ge[:], in_=B.psb(5)), deps=[mg])
                t_ge = P.tok(DVE)
            for m in range(4):
                P.op(ACT, lambda e, e9=e9, m=m: e.activation(
                    out=hegv[:, e9 * 4 + m, :], in_=B.psb(m), func=AF.Gelu),
                    deps=[mt])
            t_act = P.tok(ACT)
            if e9 < 8:
                for m in range(4):
                    P.op(DVE, lambda e, e9=e9, m=m: e.tensor_mul(
                        out=hegv[:, e9 * 4 + m, :], in0=hegv[:, e9 * 4 + m, :],
                        in1=M.ge[:]), deps=[t_act, t_ge])
                evac_toks.append(P.tok(DVE))
            else:
                evac_toks.append(t_act)
        t_p1 = evac_toks
        first_mm = {}
        for e9 in range(9):
            if e9 < 8:
                par = e9 % 2
                dep = [p2_use[par]] if par in p2_use else []
                w2_toks[e9] = P.dma(SYNC, w2bv[par], H["moe_w2r_bf"][e9],
                                    deps=dep + list(scratch_dep))
                w2 = w2bv[par]
                wdep = [w2_toks[e9]]
            else:
                w2 = M.w2s
                wdep = []
            for dc in range(ND):
                for hk in range(4):
                    P.op(PE, lambda e, w2=w2, dc=dc, hk=hk, e9=e9: e.matmul(
                        B.psb(dc), w2[:, (hk * 8 + dc) * 128:(hk * 8 + dc) * 128 + 128],
                        hegv[:, e9 * 4 + hk, :],
                        start=(e9 == 0 and hk == 0), stop=False),
                        deps=(wdep + t_p1) if (dc == 0 and hk == 0) else [])
            mt = P.tok(PE)
            if e9 < 8:
                p2_use[e9 % 2] = mt
        for dc in range(ND):
            P.op(PE, lambda e, dc=dc, tt=tt: e.matmul(
                B.psb(dc), M.b2r[0:8, dc * 128:(dc + 1) * 128],
                M.gatesT[0:8, tt * 512:(tt + 1) * 512], start=False, stop=True),
                deps=[t_gatesT] if dc == 0 else [])
        mt2 = P.tok(PE)
        for dc in range(ND):
            P.op(DVE, lambda e, dc=dc, tt=tt: e.tensor_scalar_add(
                out=outTv[:, dc, tt * 512:(tt + 1) * 512], in0=B.psb(dc),
                scalar1=M.b2s[:, dc:dc + 1]), deps=[mt2])
        prev_phase2 = [P.tok(DVE)]
        t_outT.append(prev_phase2[0])
    return t_outT[-1], prev_phase2[0]


def emit_residual_add(P, B, srcT, tok_src, xdeps, store_y=None, qy=SYNC):
    srcv = srcT.rearrange("p (dc t) -> p dc t", t=T)
    xv = B.x[:].rearrange("p (tc d) -> p tc d", d=D)
    add_toks = []
    store_toks = []
    for tc in range(NT):
        deps = [tok_src] + list(xdeps)
        if tc >= 1:
            deps.append(add_toks[tc - 1])
        for dc in range(ND):
            P.op(PE, lambda e, tc=tc, dc=dc: e.transpose(
                B.psb(dc, 1, BF16)[:, 0:128], srcv[:, dc, tc * 128:(tc + 1) * 128],
                B.ident[:]), deps=deps if dc == 0 else [])
        mt = P.tok(PE)
        for dc in range(ND):
            P.op(DVE, lambda e, tc=tc, dc=dc: e.tensor_add(
                out=xv[:, tc, dc * 128:(dc + 1) * 128],
                in0=xv[:, tc, dc * 128:(dc + 1) * 128],
                in1=B.psb(dc, 1, BF16)[:, 0:128]), deps=[mt] if dc == 0 else [])
        add_toks.append(P.tok(DVE))
        if store_y is not None:
            store_toks.append(P.dma(qy, store_y[tc], xv[:, tc, :], deps=[add_toks[-1]]))
    return add_toks[-1], store_toks, add_toks[-1]


def build_moe_launch(nc, debug=False, stage="full"):
    x_h = nc.dram_tensor("x", [B_PER_CORE, T, D], F32, kind="ExternalInput").ap()
    y_h = nc.dram_tensor("y", [B_PER_CORE, T, D], F32, kind="ExternalOutput").ap()
    H = decl_l0ffn_hbm(nc)
    if debug:
        dbg_g = nc.dram_tensor("dbg_gates", [128, 128], F32, kind="ExternalOutput").ap()
        dbg_h = nc.dram_tensor("dbg_heg", [128, 18432], BF16, kind="ExternalOutput").ap()
        dbg_o = nc.dram_tensor("dbg_outT", [128, 16384], BF16, kind="ExternalOutput").ap()
        dbg_gt = nc.dram_tensor("dbg_gatesT", [128, 2048], BF16, kind="ExternalOutput").ap()
        dbg_sel = nc.dram_tensor("dbg_sel", [8, 1024], BF16, kind="ExternalOutput").ap()
        dbg_lt = nc.dram_tensor("dbg_logT", [128, 2048], F32, kind="ExternalOutput").ap()
        dbg_ltok = [nc.dram_tensor("dbg_ltok", [128, 128], F32, kind="ExternalOutput").ap(),
                    nc.dram_tensor("dbg_gsc", [128, 512], F32, kind="ExternalOutput").ap()]
    else:
        dbg_ltok = None
    es = contextlib.ExitStack()
    with es:
        B = Bufs(nc, es)
        M = MoeBufs(nc, es, B)
        P = Prog(nc, es)
        ct = emit_const_setup(P, B)
        cs = emit_sel8_setup(P, M)
        shared_tok, store_toks, conv_fin = emit_moe_prologue(P, M, H, deps=[cs])
        scratch_dep = [store_toks[-1], store_toks[-2]]
        prev = [conv_fin]
        final_stores = []
        nb = 1 if stage != "full" else B_PER_CORE
        for b in range(nb):
            xtoks = emit_load_x(P, B, x_h, b, deps=prev)
            if stage == "prologue":
                yv = y_h[b].rearrange("(tc p) d -> p tc d", p=128)
                sts = [P.dma(SYNC, yv[:, tc, :],
                             B.x[:].rearrange("p (tc d) -> p tc d", d=D)[:, tc, :],
                             deps=xtoks) for tc in range(NT)]
                P.op(SYNC, lambda e: e.nop(), deps=sts)
                with nc.Block() as block:
                    P.emit(block)
                return nc
            import os as _os3
            lnmode = _os3.environ.get("LNMODE", "full")
            txn, txnT = emit_ln(P, B, ct, xtoks,
                                war_deps=prev if b > 0 else (),
                                transpose=(lnmode != "noT"),
                                skip_stats=(lnmode == "nostat"))
            if txnT is None:
                txnT = txn
            if stage == "ln":
                yv = y_h[b].rearrange("(tc p) d -> p tc d", p=128)
                sts = [P.dma(SYNC, yv[:, tc, :],
                             B.x[:].rearrange("p (tc d) -> p tc d", d=D)[:, tc, :],
                             deps=[txn, txnT]) for tc in range(NT)]
                P.op(SYNC, lambda e: e.nop(), deps=sts)
                with nc.Block() as block:
                    P.emit(block)
                return nc
            t_outT, t_heg = emit_moe(P, B, M, H, b, txnT,
                                     scratch_dep=scratch_dep,
                                     ge_dep=[shared_tok], stage=stage,
                                     dbg_ltok=(dbg_ltok if debug else None))
            if stage in ("r0", "r1", "r1a", "r2", "r3", "routing", "tt0"):
                yv = y_h[b].rearrange("(tc p) d -> p tc d", p=128)
                sts = [P.dma(SYNC, yv[:, tc, :],
                             B.x[:].rearrange("p (tc d) -> p tc d", d=D)[:, tc, :],
                             deps=[t_outT]) for tc in range(NT)]
                P.op(SYNC, lambda e: e.nop(), deps=sts)
                with nc.Block() as block:
                    P.emit(block)
                return nc
            dbg_deps = []
            if debug and b == 0:
                prev_dbg = [P.dma(POOL, dbg_lt, M.logT[:], deps=[t_outT]),
                            P.dma(POOL, dbg_gt, M.gatesT[:], deps=[t_outT]),
                            P.dma(POOL, dbg_sel, B.sel8[:], deps=[t_outT]),
                            P.dma(POOL, dbg_g, M.logtok[:, 0:128], deps=[t_outT]),
                            P.dma(POOL, dbg_h, M.heg, deps=[t_outT]),
                            P.dma(POOL, dbg_o, M.outT, deps=[t_outT])]
                P.op(POOL, lambda e: e.nop(), deps=prev_dbg)
                dbg_deps = prev_dbg
            xtoks2 = emit_load_x(P, B, x_h, b, deps=[t_outT] + dbg_deps)
            yv = y_h[b].rearrange("(tc p) d -> p tc d", p=128)
            store_y = [yv[:, tc, :] for tc in range(NT)]
            tok_x, st_toks, _ = emit_residual_add(P, B, M.outT, t_outT, xtoks2,
                                                  store_y=store_y)
            final_stores += st_toks
            prev = [tok_x] + st_toks
        P.op(SYNC, lambda e: e.nop(), deps=final_stores)
        with nc.Block() as block:
            P.emit(block)
    return nc


def host_prep_l0attn(inp):
    f = _as32
    Wqa = f(inp["L0_attn_Wqa"])
    Wqb = f(inp["L0_attn_Wqb"])
    Wkva = f(inp["L0_attn_Wkva"])
    Wkvb = f(inp["L0_attn_Wkvb"])
    Wout = f(inp["L0_attn_Wout"])
    bout = f(inp["L0_attn_bout"])
    H_, KVH, Dh = 16, 4, 64
    E = 1024
    perm_q = np.zeros(E, np.int64)
    for h in range(H_):
        for dd in range(0, Dh, 2):
            perm_q[h * Dh + dd // 2] = h * Dh + dd
        for dd in range(1, Dh, 2):
            perm_q[h * Dh + 32 + dd // 2] = h * Dh + dd
    swap = np.zeros(E, np.int64)
    for h in range(H_):
        swap[h * Dh:h * Dh + 32] = np.arange(h * Dh + 32, h * Dh + 64)
        swap[h * Dh + 32:h * Dh + 64] = np.arange(h * Dh, h * Dh + 32)
    perm_k = np.zeros(KVH * Dh, np.int64)
    for g in range(KVH):
        for dd in range(0, Dh, 2):
            perm_k[g * Dh + dd // 2] = g * Dh * 2 + dd * 2
        for dd in range(1, Dh, 2):
            perm_k[g * Dh + 32 + dd // 2] = g * Dh * 2 + dd * 2
    swap_k = np.zeros(KVH * Dh, np.int64)
    for g in range(KVH):
        swap_k[g * Dh:g * Dh + 32] = np.arange(g * Dh + 32, g * Dh + 64)
        swap_k[g * Dh + 32:g * Dh + 64] = np.arange(g * Dh, g * Dh + 32)
    perm_v = np.array([g * Dh * 2 + dd * 2 + 1 for g in range(KVH) for dd in range(Dh)])
    Wqb_p = Wqb[:, perm_q]
    Wqb_ps = Wqb_p[:, swap]
    Wk = Wkvb[:, perm_k]
    Wks = Wk[:, swap_k]
    Wv = Wkvb[:, perm_v]
    d = {}
    wlat = np.zeros((1024, 128), np.float32)
    wlat[:, 0:64] = Wqa
    d["a0_wlat"] = np.ascontiguousarray(
        wlat.reshape(8, 128, 128).transpose(1, 0, 2).reshape(128, 1024))
    wlat2 = np.zeros((1024, 128), np.float32)
    wlat2[:, 0:32] = Wkva
    d["a0_wlat2"] = np.ascontiguousarray(
        wlat2.reshape(8, 128, 128).transpose(1, 0, 2).reshape(128, 1024))
    d["a0_wqb"] = np.ascontiguousarray(Wqb_p)
    d["a0_wqbs"] = np.ascontiguousarray(Wqb_ps)
    krep = np.zeros((32, 4 * 128), np.float32)
    ksrep = np.zeros((32, 4 * 128), np.float32)
    for g in range(4):
        krep[:, g * 128:g * 128 + 64] = Wk[:, g * 64:(g + 1) * 64]
        krep[:, g * 128 + 64:g * 128 + 128] = Wk[:, g * 64:(g + 1) * 64]
        ksrep[:, g * 128:g * 128 + 64] = Wks[:, g * 64:(g + 1) * 64]
        ksrep[:, g * 128 + 64:g * 128 + 128] = Wks[:, g * 64:(g + 1) * 64]
    d["a0_wkvb"] = np.ascontiguousarray(
        np.concatenate([krep, ksrep, Wv], axis=1))
    d["a0_wout"] = np.ascontiguousarray(
        Wout.reshape(8, 128, 8, 128).transpose(1, 0, 2, 3).reshape(128, 8192))
    d["a0_bout"] = np.ascontiguousarray(bout.reshape(8, 128).T)
    return d


def decl_l0attn_hbm(nc):
    h = {}
    h["a0_wlat"] = nc.dram_tensor("a0_wlat", [128, 1024], F32, kind="ExternalInput").ap()
    h["a0_wlat2"] = nc.dram_tensor("a0_wlat2", [128, 1024], F32, kind="ExternalInput").ap()
    h["a0_wqb"] = nc.dram_tensor("a0_wqb", [64, 1024], F32, kind="ExternalInput").ap()
    h["a0_wqbs"] = nc.dram_tensor("a0_wqbs", [64, 1024], F32, kind="ExternalInput").ap()
    h["a0_wkvb"] = nc.dram_tensor("a0_wkvb", [32, 1280], F32, kind="ExternalInput").ap()
    h["a0_wout"] = nc.dram_tensor("a0_wout", [128, 8192], F32, kind="ExternalInput").ap()
    h["a0_wout_bf"] = nc.dram_tensor("a0_wout_bf", [128, 8192], BF16, kind="Internal").ap()
    h["a0_bout"] = nc.dram_tensor("a0_bout", [128, 8], F32, kind="ExternalInput").ap()
    return h


class AttnBufs:
    def __init__(self, nc, es, B):
        A = lambda name, shape, dt: es.enter_context(nc.sbuf_tensor(name, shape, dt))
        self.B = B
        self.lat = B.x[:, 1024:3072].bitcast(BF16)
        self.vT = B.x[:, 3072:5120].bitcast(BF16)
        self.vtok = B.x[:, 5120:9216].bitcast(BF16)
        self.PT = B.x[:, 9216:13312].bitcast(BF16)
        self.stg = B.x[:, 0:4096]
        self.attnT = B.xn[:]
        self.qT = A("a0_qT", [128, 8 * T], BF16)
        self.kT = A("a0_kT", [128, 4 * T], BF16)
        self.wob = A("a0_wob", [128, 2 * 1024], BF16)
        self.qs = A("a0_qs", [128, T], BF16)
        self.Ct = A("a0_C", [128, T], BF16)
        self.S2t = A("a0_S2", [128, T], BF16)

        self.wqb = A("a0_wqb_sb", [64, 1024], BF16)
        self.wqbs = A("a0_wqbs_sb", [64, 1024], BF16)
        self.wkvb = A("a0_wkvb_sb", [32, 1280], BF16)
        self.bout = A("a0_bout_sb", [128, 8], F32)
        self.wlat = A("a0_wlat_sb", [128, 1024], BF16)
        self.wlat2 = A("a0_wlat2_sb", [128, 1024], BF16)
        self.nsc = A("a0_nsc", [128, 256], F32)
        self.po = B.x[:, 0:512]
        self.rb = B.x[:, 512:1024]
        self.masks = B.x[:, 13312:14336].bitcast(BF16)
        self.outT = self.qT[:]


def _emit_fmod(P, out_ap, in_ap, m, iscr, fscr):
    P.op(DVE, lambda e: e.tensor_scalar_mul(out=fscr, in0=in_ap, scalar1=1.0 / m))
    P.flush(DVE)
    P.op(DVE, lambda e: e.tensor_copy(out=iscr, in_=fscr))
    P.flush(DVE)
    P.op(DVE, lambda e: e.tensor_copy(out=fscr, in_=iscr))
    P.flush(DVE)
    P.op(DVE, lambda e: e.scalar_tensor_tensor(out=out_ap, in0=fscr, scalar=-m,
                                               in1=in_ap, op0=ALU.mult, op1=ALU.add))
    P.flush(DVE)
    P.op(DVE, lambda e: e.tensor_single_scalar(out=fscr, in_=out_ap, scalar=0.0,
                                               op=ALU.is_lt))
    P.flush(DVE)
    P.op(DVE, lambda e: e.scalar_tensor_tensor(out=out_ap, in0=fscr, scalar=m,
                                               in1=out_ap, op0=ALU.mult, op1=ALU.add))
    P.flush(DVE)


def emit_l0a_tables(P, A_, const_tok):
    B = A_.B
    S = A_.nsc
    P.op(POOL, lambda e: e.iota(S[:, 0:1], pattern=[[0, 1]], base=0,
                                channel_multiplier=1,
                                allow_small_or_imprecise_dtypes=True), deps=[const_tok])
    t_i = P.tok(POOL)
    iscr1 = A_.stg[:, 3072:3073].bitcast(mybir.dt.int32)
    P.op(DVE, lambda e: e.nop() if False else None, deps=[t_i])
    _emit_fmod(P, S[:, 1:2], S[:, 0:1], 32.0, iscr1, S[:, 4:5])
    t0 = P.flush(DVE)
    P.op(ACT, lambda e: e.activation(out=S[:, 2:3], in_=S[:, 1:2], func=AF.Exp,
                                     scale=-(float(np.log(10000.0)) / 32.0)), deps=[t0])
    t1 = P.tok(ACT)
    P.op(POOL, lambda e: e.iota(A_.stg[:, 0:T], pattern=[[1, T]], base=0,
                                channel_multiplier=0,
                                allow_small_or_imprecise_dtypes=True))
    t2 = P.tok(POOL)
    ang = A_.stg[:, 0:T]
    tmp = A_.stg[:, T:2 * T]
    P.op(DVE, lambda e: e.tensor_scalar_mul(out=ang, in0=ang, scalar1=S[:, 2:3]),
         deps=[t1, t2])
    TWO_PI = float(2 * np.pi)
    iscr = A_.B.x[:, 4096:4096 + T].bitcast(mybir.dt.int32)[:, 0:T]
    P.flush(DVE)
    _emit_fmod(P, ang, ang, TWO_PI, iscr, tmp)
    P.flush(DVE)
    P.op(DVE, lambda e: e.tensor_single_scalar(out=tmp, in_=ang, scalar=float(np.pi),
                                               op=ALU.is_gt))
    P.flush(DVE)
    P.op(DVE, lambda e: e.scalar_tensor_tensor(out=ang, in0=tmp, scalar=-TWO_PI,
                                               in1=ang, op0=ALU.mult, op1=ALU.add))
    t3 = P.flush(DVE)
    P.op(ACT, lambda e: e.activation(out=tmp, in_=ang, func=AF.Sin), deps=[t3])
    t4 = P.tok(ACT)
    P.op(DVE, lambda e: e.tensor_copy(out=S[:, 5:6], in_=S[:, 0:1]), deps=[t_i])
    P.flush(DVE)
    _emit_fmod(P, S[:, 3:4], S[:, 5:6], 64.0, iscr1, S[:, 6:7])
    P.flush(DVE)
    P.op(DVE, lambda e: e.tensor_single_scalar(out=S[:, 3:4], in_=S[:, 3:4],
                                               scalar=31.5, op=ALU.is_gt))
    P.flush(DVE)
    P.op(DVE, lambda e: e.tensor_scalar(out=S[:, 3:4], in0=S[:, 3:4], scalar1=2.0,
                                        scalar2=-1.0, op0=ALU.mult, op1=ALU.add))
    t5 = P.flush(DVE)
    P.op(DVE, lambda e: e.tensor_scalar_mul(out=A_.S2t[:], in0=tmp,
                                            scalar1=S[:, 3:4]), deps=[t4, t5])
    P.op(DVE, lambda e: e.tensor_scalar_add(out=ang, in0=ang,
                                            scalar1=float(np.pi / 2)))
    P.flush(DVE)
    P.op(DVE, lambda e: e.tensor_single_scalar(out=tmp, in_=ang, scalar=float(np.pi),
                                               op=ALU.is_gt))
    P.flush(DVE)
    P.op(DVE, lambda e: e.scalar_tensor_tensor(out=ang, in0=tmp, scalar=-TWO_PI,
                                               in1=ang, op0=ALU.mult, op1=ALU.add))
    t6 = P.flush(DVE)
    P.op(ACT, lambda e: e.activation(out=A_.Ct[:], in_=ang, func=AF.Sin), deps=[t6])
    t7 = P.tok(ACT)
    return [t7, t5]


def emit_l0a_weights(P, A_, H, deps):
    stg = A_.stg
    items = [(H["a0_wlat"], A_.wlat, [128, 1024]),
             (H["a0_wlat2"], A_.wlat2, [128, 1024]),
             (H["a0_wqb"], A_.wqb, [64, 1024]),
             (H["a0_wqbs"], A_.wqbs, [64, 1024]),
             (H["a0_wkvb"], A_.wkvb, [32, 1280])]
    prev = list(deps)
    for (srch, dst, shp) in items:
        if shp[0] == 128:
            d = P.dma(SYNC, stg[:, 0:shp[1]], srch, deps=prev)
            P.op(DVE, lambda e, dst=dst, w=shp[1]: e.tensor_copy(
                out=dst[:], in_=A_.stg[:, 0:w]), deps=[d])
        else:
            d = P.dma(SYNC, stg[0:shp[0], 0:shp[1]], srch, deps=prev)
            P.op(DVE, lambda e, dst=dst, p=shp[0], w=shp[1]: e.tensor_copy(
                out=dst[:], in_=A_.stg[0:p, 0:w]), deps=[d])
        prev = [P.tok(DVE)]
    woutb = A_.stg[:, 2048:3072].bitcast(BF16)
    for pc in range(4):
        d = P.dma(SYNC, stg[:, 0:2048], H["a0_wout"][:, pc * 2048:(pc + 1) * 2048],
                  deps=prev)
        P.op(DVE, lambda e, w=woutb: e.tensor_copy(out=w, in_=A_.stg[:, 0:2048]),
             deps=[d])
        c1 = P.tok(DVE)
        prev = [P.dma(SYNC, H["a0_wout_bf"][:, pc * 2048:(pc + 1) * 2048], woutb,
                      deps=[c1]), c1]
    db = P.dma(SYNC, A_.bout[:], H["a0_bout"], deps=prev)
    return prev + [db]


def emit_l0_attn(P, B, A_, H, b, tok_xnT, tbl_toks, w_tok, xfree_tok,
                 stage="full", dbg=None):
    xnTv = B.xnT[:].rearrange("p (dc t) -> p dc t", t=T)
    qTv = A_.qT[:].rearrange("p (dc t) -> p dc t", t=T)
    scale = float(64 ** -0.5)
    deps0 = [tok_xnT, w_tok] + list(tbl_toks) + [xfree_tok]
    lat_toks = []
    for li, wl in enumerate([A_.wlat, A_.wlat2]):
        for tt in range(4):
            for k in range(ND):
                P.op(PE, lambda e, wl=wl, k=k, tt=tt: e.matmul(
                    B.psb(0), wl[:, k * 128:(k + 1) * 128],
                    xnTv[:, k, tt * 512:(tt + 1) * 512],
                    start=(k == 0), stop=(k == ND - 1)),
                    deps=(deps0 + lat_toks[-1:]))
            mt = P.tok(PE)
            P.op(DVE, lambda e, li=li, tt=tt: e.tensor_copy(
                out=A_.lat[:, li * T + tt * 512:li * T + (tt + 1) * 512],
                in_=B.psb(0)), deps=[mt])
            lat_toks.append(P.tok(DVE))
    t_lat = lat_toks[-1]
    rope_toks = []
    prev_ev = []
    for dc in range(ND):
        for tt in range(4):
            P.op(PE, lambda e, dc=dc, tt=tt: e.matmul(
                B.psb(tt % 2), A_.wqb[:, dc * 128:(dc + 1) * 128],
                A_.lat[0:64, tt * 512:(tt + 1) * 512], start=True, stop=True),
                deps=[t_lat] + prev_ev)
            P.op(PE, lambda e, dc=dc, tt=tt: e.matmul(
                B.psb(2 + tt % 2), A_.wqbs[:, dc * 128:(dc + 1) * 128],
                A_.lat[0:64, tt * 512:(tt + 1) * 512], start=True, stop=True))
            mt = P.tok(PE)
            P.op(DVE, lambda e, dc=dc, tt=tt: e.tensor_copy(
                out=qTv[:, dc, tt * 512:(tt + 1) * 512], in_=B.psb(tt % 2)),
                deps=[mt])
            P.op(DVE, lambda e, dc=dc, tt=tt: e.tensor_copy(
                out=A_.qs[:, tt * 512:(tt + 1) * 512], in_=B.psb(2 + tt % 2)))
            prev_ev = [P.tok(DVE)]
        P.op(DVE, lambda e, dc=dc: e.tensor_mul(out=qTv[:, dc, :], in0=qTv[:, dc, :],
                                                in1=A_.Ct[:]))
        P.op(DVE, lambda e, dc=dc: e.tensor_mul(out=A_.qs[:], in0=A_.qs[:],
                                                in1=A_.S2t[:]))
        P.flush(DVE)
        P.op(DVE, lambda e, dc=dc: e.tensor_add(out=qTv[:, dc, :], in0=qTv[:, dc, :],
                                                in1=A_.qs[:]))
        prev_ev = [P.tok(DVE)]
        rope_toks.append(prev_ev[0])
    kv_dsts = ([(A_.kT, g, False) for g in range(4)] +
               [(None, g, True) for g in range(4)] +
               [(A_.vT, 0, False), (A_.vT, 1, False)])
    ks_scr = A_.qs
    for mi, (dst, half, is_ks) in enumerate(kv_dsts):
        for tt in range(4):
            P.op(PE, lambda e, mi=mi, tt=tt: e.matmul(
                B.psb(4 + tt % 2), A_.wkvb[:, mi * 128:(mi + 1) * 128],
                A_.lat[0:32, T + tt * 512:T + (tt + 1) * 512], start=True, stop=True),
                deps=[t_lat] + prev_ev)
            mt = P.tok(PE)
            if is_ks:
                P.op(DVE, lambda e, mi=mi, tt=tt: e.tensor_copy(
                    out=ks_scr[:, tt * 512:(tt + 1) * 512], in_=B.psb(4 + tt % 2)),
                    deps=[mt])
            else:
                P.op(DVE, lambda e, dst=dst, half=half, tt=tt: e.tensor_copy(
                    out=dst[:, half * T + tt * 512:half * T + (tt + 1) * 512],
                    in_=B.psb(4 + tt % 2)), deps=[mt])
            prev_ev = [P.tok(DVE)]
        if is_ks:
            P.op(DVE, lambda e, half=half: e.tensor_mul(
                out=A_.kT[:, half * T:(half + 1) * T],
                in0=A_.kT[:, half * T:(half + 1) * T], in1=A_.Ct[:]))
            P.op(DVE, lambda e: e.tensor_mul(out=ks_scr[:], in0=ks_scr[:],
                                             in1=A_.S2t[:]))
            P.flush(DVE)
            P.op(DVE, lambda e, half=half: e.tensor_add(
                out=A_.kT[:, half * T:(half + 1) * T],
                in0=A_.kT[:, half * T:(half + 1) * T], in1=ks_scr[:]))
            prev_ev = [P.tok(DVE)]
    t_kv = prev_ev[0]
    t_q = rope_toks[-1]
    if stage == "proj":
        return t_kv
    P.op(POOL, lambda e: e.iota(A_.masks[:], pattern=[[0, 4], [1, 512]], base=0,
                                channel_multiplier=-1,
                                allow_small_or_imprecise_dtypes=True),
         deps=[xfree_tok])
    t8m = P.tok(POOL)
    for r in range(4):
        P.op(DVE, lambda e, r=r: e.tensor_single_scalar(
            out=A_.masks[:, r * 512:(r + 1) * 512],
            in_=A_.masks[:, r * 512:(r + 1) * 512],
            scalar=128.0 * r - 0.5, op=ALU.is_gt), deps=[t8m])
    t_masks = P.flush(DVE)
    vtokv = A_.vtok.rearrange("p (g tc c) -> p g tc c", g=4, c=128)
    pv_prev = [t_kv, xfree_tok]
    for g in range(4):
        for tc in range(NT):
            bank = (g * NT + tc) % 4
            vb = (g % 2) * 64
            P.op(PE, lambda e, g=g, tc=tc, bank=bank, vb=vb: e.transpose(
                B.psb(bank, 1, BF16)[:, 0:64],
                A_.vT[vb:vb + 64,
                      (g // 2) * T + tc * 128:(g // 2) * T + (tc + 1) * 128],
                B.ident[vb:vb + 64, vb:vb + 64]),
                deps=pv_prev if (g == 0 and tc == 0) else
                ([pv_toks[-4]] if g * NT + tc >= 4 else []))
            mt = P.tok(PE)
            P.op(DVE, lambda e, g=g, tc=tc, bank=bank: e.tensor_copy(
                out=vtokv[:, g, tc, 0:64], in_=B.psb(bank, 1, BF16)[:, 0:64]),
                deps=[mt])
            P.op(DVE, lambda e, g=g, tc=tc, bank=bank: e.tensor_copy(
                out=vtokv[:, g, tc, 64:128], in_=B.psb(bank, 1, BF16)[:, 0:64]))
            if g == 0 and tc == 0:
                pv_toks = []
            pv_toks.append(P.tok(DVE))
    t_vtok = pv_toks[-1]
    if stage == "vtok":
        return t_vtok
    S = A_.nsc
    P.op(POOL, lambda e: e.iota(A_.rb[:, 0:128], pattern=[[1, 128]], base=0,
                                channel_multiplier=0,
                                allow_small_or_imprecise_dtypes=True), deps=[t_vtok])
    P.op(POOL, lambda e: e.tensor_single_scalar(out=A_.rb[:, 0:128],
                                                in_=A_.rb[:, 0:128],
                                                scalar=63.5, op=ALU.is_le))
    P.op(POOL, lambda e: e.iota(A_.rb[:, 128:129], pattern=[[0, 1]], base=0,
                                channel_multiplier=1,
                                allow_small_or_imprecise_dtypes=True))
    P.op(POOL, lambda e: e.tensor_single_scalar(out=A_.rb[:, 128:129],
                                                in_=A_.rb[:, 128:129],
                                                scalar=63.5, op=ALU.is_le))
    t_bd0 = P.tok(POOL)
    P.op(DVE, lambda e: e.tensor_scalar_mul(out=A_.rb[:, 256:384], in0=A_.rb[:, 0:128],
                                            scalar1=A_.rb[:, 128:129]), deps=[t_bd0])
    P.op(DVE, lambda e: e.tensor_scalar_mul(out=A_.rb[:, 256:384],
                                            in0=A_.rb[:, 256:384], scalar1=2.0))
    P.op(DVE, lambda e: e.tensor_sub(out=A_.rb[:, 256:384], in0=A_.rb[:, 256:384],
                                     in1=A_.rb[:, 0:128]))
    P.op(DVE, lambda e: e.tensor_scalar(out=A_.rb[:, 256:384], in0=A_.rb[:, 256:384],
                                        scalar1=A_.rb[:, 128:129], scalar2=1.0,
                                        op0=ALU.subtract, op1=ALU.add))
    t_bd1 = P.flush(DVE)
    P.op(DVE, lambda e: e.tensor_copy(out=A_.masks[:].bitcast(F32)[:, 960:1024]
                                      if False else A_.qs[:].bitcast(BF16)[:, 0:128],
                                      in_=A_.rb[:, 256:384]), deps=[t_bd1])
    t_bd = P.flush(DVE)
    bd_bf = A_.qs[:].bitcast(BF16)[:, 0:128]
    def norm_cols(srcv, nchunks, col0, dep):
        toks = [dep]
        for c in range(nchunks):
            P.op(ACT, lambda e, c=c: e.activation(
                out=A_.rb[:, 0:512].bitcast(BF16) if False else A_.po[:].bitcast(BF16),
                in_=srcv(c)[:, 0:512], func=AF.Square), deps=toks[-1:])
            pass
        return None
    def emit_norms(get_chunk, nchunks, col0, dep0):
        dts = list(dep0) if isinstance(dep0, list) else [dep0]
        sq = A_.po[:].bitcast(BF16)
        for c in range(nchunks):
            for tt in range(4):
                P.op(ACT, lambda e, c=c, tt=tt: e.activation(
                    out=sq[:, 0:512], in_=get_chunk(c)[:, tt * 512:(tt + 1) * 512],
                    func=AF.Square), deps=dts[-1:] + [t_bd] + dts[0:1])
                ta = P.tok(ACT)
                P.op(PE, lambda e: e.matmul(B.psb(6), bd_bf, sq[:, 0:512],
                                            start=True, stop=True), deps=[ta])
                mt = P.tok(PE)
                P.op(DVE, lambda e, c=c, tt=tt, col0=col0: e.reduce_max(
                    out=S[:, col0 + 8 + tt:col0 + 9 + tt], in_=B.psb(6), axis=AX.X),
                    deps=[mt])
                dts.append(P.flush(DVE))
            P.op(DVE, lambda e, c=c, col0=col0: e.reduce_max(
                out=S[:, col0 + c:col0 + c + 1], in_=S[:, col0 + 8:col0 + 12],
                axis=AX.X))
            dts.append(P.flush(DVE))
        return dts[-1]
    tq_n = emit_norms(lambda c: qTv[:, c, :], ND, 16, t_q)
    if stage == "norms1":
        return tq_n
    tk_n = emit_norms(lambda c: A_.kT[:, c * T:(c + 1) * T], 4, 32, [t_kv, tq_n])
    P.op(DVE, lambda e: e.reduce_max(out=S[:, 48:49], in_=S[:, 16:24], axis=AX.X),
         deps=[tq_n])
    P.op(DVE, lambda e: e.reduce_max(out=S[:, 49:50], in_=S[:, 32:36], axis=AX.X),
         deps=[tk_n])
    P.op(DVE, lambda e: e.memset(S[:, 64:192], 0.0))
    t_m0 = P.flush(DVE)
    P.op(DVE, lambda e: e.tensor_copy(out=S[:, 64:65], in_=S[:, 48:49]), deps=[t_m0])
    P.op(DVE, lambda e: e.tensor_copy(out=S[:, 65:66], in_=S[:, 49:50]))
    P.op(DVE, lambda e: e.tensor_copy(out=A_.rb[:, 0:128].bitcast(BF16)[:, 0:128]
                                      if False else A_.qs[:].bitcast(BF16)[:, 128:256],
                                      in_=S[:, 64:192]), deps=[])
    t_m1 = P.flush(DVE)
    P.op(PE, lambda e: e.transpose(B.psb(6, 1, BF16)[:, 0:128],
                                   A_.qs[:].bitcast(BF16)[:, 128:256], B.ident[:]),
         deps=[t_m1])
    t_m2 = P.tok(PE)
    P.op(DVE, lambda e: e.tensor_copy(out=S[:, 96:224] if False else S[:, 96:96 + 128],
                                      in_=B.psb(6, 1, BF16)[:, 0:128]), deps=[t_m2])
    t_m3 = P.flush(DVE)
    P.op(DVE, lambda e: e.reduce_max(out=S[:, 50:51], in_=S[:, 96:224],
                                     axis=AX.X), deps=[t_m3])
    t_m4 = P.flush(DVE)
    if stage == "norms2":
        return t_m4
    P.op(ACT, lambda e: e.activation(out=S[:, 51:52], in_=S[:, 50:51],
                                     func=AF.Ln), deps=[t_m4])
    t_m5 = P.tok(ACT)
    P.op(DVE, lambda e: e.tensor_copy(out=A_.qs[:].bitcast(BF16)[0:2, 256:257],
                                      in_=S[0:2, 51:52]), deps=[t_m5])
    t_m6 = P.flush(DVE)
    P.op(PE, lambda e: e.matmul(B.psb(6)[:, 0:1], B.ones[0:2, :],
                                A_.qs[:].bitcast(BF16)[0:2, 256:257],
                                start=True, stop=True), deps=[t_m6])
    t_m7 = P.tok(PE)
    P.op(ACT, lambda e: e.activation(out=S[:, 52:53], in_=B.psb(6)[:, 0:1],
                                     func=AF.Exp, scale=0.5), deps=[t_m7])
    t_m8 = P.tok(ACT)
    P.op(DVE, lambda e: e.tensor_scalar_mul(out=S[:, 53:54], in0=S[:, 52:53],
                                            scalar1=-scale), deps=[t_m8])
    t_mhat = P.flush(DVE)
    if stage == "mhat":
        return t_mhat
    maskv = A_.masks[:].rearrange("p (r c) -> p r c", c=512)
    PTv = A_.PT.rearrange("p (tc c) -> p tc c", c=512)
    t_out_prev = []
    kTv = A_.kT
    NH = 1 if stage == "attn1h" else 16
    for h in range(NH):
        g = h // 4
        qbase = (h % 2) * 64
        kbase = qbase
        koff = g * T
        qchunk = h // 2
        for qb in range(4):
            ntc = 4 * (qb + 1)
            pt_toks = []
            for tc in range(ntc):
                bank = tc % 2
                deps = [t_mhat, t_q, t_kv]
                if tc >= 2:
                    deps.append(pt_toks[tc - 2])
                if t_out_prev:
                    deps.append(t_out_prev[-1])
                P.op(PE, lambda e, tc=tc, bank=bank, kb=kbase, ko=koff,
                     qb_=qbase, qc=qchunk, qq=qb: e.matmul(
                    B.psb(bank),
                    kTv[kb:kb + 64, ko + tc * 128:ko + (tc + 1) * 128],
                    A_.qT[qb_:qb_ + 64, qc * T + qq * 512:qc * T + (qq + 1) * 512],
                    start=True, stop=True), deps=deps)
                mt = P.tok(PE)
                P.op(ACT, lambda e, tc=tc, bank=bank: e.activation(
                    out=PTv[:, tc, :], in_=B.psb(bank), func=AF.Exp,
                    bias=S[:, 53:54], scale=scale), deps=[mt])
                ta = P.tok(ACT)
                if tc >= ntc - 4:
                    r = tc - (ntc - 4)
                    P.op(DVE, lambda e, tc=tc, r=r: e.tensor_mul(
                        out=PTv[:, tc, :], in0=PTv[:, tc, :], in1=maskv[:, r, :]),
                        deps=[ta, t_masks])
                    pt_toks.append(P.tok(DVE))
                else:
                    pt_toks.append(ta)
            for tc in range(ntc):
                P.op(PE, lambda e, tc=tc, g=g, ntc=ntc: e.matmul(
                    B.psb(7), vtokv[:, g, tc, :], PTv[:, tc, :],
                    start=(tc == 0), stop=(tc == ntc - 1)),
                    deps=[pt_toks[tc]])
                P.op(PE, lambda e, tc=tc, ntc=ntc: e.matmul(
                    B.psb(6), B.ones[:], PTv[:, tc, :],
                    start=(tc == 0), stop=(tc == ntc - 1)))
            mt = P.tok(PE)
            P.op(DVE, lambda e: e.tensor_copy(out=A_.po[:], in_=B.psb(7)), deps=[mt])
            P.op(DVE, lambda e: e.tensor_copy(out=A_.rb[:], in_=B.psb(6)))
            t_po = P.flush(DVE)
            P.op(DVE, lambda e, qb_=qbase: e.reciprocal(
                out=A_.rb[qb_:qb_ + 64, :] if qb_ else A_.rb[0:64, :],
                in_=A_.rb[qb_:qb_ + 64, :] if qb_ else A_.rb[0:64, :]), deps=[t_po])
            t_r = P.flush(DVE)
            P.op(DVE, lambda e, qb_=qbase, qc=qchunk, qq=qb: e.tensor_mul(
                out=A_.attnT[qb_:qb_ + 64,
                             qc * T + qq * 512:qc * T + (qq + 1) * 512],
                in0=A_.po[qb_:qb_ + 64, :], in1=A_.rb[qb_:qb_ + 64, :]), deps=[t_r])
            t_out_prev = [P.flush(DVE)]
    t_attn = t_out_prev[-1]
    if stage == "attn1h":
        return t_attn
    if dbg is not None and b == 0:
        dd = [P.dma(POOL, dbg["Ct"], A_.Ct[:], deps=[t_attn]),
              P.dma(POOL, dbg["S2t"], A_.S2t[:], deps=[t_attn]),
              P.dma(POOL, dbg["qT"], A_.qT[:], deps=[t_attn]),
              P.dma(POOL, dbg["kT"], A_.kT[:], deps=[t_attn]),
              P.dma(POOL, dbg["nsc"][:, 52:54], A_.nsc[:, 52:54], deps=[t_attn]),
              P.dma(POOL, dbg["attnT"], A_.attnT[:, 0:ND * T], deps=[t_attn])]
        t_attn2 = P.tok(POOL)
        P.op(POOL, lambda e: e.nop(), deps=dd)
        t_attn = P.tok(POOL)
    out_toks = []
    attnTv = A_.attnT[:].rearrange("p (dc t) -> p dc t", t=T)
    outTv = A_.outT.rearrange("p (dc t) -> p dc t", t=T)
    wsrc = H["a0_wout_bf"].rearrange("p (k m c) -> p k m c", m=8, c=128)
    wobv = A_.wob[:].rearrange("p (pb k c) -> p pb k c", pb=2, c=128)
    wtoks = {}
    for dc in range(ND):
        par = dc % 2
        wdep = [wtoks[dc - 2]] if dc >= 2 else []
        wd = P.dma(SYNC, wobv[:, par, :, :], wsrc[:, :, dc, :],
                   deps=wdep + [t_attn])
        for tt in range(4):
            bank = (dc * 4 + tt) % 2
            for k in range(ND):
                P.op(PE, lambda e, par=par, tt=tt, k=k, bank=bank: e.matmul(
                    B.psb(bank), wobv[:, par, k, :],
                    attnTv[:, k, tt * 512:(tt + 1) * 512],
                    start=(k == 0), stop=(k == ND - 1)),
                    deps=([wd, t_attn] + out_toks[-2:-1]) if k == 0 else [])
            mt = P.tok(PE)
            if tt == 3:
                wtoks[dc] = mt
            P.op(DVE, lambda e, dc=dc, tt=tt, bank=bank: e.tensor_scalar_add(
                out=outTv[:, dc, tt * 512:(tt + 1) * 512], in0=B.psb(bank),
                scalar1=A_.bout[:, dc:dc + 1]), deps=[mt])
            out_toks.append(P.tok(DVE))
    return out_toks[-1]


def build_l0attn_launch(nc):
    x_h = nc.dram_tensor("x", [B_PER_CORE, T, D], F32, kind="ExternalInput").ap()
    y_h = nc.dram_tensor("y", [B_PER_CORE, T, D], F32, kind="ExternalOutput").ap()
    H = decl_l0attn_hbm(nc)
    es = contextlib.ExitStack()
    with es:
        B = Bufs(nc, es)
        A_ = AttnBufs(nc, es, B)
        P = Prog(nc, es)
        ct = emit_const_setup(P, B)
        tbl = emit_l0a_tables(P, A_, ct)
        wt = emit_l0a_weights(P, A_, H, tbl)
        prev = list(wt)
        final_stores = []
        for b in range(B_PER_CORE):
            xtoks = emit_load_x(P, B, x_h, b, deps=prev)
            txn, txnT = emit_ln(P, B, ct, xtoks, war_deps=prev if b > 0 else ())
            stage = os.environ.get("A0_STAGE", "full")
            dbg = None
            if os.environ.get("A0_DBG", "0") == "1" and b == 0:
                dbg = {"qT": nc.dram_tensor("dbg_qT", [128, 8 * T], BF16,
                                            kind="ExternalOutput").ap(),
                       "kT": nc.dram_tensor("dbg_kT", [128, 4 * T], BF16,
                                            kind="ExternalOutput").ap(),
                       "nsc": nc.dram_tensor("dbg_nsc", [128, 256], F32,
                                             kind="ExternalOutput").ap(),
                       "attnT": nc.dram_tensor("dbg_attnT", [128, 8 * T], BF16,
                                               kind="ExternalOutput").ap(),
                       "Ct": nc.dram_tensor("dbg_Ct", [128, T], BF16,
                                            kind="ExternalOutput").ap(),
                       "S2t": nc.dram_tensor("dbg_S2t", [128, T], BF16,
                                             kind="ExternalOutput").ap()}
            t_outT = emit_l0_attn(P, B, A_, H, b, txnT, tbl, wt[0], txn, stage=stage,
                                  dbg=dbg)
            if stage != "full":
                yv0 = y_h[b].rearrange("(tc p) d -> p tc d", p=128)
                xv0 = B.stats
                sts = [P.dma(SYNC, yv0[:, tc, 0:256], B.stats[:],
                             deps=[t_outT]) for tc in range(NT)]
                P.op(SYNC, lambda e: e.nop(), deps=sts)
                with nc.Block() as block:
                    P.emit(block)
                return nc
            xtoks2 = emit_load_x(P, B, x_h, b, deps=[t_outT])
            yv = y_h[b].rearrange("(tc p) d -> p tc d", p=128)
            store_y = [yv[:, tc, :] for tc in range(NT)]
            tok_x, st_toks, _ = emit_residual_add(P, B, A_.outT, t_outT, xtoks2,
                                                  store_y=store_y)
            final_stores += st_toks
            prev = [tok_x] + st_toks
        P.op(SYNC, lambda e: e.nop(), deps=final_stores)
        with nc.Block() as block:
            P.emit(block)
    return nc


def host_prep_l1ffn(inp):
    f = _as32
    w1 = f(inp["L1_ffn_w1"])
    w2 = f(inp["L1_ffn_w2"])
    wih = f(inp["L1_ffn_gru_wih"])
    whh = f(inp["L1_ffn_gru_whh"])
    wv = f(inp["L1_ffn_wv"])
    wo = f(inp["L1_ffn_wo"])
    d = {}
    w1a = np.ascontiguousarray(w1[:, 0:1024].T)
    d["f1_w1a"] = np.ascontiguousarray(
        w1a.reshape(8, 128, 4, 128).transpose(1, 0, 2, 3).reshape(128, 4096))
    w1b = np.ascontiguousarray(w1[:, 1024:2048].T)
    d["f1_w1b"] = np.ascontiguousarray(
        w1b.reshape(8, 128, 4, 128).transpose(1, 0, 2, 3).reshape(128, 4096))
    w2tp = np.zeros((512, 128), np.float32)
    w2tp[:, 0:64] = w2.T
    d["f1_w2tp"] = np.ascontiguousarray(
        w2tp.reshape(4, 128, 128).transpose(1, 0, 2).reshape(128, 512))
    lay = lambda W, M: np.ascontiguousarray(
        W.T.reshape(8, 128, M, 128).transpose(1, 2, 0, 3).reshape(128, M * 8 * 128))
    d["f1_wih"] = lay(wih, 24)
    d["f1_whh"] = lay(whh, 24)
    d["f1_wv"] = lay(wv, 8)
    d["f1_wo"] = lay(wo, 8)
    d["f1_b1"] = np.ascontiguousarray(f(inp["L1_ffn_b1"]).reshape(4, 128).T)
    b2p = np.zeros((128, 1), np.float32)
    b2p[0:64, 0] = f(inp["L1_ffn_b2"])
    d["f1_b2"] = b2p
    d["f1_bih"] = np.ascontiguousarray(f(inp["L1_ffn_gru_bih"]).reshape(24, 128).T)
    d["f1_bhh"] = np.ascontiguousarray(f(inp["L1_ffn_gru_bhh"]).reshape(24, 128).T)
    d["f1_bv"] = np.ascontiguousarray(f(inp["L1_ffn_bv"]).reshape(8, 128).T)
    d["f1_bo"] = np.ascontiguousarray(f(inp["L1_ffn_bo"]).reshape(8, 128).T)
    d["f1_sinitT"] = np.ascontiguousarray(
        f(inp["L1_ffn_slot_init"]).T.reshape(8, 128, 64).transpose(1, 0, 2).reshape(128, 512))
    d["f1_sscale"] = np.ascontiguousarray(f(inp["L1_ffn_slot_scale"]).reshape(1, 64))
    d["f1_tau"] = f(inp["L1_ffn_tau"]).reshape(1, 1)
    return d


def decl_l1ffn_hbm(nc):
    h = {}
    for nm, shp in [("f1_w1a", [128, 4096]), ("f1_w1b", [128, 4096]),
                    ("f1_w2tp", [128, 512]), ("f1_wih", [128, 24576]),
                    ("f1_whh", [128, 24576]), ("f1_wv", [128, 8192]),
                    ("f1_wo", [128, 8192]), ("f1_b1", [128, 4]), ("f1_b2", [128, 1]),
                    ("f1_bih", [128, 24]), ("f1_bhh", [128, 24]),
                    ("f1_bv", [128, 8]), ("f1_bo", [128, 8]),
                    ("f1_sinitT", [128, 512]), ("f1_sscale", [1, 64]),
                    ("f1_tau", [1, 1])]:
        h[nm] = nc.dram_tensor(nm, shp, F32, kind="ExternalInput").ap()
    for nm, shp in [("f1_wih_bf", [128, 24576]), ("f1_whh_bf", [128, 24576]),
                    ("f1_wv_bf", [128, 8192]), ("f1_wo_bf", [128, 8192])]:
        h[nm] = nc.dram_tensor(nm, shp, BF16, kind="Internal").ap()
    if os.environ.get("F1_DBG"):
        for nm, shp, dt in [("d1_log", [128, 1024], F32), ("d1_atok", [128, 2048], BF16),
                            ("d1_aT", [128, 2048], BF16), ("d1_gi", [128, 1024], BF16),
                            ("d1_SI", [128, 1024], F32), ("d1_ST", [128, 512], BF16),
                            ("d1_SnT", [128, 512], BF16), ("d1_svT", [128, 1024], BF16),
                            ("d1_ghT", [128, 1536], F32), ("d1_hT", [128, 8192], BF16),
                            ("d1_outT", [128, 16384], BF16), ("d1_c16", [128, 96], F32), ("d1_cols", [128, 96], F32)]:
            h[nm] = nc.dram_tensor(nm, shp, dt, kind="ExternalOutput").ap()
    return h


class F1Bufs:
    def __init__(self, nc, es, B):
        A = lambda name, shape, dt: es.enter_context(nc.sbuf_tensor(name, shape, dt))
        self.B = B
        self.stg = B.xn[:].bitcast(F32)[:, 0:4096]
        self.outT = B.xnT[:]
        self.hT = A("f1_hT", [128, 4 * T], BF16)
        self.ghT = A("f1_ghT", [128, 24 * 64], F32)
        self.logtok = A("f1_logtok", [128, 16 * 64], F32)
        self.atok = A("f1_atok", [128, 16 * 128], BF16)
        self.aT = A("f1_aT", [128, 2048], BF16)
        self.giT = A("f1_giT", [128, 8 * 64], BF16)
        self.ST = A("f1_ST", [128, 8 * 64], BF16)
        self.SnT = A("f1_SnT", [128, 8 * 64], BF16)
        self.svT = A("f1_svT", [128, 8 * 128], BF16)
        self.svs = A("f1_svs", [128, 8 * 128], BF16)
        self.gsc = A("f1_gsc", [128, 16 * 16], F32)
        self.wsb = A("f1_wsb", [128, 2 * 1024], BF16)
        self.w1a = A("f1_w1a_sb", [128, 4096], BF16)
        self.w1b = A("f1_w1b_sb", [128, 4096], BF16)
        self.w2tp = A("f1_w2tp_sb", [128, 512], BF16)
        self.cols = A("f1_cols", [128, 96], F32)
        self.sg = A("f1_sg", [128, 256], F32)
        self.nb = A("f1_nb", [128, 512], F32)
        self.bvb = A("f1_bvb", [128, 8], BF16)
        self.gi = A("f1_gi", [128, 1024], BF16)
        self.SI = A("f1_SI", [128, 1024], F32)


def emit_f1_prologue(P, B, F, H, deps):
    stg = F.stg
    prev = list(deps)
    small = [("f1_b1", F.cols[:, 0:4]), ("f1_b2", F.cols[:, 8:9]),
             ("f1_bih", F.cols[:, 16:40]), ("f1_bhh", F.cols[:, 40:64]),
             ("f1_bv", F.cols[:, 64:72]), ("f1_bo", F.cols[:, 72:80])]
    toks = [P.dma(SYNC, dst, H[nm], deps=deps) for nm, dst in small]
    t_tau = P.dma(SYNC, F.cols[0:1, 9:10], H["f1_tau"], deps=deps)
    toks.append(t_tau)
    for nm, dst, w in [("f1_w1a", F.w1a, 4096), ("f1_w1b", F.w1b, 4096),
                       ("f1_w2tp", F.w2tp, 512)]:
        d = P.dma(SYNC, stg[:, 0:w], H[nm], deps=prev)
        P.op(DVE, lambda e, dst=dst, w=w: e.tensor_copy(out=dst[:], in_=F.stg[:, 0:w]),
             deps=[d])
        prev = [P.tok(DVE)]
    dsi = P.dma(SYNC, stg[:, 0:512], H["f1_sinitT"], deps=prev)
    dsc = P.dma(SYNC, stg[0:1, 512:576], H["f1_sscale"], deps=prev)
    P.op(DVE, lambda e: e.tensor_copy(out=F.sg[0:1, 0:64].bitcast(BF16)[0:1, 0:64],
                                      in_=F.stg[0:1, 512:576]), deps=[dsc])
    t0 = P.flush(DVE)
    P.op(PE, lambda e: e.matmul(B.psb(6)[:, 0:64], B.ones[0:1, :],
                                F.sg[0:1, 0:64].bitcast(BF16)[0:1, 0:64],
                                start=True, stop=True), deps=[t0])
    t1 = P.tok(PE)
    P.op(DVE, lambda e: e.tensor_copy(out=F.sg[:, 0:64],
                                      in_=B.psb(6)[:, 0:64]), deps=[t1, dsi])
    t2 = P.flush(DVE)
    for dc in range(ND):
        P.op(DVE, lambda e, dc=dc: e.tensor_mul(
            out=F.ST[:, dc * 64:(dc + 1) * 64],
            in0=F.stg[:, dc * 64:(dc + 1) * 64],
            in1=F.sg[:, 0:64]), deps=[t2])
    t3 = P.flush(DVE)
    prev = prev + [t3]
    wb = F.stg[:, 2048:3072].bitcast(BF16)
    for nm, dstnm, npieces in [("f1_wih", "f1_wih_bf", 12), ("f1_whh", "f1_whh_bf", 12),
                               ("f1_wv", "f1_wv_bf", 4), ("f1_wo", "f1_wo_bf", 4)]:
        for pc in range(npieces):
            d = P.dma(SYNC, stg[:, 0:2048], H[nm][:, pc * 2048:(pc + 1) * 2048],
                      deps=prev)
            P.op(DVE, lambda e: e.tensor_copy(out=wb[:], in_=F.stg[:, 0:2048]),
                 deps=[d])
            c1 = P.tok(DVE)
            prev = [P.dma(SYNC, H[dstnm][:, pc * 2048:(pc + 1) * 2048], wb,
                          deps=[c1]), c1]
    return prev + toks + [t3]


def emit_l1_ffn(P, B, F, H, b, tok_xn, tok_xnT, pro_tok):
    xnTv = B.xnT[:].rearrange("p (dc t) -> p dc t", t=T)
    xnv = B.xn[:].rearrange("p (tc d) -> p tc d", d=D)
    hTv = F.hT[:].rearrange("p (m t) -> p m t", t=T)
    C = F.cols
    deps0 = [tok_xn, tok_xnT] + list(pro_tok)
    P.op(ACT, lambda e: e.activation(out=C[0:1, 10:11], in_=C[0:1, 9:10],
                                     func=AF.Abs), deps=deps0)
    ta = P.tok(ACT)
    P.op(DVE, lambda e: e.tensor_scalar_add(out=C[0:1, 10:11], in0=C[0:1, 10:11],
                                            scalar1=0.1), deps=[ta])
    P.flush(DVE)
    P.op(DVE, lambda e: e.reciprocal(out=C[0:1, 11:12], in_=C[0:1, 10:11]))
    P.flush(DVE)
    P.op(DVE, lambda e: e.tensor_copy(out=F.gi[0:1, 8:9],
                                      in_=C[0:1, 11:12]))
    t0 = P.flush(DVE)
    P.op(PE, lambda e: e.matmul(B.psb(6)[:, 0:1], B.ones[0:1, :],
                                F.gi[0:1, 8:9],
                                start=True, stop=True), deps=[t0])
    t1 = P.tok(PE)
    P.op(DVE, lambda e: e.tensor_copy(out=C[:, 12:13], in_=B.psb(6)[:, 0:1]),
         deps=[t1])
    t_invt = P.flush(DVE)
    for dc in range(ND):
        P.op(DVE, lambda e, dc=dc: e.reduce_sum(out=C[:, 80 + dc:81 + dc],
                                                in_=F.ST[:, dc * 64:(dc + 1) * 64],
                                                axis=AX.X), deps=deps0)
    P.flush(DVE)
    P.op(DVE, lambda e: e.tensor_scalar_mul(out=C[:, 80:88], in0=C[:, 80:88],
                                            scalar1=1.0 / 64.0))
    P.flush(DVE)
    P.op(DVE, lambda e: e.tensor_copy(out=F.gi[:, 0:8],
                                      in_=C[:, 80:88]))
    t_sm = P.flush(DVE)
    ev = t_sm
    for hc in range(4):
        for k in range(ND):
            P.op(PE, lambda e, hc=hc, k=k: e.matmul(
                B.psb(7)[:, 0:1], F.w1b[:, (k * 4 + hc) * 128:(k * 4 + hc) * 128 + 128],
                F.gi[:, k:k + 1], start=(k == 0), stop=(k == ND - 1)),
                deps=[ev] if k == 0 else [])
        mt = P.tok(PE)
        P.op(DVE, lambda e, hc=hc: e.tensor_copy(out=C[:, 4 + hc:5 + hc],
                                                 in_=B.psb(7)[:, 0:1]), deps=[mt])
        ev = P.tok(DVE)
    P.flush(DVE)
    P.op(DVE, lambda e: e.tensor_add(out=C[:, 4:8], in0=C[:, 4:8], in1=C[:, 0:4]))
    t_c1 = P.flush(DVE)
    wstr = F.wsb[:].rearrange("p (pb k c) -> p pb k c", pb=2, c=128)
    whh_src = H["f1_whh_bf"].rearrange("p (m k c) -> p m k c", k=8, c=128)
    wih_src = H["f1_wih_bf"].rearrange("p (m k c) -> p m k c", k=8, c=128)
    wv_src = H["f1_wv_bf"].rearrange("p (m k c) -> p m k c", k=8, c=128)
    wo_src = H["f1_wo_bf"].rearrange("p (m k c) -> p m k c", k=8, c=128)
    wtk = {}
    prev_ev = [t_c1]
    for m in range(24):
        par = m % 2
        wdep = [wtk[m - 2]] if m >= 2 else [t_c1]
        wd = P.dma(SYNC, wstr[:, par, :, :], whh_src[:, m, :, :],
                   deps=wdep + list(pro_tok))
        for k in range(ND):
            P.op(PE, lambda e, par=par, k=k, m=m: e.matmul(
                B.psb(m % 2)[:, 0:64], wstr[:, par, k, :],
                F.ST[:, k * 64:(k + 1) * 64], start=(k == 0), stop=(k == ND - 1)),
                deps=[wd] + prev_ev if k == 0 else [])
        mt = P.tok(PE)
        wtk[m] = mt
        P.op(DVE, lambda e, m=m: e.tensor_scalar_add(
            out=F.ghT[:, m * 64:(m + 1) * 64], in0=B.psb(m % 2)[:, 0:64],
            scalar1=C[:, 40 + m:41 + m]), deps=[mt])
        prev_ev = [P.tok(DVE)]
    t_gh = prev_ev[0]
    h_toks = []
    for m in range(4):
        for tt in range(4):
            bank = (m * 4 + tt) % 2 + 2
            for k in range(ND):
                P.op(PE, lambda e, m=m, tt=tt, k=k, bank=bank: e.matmul(
                    B.psb(bank), F.w1a[:, (k * 4 + m) * 128:(k * 4 + m) * 128 + 128],
                    xnTv[:, k, tt * 512:(tt + 1) * 512],
                    start=(k == 0), stop=(k == ND - 1)),
                    deps=([t_c1, tok_xnT] + h_toks[-2:-1]) if k == 0 else [])
            mt = P.tok(PE)
            P.op(ACT, lambda e, m=m, tt=tt, bank=bank: e.activation(
                out=hTv[:, m, tt * 512:(tt + 1) * 512], in_=B.psb(bank), func=AF.Gelu,
                bias=C[:, 4 + m:5 + m]), deps=[mt])
            h_toks.append(P.tok(ACT))
    t_h = h_toks[-1]
    lg_toks = []
    for tt in range(4):
        for k in range(4):
            P.op(PE, lambda e, tt=tt, k=k: e.matmul(
                B.psb(4), F.w2tp[:, k * 128:(k + 1) * 128],
                hTv[:, k, tt * 512:(tt + 1) * 512],
                start=(k == 0), stop=(k == 3)),
                deps=([t_h] + lg_toks[-1:]) if k == 0 else [])
        mt = P.tok(PE)
        P.op(DVE, lambda e, tt=tt: e.tensor_scalar(
            out=F.SI[:, 0:512], in0=B.psb(4), scalar1=C[:, 8:9], scalar2=C[:, 12:13],
            op0=ALU.add, op1=ALU.mult), deps=[mt, t_invt])
        ct = P.flush(DVE)
        P.op(DVE, lambda e, tt=tt: e.tensor_copy(
            out=F.svs[:, 0:512], in_=F.SI[:, 0:512]), deps=[ct])
        ct2 = P.flush(DVE)
        for bk in range(4):
            bank = 5 + bk % 2
            P.op(PE, lambda e, tt=tt, bk=bk, bank=bank: e.transpose(
                B.psb(bank, 1, BF16)[:, 0:128], F.svs[:, bk * 128:(bk + 1) * 128],
                B.ident[:]), deps=[ct2] + (lg_toks[-1:] if (tt or bk) else []))
            mt2 = P.tok(PE)
            P.op(DVE, lambda e, tt=tt, bk=bk, bank=bank: e.tensor_scalar_add(
                out=F.logtok[:, (tt * 4 + bk) * 64:(tt * 4 + bk + 1) * 64],
                in0=B.psb(bank, 1, BF16)[:, 0:64], scalar1=0.0), deps=[mt2])
            lg_toks.append(P.tok(DVE))
    t_log = lg_toks[-1]
    G = F.gsc
    P.op(DVE, lambda e: e.memset(F.atok[:], 0.0), deps=deps0)
    t_az = P.flush(DVE)
    for blk in range(16):
        P.op(DVE, lambda e, blk=blk: e.max(out=G[:, blk * 16:blk * 16 + 8],
                                           in_=F.logtok[:, blk * 64:(blk + 1) * 64]),
             deps=[t_log])
    P.flush(DVE)
    for blk in range(16):
        P.op(DVE, lambda e, blk=blk: e.tensor_scalar_mul(
            out=G[:, blk * 16 + 8:blk * 16 + 9], in0=G[:, blk * 16:blk * 16 + 1],
            scalar1=-1.0))
        P.op(DVE, lambda e, blk=blk: e.tensor_single_scalar(
            out=F.atok[:, blk * 128:blk * 128 + 64].bitcast(F32)[:, 0:32] if False
            else F.SI[:, 0:64],
            in_=F.logtok[:, blk * 64:(blk + 1) * 64],
            scalar=0.0, op=ALU.is_ge))
    t_neg = P.flush(DVE)
    msk_toks = []
    for blk in range(16):
        P.op(DVE, lambda e, blk=blk: e.tensor_scalar(
            out=F.SI[:, 64 + blk * 0:128] if False else F.SI[:, 64:128],
            in0=F.logtok[:, blk * 64:(blk + 1) * 64],
            scalar1=G[:, blk * 16 + 7:blk * 16 + 8], scalar2=None, op0=ALU.is_ge),
            deps=[t_neg])
        tmsk = P.flush(DVE)
        P.op(ACT, lambda e, blk=blk: e.activation(
            out=F.logtok[:, blk * 64:(blk + 1) * 64],
            in_=F.logtok[:, blk * 64:(blk + 1) * 64], func=AF.Exp,
            bias=G[:, blk * 16 + 8:blk * 16 + 9]), deps=[tmsk])
        tae = P.tok(ACT)
        P.op(DVE, lambda e, blk=blk: e.tensor_mul(
            out=F.logtok[:, blk * 64:(blk + 1) * 64],
            in0=F.logtok[:, blk * 64:(blk + 1) * 64],
            in1=F.SI[:, 64:128]), deps=[tae])
        P.flush(DVE)
        P.op(DVE, lambda e, blk=blk: e.reduce_sum(
            out=G[:, blk * 16 + 9:blk * 16 + 10],
            in_=F.logtok[:, blk * 64:(blk + 1) * 64], axis=AX.X))
        P.flush(DVE)
        P.op(DVE, lambda e, blk=blk: e.reciprocal(
            out=G[:, blk * 16 + 10:blk * 16 + 11],
            in_=G[:, blk * 16 + 9:blk * 16 + 10]))
        P.flush(DVE)
        P.op(DVE, lambda e, blk=blk: e.tensor_scalar_mul(
            out=F.atok[:, blk * 128:blk * 128 + 64],
            in0=F.logtok[:, blk * 64:(blk + 1) * 64],
            scalar1=G[:, blk * 16 + 10:blk * 16 + 11]), deps=[t_az])
        msk_toks.append(P.flush(DVE))
    t_gates = msk_toks[-1]
    at_toks = []
    for blk in range(16):
        bank = blk % 2 + 5
        deps = [t_gates] + (at_toks[-2:-1] if len(at_toks) >= 2 else [])
        P.op(PE, lambda e, blk=blk, bank=bank: e.transpose(
            B.psb(bank, 1, BF16)[:, 0:128], F.atok[:, blk * 128:(blk + 1) * 128],
            B.ident[:]), deps=deps)
        mt = P.tok(PE)
        P.op(DVE, lambda e, blk=blk, bank=bank: e.tensor_copy(
            out=F.aT[:, blk * 128:(blk + 1) * 128],
            in_=B.psb(bank, 1, BF16)[:, 0:128]), deps=[mt])
        at_toks.append(P.tok(DVE))
    t_aT = at_toks[-1]
    si_toks = []
    for half in range(2):
        for tb in range(NT):
            P.op(PE, lambda e, half=half, tb=tb: e.matmul(
                B.psb(half), F.atok[:, tb * 128:(tb + 1) * 128],
                xnv[:, tb, half * 512:half * 512 + 512],
                start=(tb == 0), stop=(tb == NT - 1)),
                deps=([t_gates, tok_xn] + si_toks) if tb == 0 else [])
    for tb in range(NT):
        P.op(PE, lambda e, tb=tb: e.matmul(
            B.psb(2)[:, 0:1], F.atok[:, tb * 128:(tb + 1) * 128], B.ones[:, 0:1],
            start=(tb == 0), stop=(tb == NT - 1)))
    mt = P.tok(PE)
    P.op(DVE, lambda e: e.tensor_copy(out=F.SI[:, 0:512], in_=B.psb(0)), deps=[mt])
    P.op(DVE, lambda e: e.tensor_copy(out=F.SI[:, 512:1024], in_=B.psb(1)))
    P.op(DVE, lambda e: e.tensor_scalar_add(out=C[:, 88:89], in0=B.psb(2)[:, 0:1],
                                            scalar1=1e-8))
    t_si = P.flush(DVE)
    P.op(DVE, lambda e: e.reciprocal(out=C[:, 89:90], in_=C[:, 88:89]), deps=[t_si])
    t_ws = P.flush(DVE)
    P.op(DVE, lambda e: e.tensor_scalar_mul(out=F.gi[:, 0:1024], in0=F.SI[:, 0:1024],
                                            scalar1=C[:, 89:90]), deps=[t_ws])
    t_gi = P.flush(DVE)
    gt_toks = []
    for dc in range(ND):
        bank = dc % 2 + 3
        deps = [t_gi] + (gt_toks[-2:-1] if len(gt_toks) >= 2 else [])
        P.op(PE, lambda e, dc=dc, bank=bank: e.transpose(
            B.psb(bank, 1, BF16)[:, 0:64], F.gi[0:64, dc * 128:(dc + 1) * 128],
            B.ident[0:64, 0:64]), deps=deps)
        mt = P.tok(PE)
        P.op(DVE, lambda e, dc=dc, bank=bank: e.tensor_copy(
            out=F.giT[:, dc * 64:(dc + 1) * 64],
            in_=B.psb(bank, 1, BF16)[:, 0:64]), deps=[mt])
        gt_toks.append(P.tok(DVE))
    t_giT = gt_toks[-1]
    rb_ = F.SI[:, 0:512]
    zb_ = F.SI[:, 512:1024]
    sgA = F.sg[:, 0:64]
    sgB = F.sg[:, 64:128]
    wtk2 = {}
    prev2 = [t_giT, t_gh]
    for m in range(24):
        par = m % 2
        wdep = [wtk2[m - 2]] if m >= 2 else []
        wd = P.dma(SYNC, wstr[:, par, :, :], wih_src[:, m, :, :],
                   deps=wdep + list(pro_tok) + [t_gh])
        for k in range(ND):
            P.op(PE, lambda e, par=par, k=k, m=m: e.matmul(
                B.psb(6 + m % 2)[:, 0:64], wstr[:, par, k, :],
                F.giT[:, k * 64:(k + 1) * 64], start=(k == 0), stop=(k == ND - 1)),
                deps=([wd] + prev2) if k == 0 else [])
        mt = P.tok(PE)
        wtk2[m] = mt
        P.op(DVE, lambda e, m=m: e.tensor_add(
            out=sgA, in0=B.psb(6 + m % 2)[:, 0:64],
            in1=F.ghT[:, m * 64:(m + 1) * 64]), deps=[mt])
        tt_ = P.flush(DVE)
        if m < 8:
            P.op(ACT, lambda e, m=m: e.activation(
                out=rb_[:, m * 64:(m + 1) * 64], in_=sgA, func=AF.Sigmoid,
                bias=C[:, 16 + m:17 + m]), deps=[tt_])
            prev2 = [P.tok(ACT)]
        elif m < 16:
            P.op(ACT, lambda e, m=m: e.activation(
                out=zb_[:, (m - 8) * 64:(m - 7) * 64], in_=sgA, func=AF.Sigmoid,
                bias=C[:, 16 + m:17 + m]), deps=[tt_])
            prev2 = [P.tok(ACT)]
        else:
            dc = m - 16
            P.op(DVE, lambda e, m=m: e.tensor_scalar_add(
                out=sgA, in0=sgA, scalar1=C[:, 16 + m:17 + m]), deps=[tt_])
            P.op(DVE, lambda e, dc=dc: e.tensor_scalar_add(
                out=sgB, in0=rb_[:, dc * 64:(dc + 1) * 64], scalar1=-1.0))
            P.flush(DVE)
            P.op(DVE, lambda e, dc=dc, m=m: e.tensor_mul(
                out=sgB, in0=sgB, in1=F.ghT[:, m * 64:(m + 1) * 64]))
            P.flush(DVE)
            P.op(DVE, lambda e: e.tensor_add(out=sgA, in0=sgA, in1=sgB))
            t3_ = P.flush(DVE)
            P.op(ACT, lambda e, dc=dc: e.activation(
                out=F.nb[:, dc * 64:(dc + 1) * 64], in_=sgA, func=AF.Tanh),
                deps=[t3_])
            prev2 = [P.tok(ACT)]
    t_gru = prev2[0]
    hn_toks = []
    for dc in range(ND):
        P.op(DVE, lambda e, dc=dc: e.tensor_sub(
            out=F.sg[:, 128:192], in0=F.ST[:, dc * 64:(dc + 1) * 64],
            in1=F.nb[:, dc * 64:(dc + 1) * 64]), deps=[t_gru])
        P.flush(DVE)
        P.op(DVE, lambda e, dc=dc: e.tensor_mul(
            out=F.sg[:, 128:192], in0=F.sg[:, 128:192],
            in1=zb_[:, dc * 64:(dc + 1) * 64]))
        P.flush(DVE)
        P.op(DVE, lambda e, dc=dc: e.tensor_add(
            out=F.SnT[:, dc * 64:(dc + 1) * 64], in0=F.nb[:, dc * 64:(dc + 1) * 64],
            in1=F.sg[:, 128:192]))
        hn_toks.append(P.flush(DVE))
    t_snt = hn_toks[-1]
    P.op(DVE, lambda e: e.memset(F.svT[:], 0.0), deps=deps0)
    t_svz = P.flush(DVE)
    wtk3 = {}
    prev3 = [t_snt, t_svz]
    for m in range(ND):
        par = m % 2
        wdep = [wtk3[m - 2]] if m >= 2 else []
        wd = P.dma(SYNC, wstr[:, par, :, :], wv_src[:, m, :, :],
                   deps=wdep + list(pro_tok) + [t_gru])
        for k in range(ND):
            P.op(PE, lambda e, par=par, k=k, m=m: e.matmul(
                B.psb(6 + m % 2)[:, 0:64], wstr[:, par, k, :],
                F.SnT[:, k * 64:(k + 1) * 64], start=(k == 0), stop=(k == ND - 1)),
                deps=([wd] + prev3) if k == 0 else [])
        mt = P.tok(PE)
        wtk3[m] = mt
        P.op(DVE, lambda e, m=m: e.tensor_scalar_add(
            out=F.svT[:, m * 128:m * 128 + 64], in0=B.psb(6 + m % 2)[:, 0:64],
            scalar1=0.0), deps=[mt])
        prev3 = [P.tok(DVE)]
    t_svT = prev3[0]
    sv_toks = []
    for dc in range(ND):
        bank = dc % 2 + 3
        deps = [t_svT] + (sv_toks[-2:-1] if len(sv_toks) >= 2 else [])
        P.op(PE, lambda e, dc=dc, bank=bank: e.transpose(
            B.psb(bank, 1, BF16)[:, 0:128], F.svT[:, dc * 128:(dc + 1) * 128],
            B.ident[:]), deps=deps)
        mt = P.tok(PE)
        P.op(DVE, lambda e, dc=dc, bank=bank: e.tensor_copy(
            out=F.svs[:, dc * 128:(dc + 1) * 128],
            in_=B.psb(bank, 1, BF16)[:, 0:128]), deps=[mt])
        sv_toks.append(P.tok(DVE))
    t_svs = sv_toks[-1]
    outTv = F.outT.rearrange("p (dc t) -> p dc t", t=T)
    o_toks = []
    for dc in range(ND):
        for tt in range(4):
            bank = (dc * 4 + tt) % 2
            P.op(PE, lambda e, dc=dc, tt=tt, bank=bank: e.matmul(
                B.psb(bank), F.svs[0:64, dc * 128:(dc + 1) * 128],
                F.aT[0:64, tt * 512:(tt + 1) * 512], start=True, stop=True),
                deps=([t_svs, t_aT] + o_toks[-2:-1]))
            mt = P.tok(PE)
            P.op(DVE, lambda e, dc=dc, tt=tt, bank=bank: e.tensor_copy(
                out=outTv[:, dc, tt * 512:(tt + 1) * 512], in_=B.psb(bank)),
                deps=[mt])
            o_toks.append(P.tok(DVE))
    t_outp = o_toks[-1]
    P.op(DVE, lambda e: e.tensor_copy(out=F.bvb[:], in_=C[:, 64:72]), deps=deps0)
    t_bvb = P.flush(DVE)
    yTv = B.xn[:].rearrange("p (dc t) -> p dc t", t=T)
    wtk4 = {}
    y_toks = []
    for dc in range(ND):
        par = dc % 2
        wdep = [wtk4[dc - 2]] if dc >= 2 else [t_outp]
        wd = P.dma(SYNC, wstr[:, par, :, :], wo_src[:, dc, :, :],
                   deps=wdep + list(pro_tok) + [t_gru])
        for k in range(ND):
            P.op(PE, lambda e, par=par, k=k: e.matmul(
                B.psb(2)[:, 0:1], wstr[:, par, k, :], F.bvb[:, k:k + 1],
                start=(k == 0), stop=(k == ND - 1)),
                deps=([wd, t_bvb] + y_toks[-2:-1]))
        mtb = P.tok(PE)
        P.op(DVE, lambda e, dc=dc: e.tensor_scalar_add(
            out=C[:, 90:91], in0=B.psb(2)[:, 0:1], scalar1=C[:, 72 + dc:73 + dc]),
            deps=[mtb])
        t_bvy = P.flush(DVE)
        for tt in range(4):
            bank = (dc * 4 + tt) % 2
            for k in range(ND):
                P.op(PE, lambda e, par=par, dc=dc, tt=tt, k=k, bank=bank: e.matmul(
                    B.psb(bank), wstr[:, par, k, :],
                    outTv[:, k, tt * 512:(tt + 1) * 512],
                    start=(k == 0), stop=(k == ND - 1)),
                    deps=([t_outp] + y_toks[-2:-1]) if k == 0 else [])
            mt = P.tok(PE)
            if tt == 3:
                wtk4[dc] = mt
            P.op(DVE, lambda e, dc=dc, tt=tt, bank=bank: e.tensor_scalar_add(
                out=yTv[:, dc, tt * 512:(tt + 1) * 512], in0=B.psb(bank),
                scalar1=C[:, 90:91]), deps=[mt, t_bvy])
            y_toks.append(P.tok(DVE))
    F.dbg_toks = []
    if os.environ.get("F1_DBG") and b == 0:
        pairs = [("d1_log", F.logtok[:]), ("d1_atok", F.atok[:]),
                 ("d1_aT", F.aT[:]), ("d1_gi", F.gi[:]), ("d1_SI", F.SI[:]),
                 ("d1_ST", F.ST[:]), ("d1_SnT", F.SnT[:]), ("d1_svT", F.svT[:]),
                 ("d1_ghT", F.ghT[:]), ("d1_hT", F.hT[:]),
                 ("d1_outT", B.xnT[:]), ("d1_c16", F.cols[:, 0:96])]
        F.dbg_toks = [P.dma(SYNC, H[nm], ap, deps=[y_toks[-1]])
                      for nm, ap in pairs]
    return y_toks[-1]


def build_l1ffn_launch(nc):
    x_h = nc.dram_tensor("x", [B_PER_CORE, T, D], F32, kind="ExternalInput").ap()
    y_h = nc.dram_tensor("y", [B_PER_CORE, T, D], F32, kind="ExternalOutput").ap()
    H = decl_l1ffn_hbm(nc)
    es = contextlib.ExitStack()
    with es:
        B = Bufs(nc, es)
        F = F1Bufs(nc, es, B)
        P = Prog(nc, es)
        ct = emit_const_setup(P, B)
        pro = emit_f1_prologue(P, B, F, H, [ct])
        prev = list(pro)
        final_stores = []
        for b in range(B_PER_CORE):
            xtoks = emit_load_x(P, B, x_h, b, deps=prev)
            txn, txnT = emit_ln(P, B, ct, xtoks, war_deps=prev if b > 0 else ())
            t_yT = emit_l1_ffn(P, B, F, H, b, txn, txnT, pro)
            xtoks2 = emit_load_x(P, B, x_h, b,
                                 deps=[t_yT] + list(getattr(F, "dbg_toks", [])))
            yv = y_h[b].rearrange("(tc p) d -> p tc d", p=128)
            store_y = [yv[:, tc, :] for tc in range(NT)]
            tok_x, st_toks, _ = emit_residual_add(P, B, B.xn[:], t_yT, xtoks2,
                                                  store_y=store_y)
            final_stores += st_toks
            prev = [tok_x] + st_toks
        P.op(SYNC, lambda e: e.nop(), deps=final_stores)
        with nc.Block() as block:
            P.emit(block)
    return nc


def host_prep_l1attn(inp):
    f = _as32
    lay = lambda W, M: np.ascontiguousarray(
        W.reshape(8, 128, M, 128).transpose(1, 2, 0, 3).reshape(128, M * 8 * 128))
    d = {}
    d["a1_wq"] = lay(f(inp["L1_attn_Wq"]), 8)
    d["a1_wk"] = lay(f(inp["L1_attn_Wk"]), 8)
    d["a1_wv"] = lay(f(inp["L1_attn_Wv"]), 8)
    d["a1_wf"] = lay(f(inp["L1_attn_Wf"]), 2)
    d["a1_wo"] = lay(f(inp["L1_attn_Wo"]), 8)
    gn = f(inp["L1_attn_g_norm_w"]).ravel()
    if gn.size == 256:
        gn = np.tile(gn, 4)
    d["a1_gn"] = np.ascontiguousarray(gn.reshape(8, 128).T)
    return d


def decl_l1attn_hbm(nc):
    h = {}
    for nm, shp in [("a1_wq", [128, 8192]), ("a1_wk", [128, 8192]),
                    ("a1_wv", [128, 8192]), ("a1_wf", [128, 2048]),
                    ("a1_wo", [128, 8192]), ("a1_gn", [128, 8])]:
        h[nm] = nc.dram_tensor(nm, shp, F32, kind="ExternalInput").ap()
    for nm, shp in [("a1_wq_bf", [128, 8192]), ("a1_wk_bf", [128, 8192]),
                    ("a1_wv_bf", [128, 8192]), ("a1_wf_bf", [128, 2048]),
                    ("a1_wo_bf", [128, 8192])]:
        h[nm] = nc.dram_tensor(nm, shp, BF16, kind="Internal").ap()
    if os.environ.get("GSA_DBG"):
        for nm, shp, dt in [("d_qT", [128, 8192], BF16), ("d_kT", [128, 8192], BF16),
                            ("d_vT", [128, 8192], BF16), ("d_f", [128, 2048], F32),
                            ("d_G", [128, 2048], F32), ("d_oT", [128, 16384], BF16),
                            ("d_Hk", [128, 512], BF16), ("d_Hv", [128, 512], BF16),
                            ("d_Bt", [128, 128], BF16), ("d_B2t", [128, 128], BF16),
                            ("d_Ett", [128, 128], BF16), ("d_ptp", [128, 128], BF16),
                            ("d_Am", [128, 128], BF16), ("d_Wm", [128, 128], BF16),
                            ("d_sm", [128, 64], F32), ("d_ebc", [128, 64], BF16),
                            ("d_cs", [128, 512], F32)]:
            h[nm] = nc.dram_tensor(nm, shp, dt, kind="ExternalOutput").ap()
    return h


class A1Bufs:
    def __init__(self, nc, es, B):
        A = lambda name, shape, dt: es.enter_context(nc.sbuf_tensor(name, shape, dt))
        self.B = B
        self.stg = B.xn[:].bitcast(F32)[:, 0:4096]
        self.vTh = B.x[:, 0:4096].bitcast(BF16)
        self.G = B.x[:, 4096:6144]
        self.fv = B.x[:, 6144:8192]
        self.cs = B.x[:, 8192:10240]
        self.oT = B.xn[:]
        self.yT = B.xnT[:]
        self.qTh = A("a1_qTh", [128, 8 * 1024], BF16)
        self.kTh = A("a1_kTh", [128, 8 * 1024], BF16)
        self.wsb = A("a1_wsb", [128, 2 * 1024], BF16)
        self.Hk = A("a1_Hk", [128, 4 * 128], BF16)
        self.Hv = A("a1_Hv", [128, 2 * 256], BF16)
        self.cols = A("a1_cols", [128, 32], F32)
        self.mask128 = A("a1_mask", [128, 128], BF16)
        self.oh127 = A("a1_oh127", [128, 128], BF16)
        self.Bt = A("a1_Bt", [128, 128], BF16)
        self.B2t = A("a1_B2t", [128, 128], BF16)
        self.Ett = A("a1_Ett", [128, 128], BF16)
        self.ptp = A("a1_ptp", [128, 128], BF16)
        self.ptT = A("a1_ptT", [128, 128], BF16)
        self.ktok = A("a1_ktok", [128, 2 * 128], BF16)
        self.vtok = A("a1_vtok", [128, 2 * 128], BF16)
        self.Am = A("a1_Am", [128, 128], BF16)
        self.Wm = A("a1_Wm", [128, 128], BF16)
        self.BT2 = A("a1_BT2", [128, 256], BF16)
        self.EtT = A("a1_EtT", [128, 128], BF16)
        self.sm = A("a1_sm", [128, 64], F32)
        self.erow = A("a1_erow", [128, 64], BF16)
        self.ebc = A("a1_ebc", [128, 64], BF16)
        self.hvinc = A("a1_hvinc", [128, 256], BF16)
        self.sch = A("a1_sch", [128, 768], F32)


def emit_a1_prologue(P, A1, H, deps):
    stg = A1.stg
    prev = list(deps)
    for nm, dstnm, npieces in [("a1_wq", "a1_wq_bf", 4), ("a1_wk", "a1_wk_bf", 4),
                               ("a1_wv", "a1_wv_bf", 4), ("a1_wf", "a1_wf_bf", 1),
                               ("a1_wo", "a1_wo_bf", 4)]:
        wb = A1.stg[:, 2048:3072].bitcast(BF16)
        for pc in range(npieces):
            d = P.dma(SYNC, stg[:, 0:2048], H[nm][:, pc * 2048:(pc + 1) * 2048],
                      deps=prev)
            P.op(DVE, lambda e, wb=wb: e.tensor_copy(out=wb, in_=A1.stg[:, 0:2048]),
                 deps=[d])
            c1 = P.tok(DVE)
            prev = [P.dma(SYNC, H[dstnm][:, pc * 2048:(pc + 1) * 2048], wb,
                          deps=[c1]), c1]
    dg = P.dma(SYNC, A1.cols[:, 0:8], H["a1_gn"], deps=deps)
    return prev + [dg]


def emit_a1_consts(P, B, A1, const_tok):
    P.op(POOL, lambda e: e.iota(A1.mask128[:], pattern=[[1, 128]], base=0,
                                channel_multiplier=-1,
                                allow_small_or_imprecise_dtypes=True),
         deps=[const_tok])
    t0 = P.tok(POOL)
    P.op(POOL, lambda e: e.tensor_single_scalar(out=A1.mask128[:],
                                                in_=A1.mask128[:],
                                                scalar=-0.5, op=ALU.is_gt),
         deps=[t0])
    P.op(POOL, lambda e: e.memset(A1.oh127[:], 0.0))
    P.op(POOL, lambda e: e.iota(A1.cols[:, 12:13], pattern=[[0, 1]], base=0,
                                channel_multiplier=1,
                                allow_small_or_imprecise_dtypes=True))
    t1 = P.tok(POOL)
    P.op(POOL, lambda e: e.tensor_single_scalar(out=A1.cols[:, 13:14],
                                                in_=A1.cols[:, 12:13],
                                                scalar=126.5, op=ALU.is_gt),
         deps=[t1])
    t2 = P.tok(POOL)
    P.op(DVE, lambda e: e.tensor_copy(out=A1.oh127[:, 0:1], in_=A1.cols[:, 13:14]),
         deps=[t2])
    P.op(DVE, lambda e: e.memset(A1.cols[:, 10:11], 0.0))
    P.op(DVE, lambda e: e.memset(A1.cols[:, 11:12], 1.0))
    P.op(DVE, lambda e: e.memset(A1.cols[:, 20:21], 1e-5))
    P.op(DVE, lambda e: e.memset(A1.ptp[:], 0.0))
    return P.flush(DVE)


def emit_l1_attn(P, B, A1, H, b, tok_xnT, pro_tok, cst_tok):
    xnTv = B.xnT[:].rearrange("p (dc t) -> p dc t", t=T)
    qThv = A1.qTh[:].rearrange("p (dc t) -> p dc t", t=1024)
    kThv = A1.kTh[:].rearrange("p (dc t) -> p dc t", t=1024)
    vThv = A1.vTh.rearrange("p (dc t) -> p dc t", t=1024)
    fvv = A1.fv.rearrange("p (m t) -> p m t", t=1024)
    Gv = A1.G.rearrange("p (m t) -> p m t", t=1024)
    oTv = A1.oT.rearrange("p (dc t) -> p dc t", t=T)
    yTv = A1.yT.rearrange("p (dc t) -> p dc t", t=T)
    deps0 = [tok_xnT, cst_tok] + list(pro_tok)
    STAGE = int(os.environ.get("A1_STAGE", "9"))
    scale = float(256 ** -0.5)
    wstr = A1.wsb[:].rearrange("p (pb k c) -> p pb k c", pb=2, c=128)
    zc = A1.cols[:, 10:11]
    P.op(DVE, lambda e: e.memset(A1.Hk[:], 0.0), deps=deps0)
    P.op(DVE, lambda e: e.memset(A1.Hv[:], 0.0))
    t_st = P.flush(DVE)
    prev_c = [t_st]
    if STAGE < -4:
        return t_st
    for half in range(2):
        tb = half * 1024
        wtk = {}
        prev_ev = [prev_c[-1]] + deps0
        srcs = [("a1_wq_bf", 8, "q"), ("a1_wk_bf", 8, "k"),
                ("a1_wv_bf", 8, "v"), ("a1_wf_bf", 2, "z")]
        kinds = os.environ.get("A1_KINDS", "qkvz")
        srcs = [s for s in srcs if s[2] in kinds]
        si = 0
        for nm, nm_m, kind in srcs:
            hsrc = H[nm].rearrange("p (m k c) -> p m k c", k=8, c=128)
            for m in range(nm_m):
                par = si % 2
                wdep = [wtk[si - 2]] if si >= 2 else [prev_c[-1]]
                wd = P.dma(SYNC, wstr[:, par, :, :], hsrc[:, m, :, :],
                           deps=wdep + list(pro_tok))
                for tt in range(2):
                    bank = tt % 2
                    for k in range(ND):
                        P.op(PE, lambda e, par=par, tt=tt, k=k, bank=bank, tb=tb:
                             e.matmul(
                                 B.psb(bank), wstr[:, par, k, :],
                                 xnTv[:, k, tb + tt * 512:tb + (tt + 1) * 512],
                                 start=(k == 0), stop=(k == ND - 1)),
                             deps=([wd] + prev_ev) if k == 0 else [])
                    mt = P.tok(PE)
                    if tt == 1:
                        wtk[si] = mt
                    if kind in ("q", "k"):
                        dst = qThv if kind == "q" else kThv
                        P.op(ACT, lambda e, m=m, tt=tt, bank=bank, dst=dst:
                             e.activation(out=dst[:, m, tt * 512:(tt + 1) * 512],
                                          in_=B.psb(bank), func=AF.Silu), deps=[mt])
                        prev_ev = [P.tok(ACT)]
                    elif kind == "v":
                        P.op(DVE, lambda e, m=m, tt=tt, bank=bank: e.tensor_copy(
                            out=vThv[:, m, tt * 512:(tt + 1) * 512], in_=B.psb(bank)),
                            deps=[mt])
                        prev_ev = [P.tok(DVE)]
                    else:
                        sc1 = A1.cs[:, 0:512]
                        sc2 = A1.cs[:, 512:1024]
                        P.op(ACT, lambda e, bank=bank, sc1=sc1: e.activation(
                            out=sc1, in_=B.psb(bank), func=AF.Abs), deps=[mt])
                        t1 = P.tok(ACT)
                        P.op(DVE, lambda e, bank=bank, sc2=sc2: e.tensor_scalar_min(
                            out=sc2, in0=B.psb(bank), scalar1=0.0), deps=[mt, t1])
                        P.flush(DVE)
                        P.op(ACT, lambda e, sc1=sc1: e.activation(
                            out=sc1, in_=sc1, func=AF.Exp, scale=-1.0), deps=[t1])
                        t2 = P.tok(ACT)
                        P.op(ACT, lambda e, sc1=sc1: e.activation(
                            out=sc1, in_=sc1, func=AF.Ln, bias=A1.cols[:, 11:12]),
                            deps=[t2])
                        t3 = P.tok(ACT)
                        P.op(DVE, lambda e, sc1=sc1, sc2=sc2: e.tensor_sub(
                            out=sc2, in0=sc2, in1=sc1), deps=[t3])
                        P.flush(DVE)
                        P.op(DVE, lambda e, m=m, tt=tt, sc2=sc2: e.tensor_scalar_mul(
                            out=fvv[:, m, tt * 512:(tt + 1) * 512], in0=sc2,
                            scalar1=0.125))
                        prev_ev = [P.flush(DVE)]
                si += 1
        t_proj = prev_ev[0]
        if STAGE < -3 and STAGE >= -4 and os.environ.get("A1_NOSCAN"):
            prev_c = [t_proj]
            continue
        for dc in range(ND):
            P.op(DVE, lambda e, dc=dc: e.tensor_scalar_mul(
                out=qThv[:, dc, :], in0=qThv[:, dc, :], scalar1=scale),
                deps=[t_proj])
        P.flush(DVE)
        for m in range(2):
            P.op(DVE, lambda e, m=m: e.tensor_tensor_scan(
                out=Gv[:, m, 0:512], data0=fvv[:, m, 0:512], data1=fvv[:, m, 0:512],
                initial=0.0, op0=ALU.add, op1=ALU.bypass))
            P.op(DVE, lambda e, m=m: e.tensor_tensor_scan(
                out=Gv[:, m, 512:1024], data0=fvv[:, m, 512:1024],
                data1=fvv[:, m, 512:1024], initial=0.0, op0=ALU.add,
                op1=ALU.bypass))
            P.flush(DVE)
            P.op(DVE, lambda e, m=m: e.tensor_scalar_add(
                out=Gv[:, m, 512:1024], in0=Gv[:, m, 512:1024],
                scalar1=Gv[:, m, 511:512]))
        t_G = P.flush(DVE)
        prev_c = [t_G]
        if STAGE < -3:
            continue
        if os.environ.get("GSA_DBG") and b == 0 and half == 0:
            dt0 = P.dma(SYNC, H["d_qT"], A1.qTh[:], deps=[t_G])
            dt1 = P.dma(SYNC, H["d_kT"], A1.kTh[:], deps=[t_G])
            dt2 = P.dma(SYNC, H["d_vT"], A1.vTh, deps=[t_G])
            dt3 = P.dma(SYNC, H["d_f"], A1.fv, deps=[t_G])
            dt4 = P.dma(SYNC, H["d_G"], A1.G, deps=[t_G])
            prev_c = [t_G, dt0, dt1, dt2, dt3, dt4]
        for cl in range(8):
            for pair in range(2):
                gsl = Gv[:, pair, cl * 128:(cl + 1) * 128]
                gp = Gv[:, pair, cl * 128 - 1:cl * 128] if cl > 0 else zc
                P.op(DVE, lambda e, gsl=gsl, gp=gp: e.tensor_scalar(
                    out=A1.cs[:, 1024:1152], in0=gsl, scalar1=gp, scalar2=None,
                    op0=ALU.subtract), deps=prev_c)
                t0 = P.flush(DVE)
                P.op(ACT, lambda e: e.activation(out=A1.cs[:, 1152:1280],
                                                 in_=A1.cs[:, 1024:1152],
                                                 func=AF.Exp, scale=-1.0),
                     deps=[t0])
                P.op(ACT, lambda e: e.activation(out=A1.EtT[:],
                                                 in_=A1.cs[:, 1024:1152],
                                                 func=AF.Exp))
                P.op(ACT, lambda e: e.activation(out=A1.cols[:, 21:22],
                                                 in_=A1.cs[:, 1151:1152],
                                                 func=AF.Exp))
                P.op(ACT, lambda e, pair=pair, cl=cl: e.activation(
                    out=A1.cs[:, 1280:1408],
                    in_=fvv[:, pair, cl * 128:(cl + 1) * 128], func=AF.Exp))
                t1 = P.tok(ACT)
                P.op(DVE, lambda e: e.tensor_scalar(
                    out=A1.cs[:, 1280:1408], in0=A1.cs[:, 1280:1408],
                    scalar1=-1.0, scalar2=1.0, op0=ALU.mult, op1=ALU.add),
                    deps=[t1])
                P.flush(DVE)
                P.op(DVE, lambda e: e.tensor_mul(out=A1.BT2[:, 0:128],
                                                 in0=A1.cs[:, 1152:1280],
                                                 in1=A1.cs[:, 1280:1408]))
                t2 = P.flush(DVE)
                P.op(DVE, lambda e: e.tensor_scalar_mul(
                    out=A1.BT2[:, 128:256], in0=A1.BT2[:, 0:128],
                    scalar1=A1.cols[:, 21:22]), deps=[t2])
                t_bt = P.flush(DVE)
                if STAGE < -2:
                    prev_c = [t_bt]
                    continue
                ttoks = []
                for ti, (srcap, dstt) in enumerate(
                        [(A1.BT2[:, 0:128], A1.Bt), (A1.BT2[:, 128:256], A1.B2t),
                         (A1.EtT[:], A1.Ett)]):
                    bank = 2 + ti % 2
                    P.op(PE, lambda e, srcap=srcap, bank=bank: e.transpose(
                        B.psb(bank, 1, BF16)[:, 0:128], srcap, B.ident[:]),
                        deps=[t_bt] + ttoks[max(0, len(ttoks) - 2):
                                            max(0, len(ttoks) - 1)])
                    mt = P.tok(PE)
                    P.op(DVE, lambda e, dstt=dstt, bank=bank: e.tensor_copy(
                        out=dstt[:], in_=B.psb(bank, 1, BF16)[:, 0:128]),
                        deps=[mt])
                    ttoks.append(P.tok(DVE))
                P.op(DVE, lambda e: e.memset(A1.hvinc[:], 0.0))
                ttoks.append(P.flush(DVE))
                if STAGE < 1:
                    prev_c = [ttoks[-1]]
                    continue
                for hh in range(2):
                    h = 2 * pair + hh
                    mb = hh * 64
                    for i in range(2):
                        dc = 2 * h + i
                        bank = 2 + i
                        P.op(PE, lambda e, dc=dc, cl=cl, bank=bank: e.transpose(
                            B.psb(bank, 1, BF16)[:, 0:128],
                            kThv[:, dc, cl * 128:(cl + 1) * 128], B.ident[:]),
                            deps=ttoks[-2:])
                        mt = P.tok(PE)
                        P.op(DVE, lambda e, i=i, bank=bank: e.tensor_copy(
                            out=A1.ktok[:, i * 128:(i + 1) * 128],
                            in_=B.psb(bank, 1, BF16)[:, 0:128]), deps=[mt])
                        ttoks.append(P.tok(DVE))
                    for i in range(2):
                        dc = 2 * h + i
                        bank = 2 + i
                        P.op(PE, lambda e, dc=dc, cl=cl, bank=bank: e.transpose(
                            B.psb(bank, 1, BF16)[:, 0:128],
                            vThv[:, dc, cl * 128:(cl + 1) * 128], B.ident[:]),
                            deps=ttoks[-2:])
                        mt = P.tok(PE)
                        P.op(DVE, lambda e, i=i, bank=bank: e.tensor_copy(
                            out=A1.vtok[:, i * 128:(i + 1) * 128],
                            in_=B.psb(bank, 1, BF16)[:, 0:128]), deps=[mt])
                        ttoks.append(P.tok(DVE))
                    t_tr = ttoks[-1]
                    for i in range(2):
                        P.op(PE, lambda e, h=h, i=i, cl=cl: e.matmul(
                            B.psb(4)[:, 0:128],
                            kThv[:, 2 * h + i, cl * 128:(cl + 1) * 128],
                            qThv[:, 2 * h + i, cl * 128:(cl + 1) * 128],
                            start=(i == 0), stop=(i == 1)), deps=[t_tr])
                    mt = P.tok(PE)
                    P.op(DVE, lambda e: e.tensor_mul(out=A1.Am[:],
                                                     in0=B.psb(4)[:, 0:128],
                                                     in1=A1.mask128[:]), deps=[mt])
                    t_am = P.flush(DVE)
                    if STAGE < 2:
                        prev_c = [t_am]
                        continue
                    P.op(PE, lambda e, mb=mb: e.matmul(
                        B.psb(5)[:, 0:64], A1.Am[:], A1.Bt[:, mb:mb + 64],
                        start=True, stop=False), deps=[t_am])
                    for i in range(2):
                        P.op(PE, lambda e, h=h, i=i, cl=cl: e.matmul(
                            B.psb(5)[:, 0:64],
                            qThv[:, 2 * h + i, cl * 128:(cl + 1) * 128],
                            A1.Hk[:, h * 128 + i * 64:h * 128 + (i + 1) * 64],
                            start=False, stop=(i == 1)))
                    mt = P.tok(PE)
                    P.op(DVE, lambda e, mb=mb: e.tensor_mul(
                        out=A1.sm[:], in0=B.psb(5)[:, 0:64],
                        in1=A1.Ett[:, mb:mb + 64]), deps=[mt])
                    P.flush(DVE)
                    P.op(DVE, lambda e: e.reduce_max(out=A1.cols[:, 16:17],
                                                     in_=A1.sm[:], axis=AX.X))
                    P.flush(DVE)
                    P.op(DVE, lambda e: e.tensor_scalar_mul(
                        out=A1.cols[:, 17:18], in0=A1.cols[:, 16:17],
                        scalar1=-1.0))
                    t4 = P.flush(DVE)
                    P.op(ACT, lambda e: e.activation(out=A1.sm[:], in_=A1.sm[:],
                                                     func=AF.Exp,
                                                     bias=A1.cols[:, 17:18]),
                         deps=[t4])
                    t5 = P.tok(ACT)
                    P.op(DVE, lambda e: e.reduce_sum(out=A1.cols[:, 18:19],
                                                     in_=A1.sm[:], axis=AX.X),
                         deps=[t5])
                    P.flush(DVE)
                    P.op(DVE, lambda e: e.reciprocal(out=A1.cols[:, 19:20],
                                                     in_=A1.cols[:, 18:19]))
                    t6 = P.flush(DVE)
                    P.op(DVE, lambda e: e.tensor_scalar_mul(
                        out=A1.sm[:], in0=A1.sm[:], scalar1=A1.cols[:, 19:20]),
                        deps=[t6])
                    P.flush(DVE)
                    P.op(DVE, lambda e, mb=mb: e.tensor_mul(
                        out=A1.ptp[:, mb:mb + 64], in0=A1.sm[:],
                        in1=A1.Ett[:, mb:mb + 64]))
                    t_pt = P.flush(DVE)
                    P.op(PE, lambda e: e.transpose(B.psb(6, 1, BF16)[:, 0:128],
                                                   A1.ptp[:], B.ident[:]),
                         deps=[t_pt])
                    mt = P.tok(PE)
                    P.op(DVE, lambda e: e.tensor_copy(
                        out=A1.ptT[:], in_=B.psb(6, 1, BF16)[:, 0:128]), deps=[mt])
                    t_ptT = P.flush(DVE)
                    if STAGE < 3:
                        prev_c = [t_ptT]
                        continue
                    P.op(PE, lambda e, mb=mb: e.matmul(
                        B.psb(7)[:, 0:128], A1.BT2[mb:mb + 64, 0:128],
                        A1.ptT[mb:mb + 64, 0:128], start=True, stop=True),
                        deps=[t_ptT])
                    mt = P.tok(PE)
                    P.op(DVE, lambda e: e.tensor_mul(out=A1.Wm[:],
                                                     in0=B.psb(7)[:, 0:128],
                                                     in1=A1.mask128[:]), deps=[mt])
                    t_wm = P.flush(DVE)
                    for i in range(2):
                        bank = 4 + i
                        P.op(PE, lambda e, i=i, bank=bank: e.matmul(
                            B.psb(bank)[:, 0:128],
                            A1.vtok[:, i * 128:(i + 1) * 128], A1.Wm[:],
                            start=True, stop=False), deps=[t_wm])
                        P.op(PE, lambda e, i=i, mb=mb, pair=pair, bank=bank:
                             e.matmul(
                                 B.psb(bank)[:, 0:128],
                                 A1.Hv[mb:mb + 64,
                                       pair * 256 + i * 128:
                                       pair * 256 + (i + 1) * 128],
                                 A1.ptT[mb:mb + 64, 0:128],
                                 start=False, stop=True))
                    mt = P.tok(PE)
                    for i in range(2):
                        P.op(DVE, lambda e, h=h, i=i, cl=cl, bank=4 + i, tb=tb:
                             e.tensor_copy(
                                 out=oTv[:, 2 * h + i,
                                         tb + cl * 128:tb + (cl + 1) * 128],
                                 in_=B.psb(bank)[:, 0:128]),
                             deps=[mt] if i == 0 else [])
                    t_o = P.flush(DVE)
                    if STAGE < 4:
                        prev_c = [t_o]
                        continue
                    P.op(PE, lambda e, mb=mb: e.matmul(
                        B.psb(6)[:, 0:64], A1.oh127[:], A1.Ett[:, mb:mb + 64],
                        start=True, stop=True), deps=[t_o])
                    mt0 = P.tok(PE)
                    P.op(DVE, lambda e: e.tensor_scalar_add(
                        out=A1.erow[0:1, :], in0=B.psb(6)[0:1, 0:64],
                        scalar1=0.0), deps=[mt0])
                    t_er = P.flush(DVE)
                    P.op(PE, lambda e: e.matmul(
                        B.psb(6)[:, 0:64], B.ones[0:1, :], A1.erow[0:1, :],
                        start=True, stop=True), deps=[t_er])
                    mt1 = P.tok(PE)
                    P.op(DVE, lambda e: e.tensor_copy(out=A1.ebc[:],
                                                      in_=B.psb(6)[:, 0:64]),
                         deps=[mt1])
                    t_ebc = P.flush(DVE)
                    for i in range(2):
                        bank = 4 + i
                        P.op(PE, lambda e, i=i, mb=mb, bank=bank: e.matmul(
                            B.psb(bank)[:, 0:64],
                            A1.ktok[:, i * 128:(i + 1) * 128],
                            A1.B2t[:, mb:mb + 64], start=True, stop=True),
                            deps=[t_o, t_ebc])
                    mt2 = P.tok(PE)
                    for i in range(2):
                        hkap = A1.Hk[:, h * 128 + i * 64:h * 128 + (i + 1) * 64]
                        P.op(DVE, lambda e, hkap=hkap: e.tensor_mul(
                            out=hkap, in0=hkap, in1=A1.ebc[:]),
                            deps=[mt2] if i == 0 else [])
                    P.flush(DVE)
                    for i in range(2):
                        hkap = A1.Hk[:, h * 128 + i * 64:h * 128 + (i + 1) * 64]
                        P.op(DVE, lambda e, hkap=hkap, bank=4 + i: e.tensor_add(
                            out=hkap, in0=hkap, in1=B.psb(bank)[:, 0:64]))
                    t_hk = P.flush(DVE)
                    if STAGE < 5:
                        prev_c = [t_hk]
                        continue
                    for i in range(2):
                        bank = 6 + i
                        P.op(PE, lambda e, i=i, bank=bank: e.matmul(
                            B.psb(bank)[:, 0:128], A1.B2t[:],
                            A1.vtok[:, i * 128:(i + 1) * 128],
                            start=True, stop=True), deps=[t_hk, t_wm, t_ebc])
                    mt3 = P.tok(PE)
                    for i in range(2):
                        P.op(DVE, lambda e, i=i, mb=mb, bank=6 + i:
                             e.tensor_scalar_add(
                                 out=A1.hvinc[mb:mb + 64, i * 128:(i + 1) * 128],
                                 in0=B.psb(bank)[mb:mb + 64, 0:128], scalar1=0.0),
                             deps=[mt3] if i == 0 else [])
                    prev_c = [P.flush(DVE)]
                hv = A1.Hv[:, pair * 256:(pair + 1) * 256]
                P.op(DVE, lambda e, hv=hv: e.tensor_scalar_mul(
                    out=hv, in0=hv, scalar1=A1.cols[:, 21:22]), deps=prev_c)
                P.flush(DVE)
                P.op(DVE, lambda e, hv=hv: e.tensor_add(out=hv, in0=hv,
                                                        in1=A1.hvinc[:]))
                prev_c = [P.flush(DVE)]
                if (os.environ.get("GSA_DBG") and b == 0 and half == 0
                        and cl == 0 and pair == 1):
                    dts = [P.dma(SYNC, H["d_" + nm], getattr(A1, nm)[:]
                                 if nm != "cs" else A1.cs[:, 1024:1536],
                                 deps=prev_c)
                           for nm in ["Hk", "Hv", "Bt", "B2t", "Ett", "ptp",
                                      "Am", "Wm", "sm", "ebc", "cs"]]
                    prev_c = prev_c + dts
    t_chunks = prev_c[0]
    if STAGE < 0:
        return t_chunks
    if os.environ.get("GSA_DBG") and b == 0:
        t_chunks = P.dma(SYNC, H["d_oT"], A1.oT, deps=prev_c)
    last_pe = [t_chunks]
    rms_fl = [t_chunks]
    sq = A1.sch[:, 512:768].bitcast(BF16)
    for h in range(4):
        for tt in range(4):
            for i in range(2):
                P.op(ACT, lambda e, h=h, i=i, tt=tt, sq=sq: e.activation(
                    out=sq, in_=oTv[:, 2 * h + i, tt * 512:(tt + 1) * 512],
                    func=AF.Square), deps=last_pe + rms_fl[-1:])
                ta = P.tok(ACT)
                P.op(PE, lambda e, i=i, sq=sq: e.matmul(
                    B.psb(4), B.ones[:], sq, start=(i == 0), stop=(i == 1)),
                    deps=[ta])
                last_pe = [P.tok(PE)]
            mt = last_pe[0]
            P.op(ACT, lambda e: e.activation(out=A1.sch[:, 0:512], in_=B.psb(4),
                                             func=AF.Ln, scale=1.0 / 256.0,
                                             bias=A1.cols[:, 20:21]),
                 deps=[mt] + rms_fl[-1:])
            ta2 = P.tok(ACT)
            P.op(ACT, lambda e: e.activation(out=A1.sch[:, 0:512],
                                             in_=A1.sch[:, 0:512], func=AF.Exp,
                                             scale=-0.5), deps=[ta2])
            ta3 = P.tok(ACT)
            for i in range(2):
                P.op(DVE, lambda e, h=h, i=i, tt=tt: e.scalar_tensor_tensor(
                    out=oTv[:, 2 * h + i, tt * 512:(tt + 1) * 512],
                    in0=oTv[:, 2 * h + i, tt * 512:(tt + 1) * 512],
                    scalar=A1.cols[:, 2 * h + i:2 * h + i + 1],
                    in1=A1.sch[:, 0:512],
                    op0=ALU.mult, op1=ALU.mult), deps=[ta3] if i == 0 else [])
            rms_fl.append(P.flush(DVE))
    t_rms = rms_fl[-1]
    wo_src = H["a1_wo_bf"].rearrange("p (m k c) -> p m k c", k=8, c=128)
    wtk2 = {}
    y_toks = []
    for dc in range(ND):
        par = dc % 2
        wdep = [wtk2[dc - 2]] if dc >= 2 else []
        wd = P.dma(SYNC, wstr[:, par, :, :], wo_src[:, dc, :, :],
                   deps=wdep + list(pro_tok) + [t_chunks])
        for tt in range(4):
            bank = (dc * 4 + tt) % 2
            for k in range(ND):
                P.op(PE, lambda e, par=par, tt=tt, k=k, bank=bank: e.matmul(
                    B.psb(bank), wstr[:, par, k, :],
                    oTv[:, k, tt * 512:(tt + 1) * 512],
                    start=(k == 0), stop=(k == ND - 1)),
                    deps=([wd, t_rms] + y_toks[-2:-1]) if k == 0 else [])
            mt = P.tok(PE)
            if tt == 3:
                wtk2[dc] = mt
            P.op(DVE, lambda e, dc=dc, tt=tt, bank=bank: e.tensor_copy(
                out=yTv[:, dc, tt * 512:(tt + 1) * 512], in_=B.psb(bank)),
                deps=[mt])
            y_toks.append(P.tok(DVE))
    return y_toks[-1]



def build_l1attn_launch(nc):
    x_h = nc.dram_tensor("x", [B_PER_CORE, T, D], F32, kind="ExternalInput").ap()
    y_h = nc.dram_tensor("y", [B_PER_CORE, T, D], F32, kind="ExternalOutput").ap()
    H = decl_l1attn_hbm(nc)
    es = contextlib.ExitStack()
    with es:
        B = Bufs(nc, es)
        A1 = A1Bufs(nc, es, B)
        P = Prog(nc, es)
        ct = emit_const_setup(P, B)
        cst = emit_a1_consts(P, B, A1, ct)
        pro = emit_a1_prologue(P, A1, H, [ct])
        prev = list(pro) + [cst]
        final_stores = []
        for b in range(B_PER_CORE):
            xtoks = emit_load_x(P, B, x_h, b, deps=prev)
            txn, txnT = emit_ln(P, B, ct, xtoks, war_deps=prev if b > 0 else ())
            t_yT = emit_l1_attn(P, B, A1, H, b, txnT, pro, cst)
            xtoks2 = emit_load_x(P, B, x_h, b, deps=[t_yT])
            yv = y_h[b].rearrange("(tc p) d -> p tc d", p=128)
            store_y = [yv[:, tc, :] for tc in range(NT)]
            tok_x, st_toks, _ = emit_residual_add(P, B, B.xnT[:], t_yT, xtoks2,
                                                  store_y=store_y)
            final_stores += st_toks
            prev = [tok_x] + st_toks
        P.op(SYNC, lambda e: e.nop(), deps=final_stores)
        with nc.Block() as block:
            P.emit(block)
    return nc


def _as32(a):
    return np.ascontiguousarray(np.asarray(a, dtype=np.float32))




def build_full_launch(nc):
    x_h = nc.dram_tensor("x", [B_PER_CORE, T, D], F32, kind="ExternalInput").ap()
    y_h = nc.dram_tensor("y", [B_PER_CORE, T, D], F32, kind="ExternalOutput").ap()
    xs1 = nc.dram_tensor("xs1", [B_PER_CORE, T, D], F32, kind="Internal").ap()
    xs2 = nc.dram_tensor("xs2", [B_PER_CORE, T, D], F32, kind="Internal").ap()
    xs3 = nc.dram_tensor("xs3", [B_PER_CORE, T, D], F32, kind="Internal").ap()
    H1 = decl_l0attn_hbm(nc)
    H2 = decl_l0ffn_hbm(nc)
    H3 = decl_l1attn_hbm(nc)
    H4 = decl_l1ffn_hbm(nc)
    es = contextlib.ExitStack()
    with es:
        B = Bufs(nc, es)
        P = Prog(nc, es)
        ct = emit_const_setup(P, B)
        prev = [ct]

        def barrier(prev):
            for eng in (ACT, DVE, POOL, SYNC):
                P.op(eng, lambda e: e.nop(), deps=prev)

        with contextlib.ExitStack() as es2:
            A_ = AttnBufs(nc, es2, B)
            barrier(prev)
            tbl = emit_l0a_tables(P, A_, ct)
            wt = emit_l0a_weights(P, A_, H1, tbl)
            prev = list(wt)
            for b in range(B_PER_CORE):
                xtoks = emit_load_x(P, B, x_h, b, deps=prev)
                txn, txnT = emit_ln(P, B, ct, xtoks, war_deps=prev)
                t_outT = emit_l0_attn(P, B, A_, H1, b, txnT, tbl, wt[0], txn,
                                      stage="full", dbg=None)
                xtoks2 = emit_load_x(P, B, x_h, b, deps=[t_outT])
                yv = xs1[b].rearrange("(tc p) d -> p tc d", p=128)
                store_y = [yv[:, tc, :] for tc in range(NT)]
                tok_x, st_toks, _ = emit_residual_add(P, B, A_.outT, t_outT,
                                                      xtoks2, store_y=store_y)
                prev = [tok_x] + st_toks

        with contextlib.ExitStack() as es2:
            M = MoeBufs(nc, es2, B)
            barrier(prev)
            cs = emit_sel8_setup(P, M)
            shared_tok, store_toks, conv_fin = emit_moe_prologue(
                P, M, H2, deps=[cs] + prev)
            scratch_dep = [store_toks[-1], store_toks[-2]]
            prev = [conv_fin]
            for bi, b in enumerate([1, 0]):
                xtoks = prev if bi == 0 else emit_load_x(P, B, xs1, b, deps=prev)
                txn, txnT = emit_ln(P, B, ct, xtoks, war_deps=prev)
                t_outT, _ = emit_moe(P, B, M, H2, b, txnT,
                                     scratch_dep=scratch_dep,
                                     ge_dep=[shared_tok], stage="full",
                                     dbg_ltok=None)
                xtoks2 = emit_load_x(P, B, xs1, b, deps=[t_outT])
                yv = xs2[b].rearrange("(tc p) d -> p tc d", p=128)
                store_y = [yv[:, tc, :] for tc in range(NT)]
                tok_x, st_toks, _ = emit_residual_add(P, B, M.outT, t_outT,
                                                      xtoks2, store_y=store_y)
                prev = [tok_x] + st_toks

        with contextlib.ExitStack() as es2:
            A1 = A1Bufs(nc, es2, B)
            barrier(prev)
            cst = emit_a1_consts(P, B, A1, ct)
            pro = emit_a1_prologue(P, A1, H3, [ct] + prev)
            prev = list(pro) + [cst]
            for bi, b in enumerate([0, 1]):
                xtoks = prev if bi == 0 else emit_load_x(P, B, xs2, b, deps=prev)
                txn, txnT = emit_ln(P, B, ct, xtoks, war_deps=prev)
                t_yT = emit_l1_attn(P, B, A1, H3, b, txnT, pro, cst)
                xtoks2 = emit_load_x(P, B, xs2, b, deps=[t_yT])
                if b == 0:
                    yv = xs3[b].rearrange("(tc p) d -> p tc d", p=128)
                    store_y = [yv[:, tc, :] for tc in range(NT)]
                else:
                    store_y = None
                tok_x, st_toks, _ = emit_residual_add(P, B, B.xnT[:], t_yT,
                                                      xtoks2, store_y=store_y)
                prev = [tok_x] + st_toks

        with contextlib.ExitStack() as es2:
            F = F1Bufs(nc, es2, B)
            barrier(prev)
            pro = emit_f1_prologue(P, B, F, H4, [ct] + prev)
            prev = list(pro)
            final_stores = []
            for bi, b in enumerate([1, 0]):
                xtoks = prev if bi == 0 else emit_load_x(P, B, xs3, b, deps=prev)
                txn, txnT = emit_ln(P, B, ct, xtoks, war_deps=prev)
                t_yT = emit_l1_ffn(P, B, F, H4, b, txn, txnT, pro)
                yv = y_h[b].rearrange("(tc p) d -> p tc d", p=128)
                store_y = [yv[:, tc, :] for tc in range(NT)]
                tok_x, st_toks, _ = emit_residual_add(P, B, B.xn[:], t_yT,
                                                      xtoks, store_y=store_y)
                final_stores += st_toks
                prev = [tok_x] + st_toks
            P.op(SYNC, lambda e: e.nop(), deps=final_stores)
        with nc.Block() as block:
            P.emit(block)
    return nc


def kernel(**inputs):
    x = _as32(inputs["L0_attn_x"])
    NB = x.shape[0]
    prep = {}
    for p in (host_prep_l0attn(inputs), host_prep_l0ffn(inputs),
              host_prep_l1attn(inputs), host_prep_l1ffn(inputs)):
        prep.update(p)
    n_cores = 8
    shards = [np.ascontiguousarray(x[c * B_PER_CORE:(c + 1) * B_PER_CORE])
              for c in range(n_cores)]
    nc = bass.Bass("TRN2", target_bir_lowering=False, debug=False)
    build_full_launch(nc)
    in_maps = [dict(prep, x=shards[c]) for c in range(n_cores)]
    res = run_bass_kernel_spmd(nc, in_maps, core_ids=list(range(n_cores)))
    shards = [np.asarray(res.results[c]["y"], dtype=np.float32)
              for c in range(n_cores)]
    return np.concatenate(shards, axis=0).reshape(NB, T, D)
